# version 06
# baseline (speedup 1.0000x reference)
import numpy as np

import concourse.bass as bass
import concourse.mybir as mybir
from concourse.bass_utils import run_bass_kernel_spmd
from concourse.bass_core_rust import IndirectOffsetOnAxis

F32 = mybir.dt.float32
BF16 = mybir.dt.bfloat16
I32 = mybir.dt.int32
AF = mybir.ActivationFunctionType
ALU = mybir.AluOpType
AXX = mybir.AxisListType.X

DBG_STAGE = None
FFN_AHEAD = True

T = 2048
D = 1024
NB = T // 128
NH = 16
FH = 8
NKEY = 256
NPAIR = 64
SCALE = 0.125
import os as _os
NRING = int(_os.environ.get("K_NRING", "6"))
NSLOT = 2
NXF = 2

ENGS = ("sync", "tensor", "vector", "scalar", "gpsimd")


class Emit:

    RECORD = False

    def __init__(self, nc, sems):
        self.nc = nc
        self.sems = sems
        self.active = None
        self.c = {k: 0 for k in sems}
        self.streams = {}

    def _line(self):
        import traceback
        out = []
        for fr in reversed(traceback.extract_stack()):
            if "kernel" in fr.filename.rsplit("/", 1)[-1]:
                out.append(f"{fr.name}:{fr.lineno}")
                if len(out) == 2:
                    break
        return ",".join(out)

    def op(self, eng, sem, fn):
        if sem is not None:
            self.c[sem] += 1
        if self.active is not None and self.active[0] == eng:
            if Emit.RECORD:
                self.streams.setdefault(eng, []).append(("i", sem, 1, self._line()))
                return
            ins = fn(self.active[1])
            if sem is not None:
                ins.then_inc(self.sems[sem], 1)

    def dma(self, eng, sem, fn):
        if sem is not None:
            self.c[sem] += 16
        if self.active is not None and self.active[0] == eng:
            if Emit.RECORD:
                self.streams.setdefault(eng, []).append(("i", sem, 16, self._line()))
                return
            ins = fn(self.active[1])
            if sem is not None:
                ins.then_inc(self.sems[sem], 16)

    def wait(self, eng, sem, val=None):
        if self.active is not None and self.active[0] == eng:
            tgt = self.c[sem] if val is None else val
            if tgt > 0:
                if Emit.RECORD:
                    self.streams.setdefault(eng, []).append(("w", sem, tgt, self._line()))
                    return
                self.active[1].wait_ge(self.sems[sem], tgt)

    def snap(self):
        return dict(self.c)

    _INSTANCES = []


def build_program(nc):
    ap = {}
    for nm, sh, dt in [
        ("x", [T, D], F32), ("wqkv", [3 * D, D], F32), ("wo", [D, D], F32),
        ("wq", [2 * D, D], F32), ("keys", [FH, NKEY, 2, 128], F32),
        ("edown", [65536, D], F32), ("eup", [65536, D], F32),
        ("normw", [D], F32),
    ]:
        ap[nm] = nc.dram_tensor(nm, sh, dt, kind="ExternalInput").ap()
    out = nc.dram_tensor("out", [T, D], F32, kind="ExternalOutput").ap()
    if DBG_STAGE == "QKV":
        kdbg = nc.dram_tensor("kdbg", [128, 8 * T], BF16, kind="ExternalOutput").ap()
        vdbg = nc.dram_tensor("vdbg", [128, NB * D], BF16, kind="ExternalOutput").ap()
        qdbg = nc.dram_tensor("qdbg", [NB, 128, 8 * 128], BF16, kind="ExternalOutput").ap()
    if DBG_STAGE == "ATT0":
        ofdbg = nc.dram_tensor("ofdbg", [128, 8 * 128], BF16, kind="ExternalOutput").ap()
        ptdbg = nc.dram_tensor("ptdbg", [128, T], BF16, kind="ExternalOutput").ap()
        ptrdbg = nc.dram_tensor("ptrdbg", [128, T], BF16, kind="ExternalOutput").ap()
        dendbg = nc.dram_tensor("dendbg", [128, 4], F32, kind="ExternalOutput").ap()
    q_sp = nc.dram_tensor("q_sp", [NB, 128, 8 * 128], BF16, kind="Internal").ap()

    from contextlib import ExitStack
    ctx_p = ExitStack()
    sem_names = ["w", "xl", "qs", "ql", "g", "pe", "dv", "ac", "st", "gp", "gt"]
    sems = {nm: ctx_p.enter_context(nc.semaphore("s_" + nm)) for nm in sem_names}
    E = Emit(nc, sems)
    Emit._INSTANCES.append(E)

    def sbp(name, shape, dt=F32):
        return ctx_p.enter_context(nc.sbuf_tensor(name, shape, dt))

    x_tm = sbp("x_tm", [128, NB * D], BF16)
    k_fm = sbp("k_fm", [128, 8 * T], BF16)
    v_aug = sbp("v_aug", [128, NB * NH * 65], BF16)
    ones_c = sbp("ones_c", [128, 1], BF16)
    ones_r = sbp("ones_r", [1, 128], BF16)
    km2 = sbp("km2", [1, 16], F32)
    tbm = sbp("tbm", [128, 8], F32)
    rowm = sbp("rowm", [128, 1], BF16)
    tb1 = sbp("tb1", [1, 4], F32)
    tb2 = sbp("tb2", [1, 1], F32)
    tb3 = sbp("tb3", [1, 1], BF16)
    negB = sbp("negB", [128, 2], F32)
    dbg8 = sbp("dbg8", [128, 8], F32)
    qbuf = [sbp(f"qbuf{i}", [128, 8 * 128], BF16) for i in range(2)]
    ident_b = sbp("ident_b", [128, 128], BF16)
    ident_f = sbp("ident_f", [128, 128], F32)
    iota256 = sbp("iota256", [128, NKEY], F32)
    epsA = sbp("epsA", [128, 1], F32)
    epsB = sbp("epsB", [128, 1], F32)
    t1b = sbp("t1b", [128, D], BF16)
    sx = sbp("sx", [128, 1], F32)
    sxx = sbp("sxx", [128, 1], F32)
    mln = sbp("mln", [128, 1], F32)
    m2ln = sbp("m2ln", [128, 1], F32)
    varln = sbp("varln", [128, 1], F32)
    stdln = sbp("stdln", [128, 1], F32)
    istd = sbp("istd", [128, 1], F32)
    scr256 = sbp("scr256", [128, NKEY], F32)

    P_S = ctx_p.enter_context(nc.psum_tensor("P_S", [128, 2048], F32))
    P_T = ctx_p.enter_context(nc.psum_tensor("P_T", [128, 1024], BF16))
    P_W = ctx_p.enter_context(nc.psum_tensor("P_W", [128, 1024], F32))
    P_M = ctx_p.enter_context(nc.psum_tensor("P_M", [128, 512], F32))

    ctx_a = ExitStack()
    xn_fm = ctx_a.enter_context(nc.sbuf_tensor("xn_fm", [128, 8 * T], BF16))
    wT = [ctx_a.enter_context(nc.sbuf_tensor(f"wT{c}", [128, D], BF16))
          for c in range(8)]
    wstg = [ctx_a.enter_context(nc.sbuf_tensor(f"wstg{i}", [128, D], F32))
            for i in range(2)]
    xf32 = [ctx_a.enter_context(nc.sbuf_tensor(f"xf32{i}", [128, D], F32))
            for i in range(2)]
    qsp4 = ctx_a.enter_context(nc.sbuf_tensor("qsp4", [128, 4096], BF16))

    R = {}
    lt_seq = [0]
    qk_seq = [0]

    def load_transpose(E, src, nft, dst, stage):
        for ft in range(nft):
            g = lt_seq[0]
            lt_seq[0] += 1
            buf = stage[g % 2]
            if g >= 2:
                E.wait("sync", "pe", R.get(("lt", g - 2)))
            E.dma("sync", "w", lambda h, ft=ft, buf=buf: h.dma_start(
                out=buf[:], in_=src[ft * 128:(ft + 1) * 128, :]))
            E.wait("tensor", "w")
            E.wait("tensor", "ac")
            for c in range(8):
                E.op("tensor", "pe" if c == 7 else None,
                     lambda t, c=c, buf=buf: t.transpose(
                         P_S[:, c * 128:(c + 1) * 128],
                         in_=buf[:, c * 128:(c + 1) * 128], identity=ident_f[:]))
            R[("lt", g)] = E.c["pe"]
            E.wait("scalar", "pe")
            for c in range(8):
                E.op("scalar", "ac", lambda s, c=c, ft=ft: s.copy(
                    out=dst(c, ft), in_=P_S[:, c * 128:(c + 1) * 128]))

    def phase_a(E):
        lt_seq[0] = 0
        qk_seq[0] = 0
        E.op("gpsimd", None, lambda g: g.memset(epsA[:], 1e-5))
        E.op("gpsimd", None, lambda g: g.memset(ones_c[:], 1.0))
        E.op("gpsimd", None, lambda g: g.memset(ones_r[:], 1.0))
        E.op("gpsimd", "gp", lambda g: g.memset(v_aug[:], 1.0))
        E.op("gpsimd", None, lambda g: g.memset(epsB[:], 1e-11))
        E.op("gpsimd", None, lambda g: g.iota(
            iota256[:], pattern=[[1, NKEY]], base=0, channel_multiplier=0,
            allow_small_or_imprecise_dtypes=True))
        E.op("gpsimd", None, lambda g: g.memset(ident_b[:], 0.0))
        E.op("gpsimd", "gp", lambda g: g.affine_select(
            out=ident_b[:], in_=ident_b[:], compare_op=ALU.not_equal,
            fill=1.0, base=0, pattern=[[-1, 128]], channel_multiplier=1))
        E.op("gpsimd", None, lambda g: g.memset(ident_f[:], 0.0))
        E.op("gpsimd", "gp", lambda g: g.affine_select(
            out=ident_f[:], in_=ident_f[:], compare_op=ALU.not_equal,
            fill=1.0, base=0, pattern=[[-1, 128]], channel_multiplier=1))
        E.wait("tensor", "gp")
        E.wait("vector", "gp")

        for b in range(NB):
            xb = xf32[b % 2]
            if b >= 2:
                E.wait("sync", "dv", R.get(("xfree", b - 2)))
            E.dma("sync", "xl", lambda h, b=b, xb=xb: h.dma_start(
                out=xb[:], in_=ap["x"][b * 128:(b + 1) * 128, :]))
            E.wait("vector", "xl")
            E.op("vector", None, lambda v, xb=xb: v.tensor_reduce(
                out=sx[:], in_=xb[:], axis=AXX, op=ALU.add))
            E.op("vector", "dv", lambda v, xb=xb: v.scalar_tensor_tensor(
                out=t1b[:], in0=xb[:], scalar=1.0, in1=xb[:],
                op0=ALU.mult, op1=ALU.mult, accum_out=sxx[:]))
            E.wait("vector", "dv")
            E.op("vector", "dv", lambda v: v.tensor_single_scalar(
                out=mln[:], in_=sx[:], scalar=1.0 / D, op=ALU.mult))
            E.wait("vector", "dv")
            E.op("vector", "dv", lambda v: v.tensor_tensor(
                out=m2ln[:], in0=mln[:], in1=mln[:], op=ALU.mult))
            E.wait("vector", "dv")
            E.op("vector", "dv", lambda v: v.scalar_tensor_tensor(
                out=varln[:], in0=sxx[:], scalar=1.0 / D, in1=m2ln[:],
                op0=ALU.mult, op1=ALU.subtract))
            E.wait("scalar", "dv")
            E.op("scalar", "ac", lambda s: s.activation(
                out=stdln[:], in_=varln[:], func=AF.Sqrt, bias=epsA[:], scale=1.0))
            E.wait("vector", "ac")
            E.op("vector", "dv", lambda v: v.reciprocal(istd[:], stdln[:]))
            E.wait("vector", "dv")
            E.op("vector", "dv", lambda v, xb=xb: v.scalar_tensor_tensor(
                out=t1b[:], in0=xb[:], scalar=mln[:],
                in1=istd[:].broadcast_to((128, D)),
                op0=ALU.subtract, op1=ALU.mult))
            E.op("vector", "dv", lambda v, b=b, xb=xb: v.tensor_copy(
                x_tm[:, b * D:(b + 1) * D], xb[:]))
            R[("xfree", b)] = E.c["dv"]
            E.wait("tensor", "dv")
            E.wait("tensor", "ac")
            for c in range(8):
                E.op("tensor", "pe" if c == 7 else None, lambda t, c=c: t.transpose(
                    P_T[:, c * 128:(c + 1) * 128], in_=t1b[:, c * 128:(c + 1) * 128],
                    identity=ident_b[:]))
            E.wait("scalar", "pe")
            for c in range(8):
                E.op("scalar", "ac", lambda s, c=c, b=b: s.copy(
                    out=xn_fm[:, c * T + b * 128:c * T + (b + 1) * 128],
                    in_=P_T[:, c * 128:(c + 1) * 128]))

        for part in range(3):
            load_transpose(E, ap["wqkv"][part * D:(part + 1) * D, :], 8,
                           lambda c, ft: wT[c][:, ft * 128:(ft + 1) * 128],
                           wstg)
            E.wait("tensor", "ac")
            for bq in range(NB // 4):
                if part < 2:
                    if part == 0 and bq >= 1:
                        E.wait("scalar", "qs", R.get(("spill", bq - 1)))
                    for ft in range(8):
                        qk = qk_seq[0]
                        qk_seq[0] += 1
                        ps = P_M[:, 0:512] if qk % 2 == 0 else P_W[:, 0:512]
                        if qk >= 2:
                            E.wait("tensor", "ac", R.get(("qkevac", qk - 2)))
                        for c in range(8):
                            E.op("tensor", "pe" if c == 7 else None,
                                 lambda t, ft=ft, c=c, bq=bq, ps=ps: t.matmul(
                                     ps,
                                     lhsT=wT[c][:, ft * 128:(ft + 1) * 128],
                                     rhs=xn_fm[:, c * T + bq * 512:c * T + (bq + 1) * 512],
                                     start=(c == 0), stop=(c == 7)))
                        E.wait("scalar", "pe")
                        if part == 0:
                            for sb4 in range(4):
                                E.op("scalar",
                                     "ac" if sb4 == 3 else None,
                                     lambda s, ft=ft, sb4=sb4, ps=ps: s.activation(
                                         out=qsp4[:, sb4 * 1024 + ft * 128:sb4 * 1024 + (ft + 1) * 128],
                                         in_=ps[:, sb4 * 128:(sb4 + 1) * 128],
                                         func=AF.Copy, scale=SCALE))
                            if ft == 7:
                                E.wait("sync", "ac")
                                for sb4 in range(4):
                                    E.dma("sync", "qs", lambda h, bq=bq, sb4=sb4: h.dma_start(
                                        out=q_sp[4 * bq + sb4],
                                        in_=qsp4[:, sb4 * 1024:(sb4 + 1) * 1024]))
                                R[("spill", bq)] = E.c["qs"]
                        else:
                            for sb4 in range(4):
                                b = bq * 4 + sb4
                                E.op("scalar", "ac" if sb4 == 3 else None,
                                     lambda s, ft=ft, b=b, sb4=sb4, ps=ps: s.copy(
                                         out=k_fm[:, ft * T + b * 128:ft * T + (b + 1) * 128],
                                         in_=ps[:, sb4 * 128:(sb4 + 1) * 128]))
                        R[("qkevac", qk)] = E.c["ac"]
                else:
                    for sb4 in range(4):
                        b = bq * 4 + sb4
                        E.wait("tensor", "ac")
                        for half in range(2):
                            for c in range(8):
                                E.op("tensor",
                                     "pe" if (c == 7 and half == 1) else None,
                                     lambda t, c=c, b=b, half=half: t.matmul(
                                         P_S[:, half * 512:(half + 1) * 512],
                                         lhsT=xn_fm[:, c * T + b * 128:c * T + (b + 1) * 128],
                                         rhs=wT[c][:, half * 512:(half + 1) * 512],
                                         start=(c == 0), stop=(c == 7)))
                        E.wait("scalar", "pe")
                        E.wait("scalar", "gp")
                        E.op("scalar", "ac", lambda s, b=b: s.copy(
                            out=v_aug[:, b * NH * 65:(b + 1) * NH * 65]
                                .rearrange("p (h u) -> p h u", h=NH)[:, :, 0:64],
                            in_=P_S[:, 0:1024].rearrange("p (h u) -> p h u", h=NH)))
        R["phaseA_pe"] = E.c["pe"]
        R["phaseA_ac"] = E.c["ac"]
        R["phaseA_dv"] = E.c["dv"]
        R["phaseA_qs"] = E.c["qs"]

    phase_a(E)
    snapA = E.snap()

    with nc.Block() as blockA:
        for eng in ENGS:
            def _sect(handle, eng=eng):
                E.c = {k: 0 for k in sems}
                E.active = (eng, handle)
                phase_a(E)
                E.active = None
            getattr(blockA, eng)(_sect)

    ctx_a.close()

    ctx_b = ExitStack()

    def sbb(name, shape, dt=F32):
        return ctx_b.enter_context(nc.sbuf_tensor(name, shape, dt))

    woT = [sbb(f"woT{c}", [128, D], BF16) for c in range(8)]
    wqT = [sbb(f"wqT{c}", [128, 2 * D], BF16) for c in range(8)]
    keysT = sbb("keysT", [128, 16 * NKEY], BF16)
    normw_b = sbb("normw_b", [128, D], BF16)
    p_tm = sbb("p_tm", [128, T], BF16)
    ptr_sb = sbb("ptr_sb", [128, T], BF16)
    ptr2 = sbb("ptr2", [128, T], BF16)
    wstg2 = [p_tm[:].bitcast(F32), ptr_sb[:].bitcast(F32)]
    o_fm = sbb("o_fm", [128, 8 * 128], BF16)
    o_pair = sbb("o_pair", [128, 128], BF16)
    gtmp = sbb("gtmp", [128, 64], F32)
    invd_att = sbb("invd_att", [128, 1], F32)
    xnffm = [sbb(f"xnffm{i}", [128, 8 * 128], BF16) for i in range(NXF)]
    qffn = sbb("qffn", [128, 2 * 128], BF16)
    v8a = sbb("v8a", [128, 8], F32)
    v8b = sbb("v8b", [128, 8], F32)
    i1x = sbb("i1x", [128, 8], F32)
    i2x = sbb("i2x", [128, 8], F32)
    grid = sbb("grid", [128, 64], F32)
    fs8 = sbb("fs8", [128, 8], F32)
    pk8 = sbb("pk8", [128, 8], F32)
    pki = sbb("pki", [128, 8], I32)
    a8i = sbb("a8i", [128, 8], I32)
    b8i = sbb("b8i", [128, 8], I32)
    a8 = sbb("a8", [128, 8], F32)
    b8 = sbb("b8", [128, 8], F32)
    i1s = sbb("i1s", [128, 8], F32)
    i2s = sbb("i2s", [128, 8], F32)
    tq1 = sbb("tq1", [128, 8], F32)
    tq2 = sbb("tq2", [128, 8], F32)
    fi_all = [sbb(f"fi_all{i}", [128, 64], F32) for i in range(NSLOT)]
    co_all = [sbb(f"co_all{i}", [128, 64], F32) for i in range(NSLOT)]
    gfi = [sbb(f"gfi{i}", [128, 64], I32) for i in range(NSLOT)]
    gco = [sbb(f"gco{i}", [128, 64], F32) for i in range(NSLOT)]
    gfi_f = sbb("gfi_f", [128, 64], F32)
    ring_d = [sbb(f"ring_d{i}", [128, D], BF16) for i in range(2)]
    ring_u = [sbb(f"ring_u{i}", [128, D], BF16) for i in range(2)]
    ring_d = [t[:] for t in ring_d] + [p_tm[:, 0:D], p_tm[:, D:2 * D],
                                       qbuf[0][:], ptr2[:, 0:D]]
    ring_u = [t[:] for t in ring_u] + [ptr_sb[:, 0:D], ptr_sb[:, D:2 * D],
                                       qbuf[1][:], ptr2[:, D:2 * D]]
    ring_d = ring_d[:NRING]
    ring_u = ring_u[:NRING]
    wdfm = [sbb(f"wdfm{i}", [128, D], BF16) for i in range(2)]
    h4 = [sbb(f"h4_{i}", [128, 4], F32) for i in range(2)]
    hid4 = [sbb(f"hid4_{i}", [128, 4], F32) for i in range(2)]
    chid = [sbb(f"chid{i}", [128, 8], BF16) for i in range(2)]
    out_fm = sbb("out_fm", [128, 8 * 128], F32)

    def phase_b(E):
        lt_seq[0] = 24
        E.wait("sync", "pe", R["phaseA_pe"])
        E.wait("sync", "ac", R["phaseA_ac"])
        E.wait("sync", "dv", R["phaseA_dv"])
        load_transpose(E, ap["wo"], 8,
                       lambda c, ft: woT[c][:, ft * 128:(ft + 1) * 128], wstg2)
        load_transpose(E, ap["wq"], 16,
                       lambda c, ft: wqT[c][:, ft * 128:(ft + 1) * 128], wstg2)
        for h in range(FH):
            for half in range(2):
                j = 2 * h + half
                buf = wstg2[j % 2]
                if j >= 2:
                    E.wait("sync", "pe", R.get(("keytp", j - 2)))
                else:
                    E.wait("sync", "pe", R.get(("lt", lt_seq[0] - 2 + j)))
                E.dma("sync", "w", lambda s, h=h, half=half, buf=buf: s.dma_start(
                    out=buf[:, 0:256].rearrange("p (a b) -> p a b", a=2),
                    in_=ap["keys"][h, :, half, :].rearrange("(a p) d -> p a d", p=128)))
                E.wait("tensor", "w")
                E.wait("tensor", "ac")
                for c in range(2):
                    E.op("tensor", "pe" if c == 1 else None,
                         lambda t, c=c, buf=buf: t.transpose(
                             P_S[:, c * 128:(c + 1) * 128],
                             in_=buf[:, c * 128:(c + 1) * 128],
                             identity=ident_f[:]))
                R[("keytp", j)] = E.c["pe"]
                E.wait("scalar", "pe")
                for c in range(2):
                    E.op("scalar", "ac", lambda s, c=c, j=j: s.copy(
                        out=keysT[:, j * 256 + c * 128:j * 256 + (c + 1) * 128],
                        in_=P_S[:, c * 128:(c + 1) * 128]))
        E.dma("gpsimd", "g", lambda g: g.dma_start(
            out=normw_b[:], in_=ap["normw"].unsqueeze(0).broadcast_to((128, D))))
        R["w_done_ac"] = E.c["ac"]
        R["w_done_g"] = E.c["g"]

        if DBG_STAGE == "A":
            dbg_store(E)
            E.wait("sync", "st", 16 * NB)
            return
        if DBG_STAGE == "QKV":
            E.wait("gpsimd", "ac")
            E.wait("gpsimd", "qs")
            E.dma("gpsimd", "st", lambda g: g.dma_start(out=kdbg, in_=k_fm[:]))
            E.dma("gpsimd", "st", lambda g: g.dma_start(out=vdbg, in_=v_tm[:]))
            E.dma("gpsimd", "st", lambda g: g.dma_start(
                out=qdbg.rearrange("a b d -> (a b) d"),
                in_=q_sp.rearrange("a b d -> (a b) d")))
            E.dma("gpsimd", "st", lambda g: g.dma_start(
                out=out[0:128, :], in_=x_tm[:, 0:D]))
            E.wait("gpsimd", "st")
            return
        if DBG_STAGE == "ATT0":
            attention(E, only=1)
            E.wait("gpsimd", "dv")
            E.wait("gpsimd", "ac")
            E.dma("gpsimd", "st", lambda g: g.dma_start(out=ofdbg, in_=o_fm[:]))
            E.dma("gpsimd", "st", lambda g: g.dma_start(out=ptdbg[:, 0:1024], in_=qbuf[0][:]))
            E.dma("gpsimd", "st", lambda g: g.dma_start(out=ptrdbg, in_=ptr_sb[:]))
            E.dma("gpsimd", "st", lambda g: g.dma_start(
                out=dendbg[:, 0:2], in_=negB[:]))
            E.dma("gpsimd", "st", lambda g: g.dma_start(
                out=ptdbg[:, 1024:1032], in_=dbg8[:]))
            E.dma("gpsimd", "st", lambda g: g.dma_start(
                out=dendbg[:, 2:3], in_=invd_att[:], allow_slow_non_contiguous=True))
            E.dma("gpsimd", "st", lambda g: g.dma_start(
                out=dendbg[0:1, 3:4].broadcast_to((1, 1)), in_=km2[:, 15:16],
                allow_slow_non_contiguous=True))
            E.dma("gpsimd", "st", lambda g: g.dma_start(
                out=out[0:128, :], in_=x_tm[:, 0:D]))
            E.wait("gpsimd", "st")
            return
        attention(E)
        if DBG_STAGE == "ATT":
            dbg_store(E)
            E.wait("sync", "st", 16 * NB)
            return
        ffn(E)
        E.wait("sync", "st", E.c["st"])

    def dbg_store(E):
        E.wait("gpsimd", "dv")
        E.wait("gpsimd", "ac")
        E.wait("gpsimd", "pe")
        for b in range(NB):
            E.dma("gpsimd", "st", lambda g, b=b: g.dma_start(
                out=out[b * 128:(b + 1) * 128, :], in_=x_tm[:, b * D:(b + 1) * D]))

    def attention(E, only=NB):
        E.wait("sync", "qs", R["phaseA_qs"])
        E.wait("scalar", "ac", R["w_done_ac"])
        for i in range(only):
            qb = qbuf[i % 2]
            if i >= 2:
                E.wait("sync", "pe", R.get(("qfree", i - 2)))
            E.dma("sync", "ql", lambda h, i=i, qb=qb: h.dma_start(
                out=qb[:], in_=q_sp[i]))
            E.wait("tensor", "dv")
            nhl = int(_os.environ.get("K_NHLIM", str(NH)))
            for h in range(nhl):
                ft, po = h // 2, (h % 2) * 64
                hp = h % 2
                ptr = ptr_sb if (h % 2 == 0) else ptr2
                for q4 in range(4):
                    if q4 < 2:
                        E.wait("tensor", "ac")
                    E.op("tensor", "pe", lambda t, q4=q4, ft=ft, po=po, qb=qb: t.matmul(
                        P_S[:, q4 * 512:(q4 + 1) * 512],
                        lhsT=qb[po:po + 64, ft * 128:(ft + 1) * 128],
                        rhs=k_fm[po:po + 64,
                                 ft * T + q4 * 512:ft * T + (q4 + 1) * 512],
                        start=True, stop=True))
                    E.wait("vector", "pe")
                    E.op("vector", "dv", lambda v, q4=q4: v.reduce_max(
                        out=tbm[:, q4:q4 + 1], in_=P_S[:, q4 * 512:(q4 + 1) * 512],
                        axis=AXX))
                E.wait("vector", "dv")
                E.op("vector", None, lambda v: v.tensor_tensor(
                    out=tbm[:, 4:5], in0=tbm[:, 0:1], in1=tbm[:, 1:2], op=ALU.max))
                E.op("vector", "dv", lambda v: v.tensor_tensor(
                    out=tbm[:, 5:6], in0=tbm[:, 2:3], in1=tbm[:, 3:4], op=ALU.max))
                E.wait("vector", "dv")
                E.op("vector", "dv", lambda v, hp=hp: v.scalar_tensor_tensor(
                    out=negB[:, hp:hp + 1], in0=tbm[:, 4:5], scalar=-1.0,
                    in1=tbm[:, 5:6], op0=ALU.bypass, op1=ALU.max))
                E.wait("vector", "dv")
                E.op("vector", "dv", lambda v, hp=hp: v.tensor_single_scalar(
                    out=negB[:, hp:hp + 1], in_=negB[:, hp:hp + 1], scalar=-1.0,
                    op=ALU.mult))
                for g2 in range(2):
                    E.wait("scalar", "pe")
                    E.wait("scalar", "dv")
                    E.op("scalar", "ac", lambda s, g2=g2, hp=hp: s.activation(
                        out=p_tm[:, g2 * 1024:(g2 + 1) * 1024],
                        in_=P_S[:, g2 * 1024:(g2 + 1) * 1024], func=AF.Exp,
                        bias=negB[:, hp:hp + 1], scale=1.0,
                        accum_out=tbm[:, 6 + g2:7 + g2]))
                E.wait("vector", "ac")
                E.op("vector", "dv", lambda v: v.tensor_tensor(
                    out=tbm[:, 5:6], in0=tbm[:, 6:7], in1=tbm[:, 7:8], op=ALU.add))
                E.wait("vector", "dv")
                E.op("vector", "dv", lambda v: v.reciprocal(
                    invd_att[:], tbm[:, 5:6]))
                for g2 in range(2):
                    E.wait("tensor", "ac")
                    E.wait("tensor", "dv")
                    for j in range(8):
                        kb = g2 * 8 + j
                        E.op("tensor", "pe" if j == 7 else None,
                             lambda t, kb=kb, j=j: t.transpose(
                                 P_T[:, j * 128:(j + 1) * 128],
                                 in_=p_tm[:, kb * 128:(kb + 1) * 128],
                                 identity=ident_b[:]))
                    if g2 == 0:
                        E.wait("scalar", "pe")
                        E.op("scalar", "ac", lambda s, g2=g2, ptr=ptr: s.copy(
                            out=ptr[:, g2 * 1024:(g2 + 1) * 1024], in_=P_T[:, :]))
                    else:
                        E.wait("vector", "pe")
                        E.op("vector", "dv", lambda v, g2=g2, ptr=ptr: v.tensor_copy(
                            ptr[:, g2 * 1024:(g2 + 1) * 1024], P_T[:, :]))
                E.wait("tensor", "ac")
                E.wait("tensor", "dv")
                for kb in range(16):
                    E.op("tensor", "pe" if kb == 15 else None,
                         lambda t, kb=kb, h=h, ptr=ptr: t.matmul(
                             P_M[:, 256:320],
                             lhsT=ptr[:, kb * 128:(kb + 1) * 128],
                             rhs=v_aug[:, (kb * NH + h) * 65:(kb * NH + h) * 65 + 64],
                             start=(kb == 0), stop=(kb == 15)))
                E.wait("vector", "pe")
                if h % 2 == 0:
                    E.wait("vector", "pe")
                E.op("vector", "dv", lambda v, po=po: v.tensor_scalar_mul(
                    o_pair[:, po:po + 64], P_M[:, 256:320], invd_att[:]))
                if h % 2 == 1:
                    E.wait("tensor", "dv")
                    E.op("tensor", "pe", lambda t: t.transpose(
                        P_T[:, 0:128], in_=o_pair[:], identity=ident_b[:]))
                    E.wait("scalar", "pe")
                    E.op("scalar", "ac", lambda s, ft=ft: s.copy(
                        out=o_fm[:, ft * 128:(ft + 1) * 128],
                        in_=P_T[:, 0:128]))
                if h == NH - 1:
                    R[("qfree", i)] = E.c["pe"]
            E.wait("tensor", "ac")
            for half in range(2):
                for c in range(8):
                    E.op("tensor", "pe" if (c == 7 and half == 1) else None,
                         lambda t, c=c, half=half: t.matmul(
                             P_W[:, half * 512:(half + 1) * 512],
                             lhsT=o_fm[:, c * 128:(c + 1) * 128],
                             rhs=woT[c][:, half * 512:(half + 1) * 512],
                             start=(c == 0), stop=(c == 7)))
            E.wait("vector", "pe")
            E.op("vector", "dv", lambda v, i=i: v.tensor_tensor(
                out=x_tm[:, i * D:(i + 1) * D], in0=P_W[:, :],
                in1=x_tm[:, i * D:(i + 1) * D], op=ALU.add))
            R[("attnadd", i)] = E.c["dv"]
            E.wait("tensor", "dv")
        R["attn_dv"] = E.c["dv"]

    def ln2_ffn_norm(E, b):
        xs = x_tm[:, b * D:(b + 1) * D]
        E.op("vector", None, lambda v: v.tensor_reduce(
            out=sx[:], in_=xs, axis=AXX, op=ALU.add))
        E.op("vector", "dv", lambda v: v.scalar_tensor_tensor(
            out=t1b[:], in0=xs, scalar=1.0, in1=xs,
            op0=ALU.mult, op1=ALU.mult, accum_out=sxx[:]))
        E.wait("vector", "dv")
        E.op("vector", "dv", lambda v: v.tensor_single_scalar(
            out=mln[:], in_=sx[:], scalar=1.0 / D, op=ALU.mult))
        E.wait("vector", "dv")
        E.op("vector", "dv", lambda v: v.tensor_tensor(
            out=m2ln[:], in0=mln[:], in1=mln[:], op=ALU.mult))
        E.wait("vector", "dv")
        E.op("vector", "dv", lambda v: v.scalar_tensor_tensor(
            out=varln[:], in0=sxx[:], scalar=1.0 / D, in1=m2ln[:],
            op0=ALU.mult, op1=ALU.subtract))
        E.wait("scalar", "dv")
        E.op("scalar", "ac", lambda s: s.activation(
            out=stdln[:], in_=varln[:], func=AF.Sqrt, bias=epsB[:], scale=1.000001))
        E.wait("vector", "ac")
        E.op("vector", "dv", lambda v: v.reciprocal(istd[:], stdln[:]))
        E.wait("vector", "dv")
        E.op("vector", "dv", lambda v: v.scalar_tensor_tensor(
            out=t1b[:], in0=xs, scalar=mln[:], in1=normw_b[:],
            op0=ALU.subtract, op1=ALU.mult))
        E.wait("vector", "dv")
        E.op("vector", "dv", lambda v: v.tensor_scalar_mul(t1b[:], t1b[:], istd[:]))

    def routing_block(E, b, slot):
        xslot = b % NXF
        if b >= NXF:
            E.wait("vector", "pe", R.get(("pairs_done", b - NXF)))
        E.wait("tensor", "dv")
        E.wait("tensor", "ac")
        if b >= NXF:
            E.wait("tensor", "pe", R.get(("pairs_done", b - NXF)))
        for c in range(8):
            E.op("tensor", "pe" if c == 7 else None, lambda t, c=c: t.transpose(
                P_T[:, c * 128:(c + 1) * 128], in_=t1b[:, c * 128:(c + 1) * 128],
                identity=ident_b[:]))
        E.wait("scalar", "pe")
        for c in range(8):
            E.op("scalar", "ac", lambda s, c=c, xslot=xslot: s.copy(
                out=xnffm[xslot][:, c * 128:(c + 1) * 128],
                in_=P_T[:, c * 128:(c + 1) * 128]))

        fia, coa = fi_all[slot], co_all[slot]
        if b >= NSLOT:
            E.wait("vector", "g", R.get(("gath_done", b - NSLOT)))
        for h in range(FH):
            for sub in range(2):
                ft = 2 * h + sub
                E.wait("tensor", "ac")
                for c in range(8):
                    E.op("tensor", "pe" if c == 7 else None,
                         lambda t, ft=ft, c=c, xslot=xslot: t.matmul(
                             P_S[:, 0:128],
                             lhsT=wqT[c][:, ft * 128:(ft + 1) * 128],
                             rhs=xnffm[xslot][:, c * 128:(c + 1) * 128],
                             start=(c == 0), stop=(c == 7)))
                E.wait("scalar", "pe")
                E.op("scalar", "ac", lambda s, sub=sub: s.copy(
                    out=qffn[:, sub * 128:(sub + 1) * 128], in_=P_S[:, 0:128]))
            E.wait("tensor", "ac")
            E.wait("tensor", "dv")
            for half in range(2):
                j = 2 * h + half
                E.op("tensor", "pe" if half == 1 else None,
                     lambda t, half=half, j=j: t.matmul(
                         P_W[:, half * 256:(half + 1) * 256],
                         lhsT=qffn[:, half * 128:(half + 1) * 128],
                         rhs=keysT[:, j * 256:(j + 1) * 256],
                         start=True, stop=True))
            E.wait("vector", "pe")
            E.op("vector", None, lambda v: v.max(out=v8a[:], in_=P_W[:, 0:256]))
            E.op("vector", "dv", lambda v: v.max(out=v8b[:], in_=P_W[:, 256:512]))
            E.wait("vector", "dv")
            for k in range(8):
                E.op("vector", None, lambda v, k=k: v.scalar_tensor_tensor(
                    out=scr256[:], in0=P_W[:, 0:256], scalar=v8a[:, k:k + 1],
                    in1=iota256[:], op0=ALU.is_equal, op1=ALU.mult,
                    accum_out=i1x[:, k:k + 1]))
                E.op("vector", "dv" if k == 7 else None,
                     lambda v, k=k: v.scalar_tensor_tensor(
                         out=scr256[:], in0=P_W[:, 256:512], scalar=v8b[:, k:k + 1],
                         in1=iota256[:], op0=ALU.is_equal, op1=ALU.mult,
                         accum_out=i2x[:, k:k + 1]))
            E.op("vector", None, lambda v: v.tensor_tensor(
                out=grid[:].rearrange("p (a b) -> p a b", a=8),
                in0=v8a[:].unsqueeze(2).broadcast_to((128, 8, 8)),
                in1=v8b[:].unsqueeze(1).broadcast_to((128, 8, 8)),
                op=ALU.add))
            E.op("vector", "dv", lambda v: v.max(out=fs8[:], in_=grid[:]))
            E.wait("vector", "dv")
            for k in range(8):
                E.op("vector", "dv" if k == 7 else None,
                     lambda v, k=k: v.scalar_tensor_tensor(
                         out=scr256[:, 0:64], in0=grid[:], scalar=fs8[:, k:k + 1],
                         in1=iota256[:, 0:64], op0=ALU.is_equal, op1=ALU.mult,
                         accum_out=pk8[:, k:k + 1]))
            E.wait("vector", "dv")
            E.op("vector", "dv", lambda v: v.tensor_copy(pki[:], pk8[:]))
            E.wait("vector", "dv")
            E.op("vector", None, lambda v: v.tensor_single_scalar(
                out=b8i[:], in_=pki[:], scalar=7, op=ALU.bitwise_and))
            E.op("vector", "dv", lambda v: v.tensor_single_scalar(
                out=a8i[:], in_=pki[:], scalar=3, op=ALU.logical_shift_right))
            E.wait("vector", "dv")
            E.op("vector", None, lambda v: v.tensor_copy(b8[:], b8i[:]))
            E.op("vector", None, lambda v: v.tensor_copy(a8[:], a8i[:]))
            E.op("vector", None, lambda v: v.memset(i1s[:], 0.0))
            E.op("vector", "dv", lambda v: v.memset(i2s[:], 0.0))
            E.wait("vector", "dv")
            for r in range(8):
                E.op("vector", None, lambda v, r=r: v.scalar_tensor_tensor(
                    out=tq1[:], in0=a8[:], scalar=float(r),
                    in1=i1x[:, r:r + 1].broadcast_to((128, 8)),
                    op0=ALU.is_equal, op1=ALU.mult))
                E.op("vector", None, lambda v, r=r: v.scalar_tensor_tensor(
                    out=tq2[:], in0=b8[:], scalar=float(r),
                    in1=i2x[:, r:r + 1].broadcast_to((128, 8)),
                    op0=ALU.is_equal, op1=ALU.mult))
                E.op("vector", None, lambda v: v.tensor_add(
                    out=i1s[:], in0=i1s[:], in1=tq1[:]))
                E.op("vector", "dv" if r == 7 else None, lambda v: v.tensor_add(
                    out=i2s[:], in0=i2s[:], in1=tq2[:]))
            E.wait("vector", "dv")
            E.op("vector", "dv", lambda v, h=h, fia=fia: v.scalar_tensor_tensor(
                out=fia[:, h * 8:(h + 1) * 8], in0=i1s[:], scalar=256.0, in1=i2s[:],
                op0=ALU.mult, op1=ALU.add))
            E.wait("vector", "dv")
            E.op("vector", None, lambda v, h=h, fia=fia: v.tensor_single_scalar(
                out=fia[:, h * 8:(h + 1) * 8], in_=fia[:, h * 8:(h + 1) * 8],
                scalar=65535.0, op=ALU.min))
            E.op("vector", "dv", lambda v, h=h, coa=coa: v.tensor_single_scalar(
                out=coa[:, h * 8:(h + 1) * 8], in_=fs8[:], scalar=0.0, op=ALU.max))

        E.wait("tensor", "dv")
        for (srct, tgt) in ((fia, gfi_f), (coa, gco[slot])):
            E.wait("tensor", "gt")
            E.op("tensor", "pe", lambda t, srct=srct: t.transpose(
                P_W[0:64, 512:640], in_=srct[:], identity=ident_f[:]))
            E.wait("vector", "pe")
            E.wait("vector", "gt")
            E.op("vector", None, lambda v, tgt=tgt: v.tensor_copy(
                tgt[0:64, :], P_W[0:64, 512:640][:, 0:128:2]))
            E.op("vector", "dv", lambda v: v.tensor_copy(
                gtmp[0:64, :], P_W[0:64, 512:640][:, 1:128:2]))
            E.wait("sync", "dv")
            E.dma("sync", "gt", lambda h, tgt=tgt: h.dma_start(
                out=tgt[64:128, :], in_=gtmp[0:64, :]))
            E.wait("tensor", "dv")
        E.wait("vector", "gt")
        E.op("vector", "dv", lambda v: v.tensor_copy(gfi[slot][:], gfi_f[:]))
        R[("route_done", b)] = E.c["dv"]
        R[("route_done_gt", b)] = E.c["gt"]

    pair_free = {}
    gath_val = {}

    def expert_block(E, b, slot):
        E.wait("vector", "gt", R[("route_done_gt", b)])
        E.op("vector", "dv", lambda v: v.memset(out_fm[:], 0.0))
        E.wait("tensor", "dv")
        def emit_ups(k, slot):
            gbk = k % 2
            mbase = (k * 2) % NPAIR
            bk = (k * 2) // NPAIR
            E.wait("tensor", "dv")
            for mm in range(mbase, mbase + 2):
                pp = bk * NPAIR + mm
                rr = pp % NRING
                upsum = P_W[:, 656:672] if mm % 2 == 0 else P_M[:, 256:272]
                if pp >= 2:
                    E.wait("tensor", "dv", R.get(("up_evac", pp - 2)))
                for c in range(8):
                    E.op("tensor", "pe" if c == 7 else None,
                         lambda t, c=c, rr=rr, mm=mm, gbk=gbk, upsum=upsum: t.matmul(
                             upsum[:, c * 2:c * 2 + 2],
                             lhsT=ring_u[rr][:, c * 128:(c + 1) * 128],
                             rhs=chid[gbk][:, (mm % 2) * 2:(mm % 2) * 2 + 2],
                             start=True, stop=True))
                pair_free[pp] = E.c["pe"]
                E.wait("vector", "pe")
                E.op("vector", "dv", lambda v, mm=mm, upsum=upsum: v.tensor_tensor(
                    out=out_fm[:].rearrange("p (c t) -> p c t", c=8)
                        [:, :, 2 * mm:2 * mm + 2],
                    in0=upsum.rearrange("p (c t) -> p c t", t=2),
                    in1=out_fm[:].rearrange("p (c t) -> p c t", c=8)
                        [:, :, 2 * mm:2 * mm + 2],
                    op=ALU.add))
                R[("up_evac", pp)] = E.c["dv"]
            R[("upg_done", k)] = E.c["dv"]
            R[("upmm", k)] = E.c["pe"]

        for m in range(NPAIR):
            p = b * NPAIR + m
            r = p % NRING
            mg = m % 2
            gg = p // 2
            gb = gg % 2
            E.wait("gpsimd", "dv", R[("route_done", b)])
            E.wait("gpsimd", "gt", R[("route_done_gt", b)])
            if p >= NRING:
                E.wait("gpsimd", "pe", pair_free.get(p - NRING))
            E.dma("gpsimd", "g", lambda g, m=m, r=r, slot=slot: g.indirect_dma_start(
                out=ring_d[r], out_offset=None, in_=ap["edown"],
                in_offset=IndirectOffsetOnAxis(ap=gfi[slot][:, m:m + 1], axis=0)))
            E.dma("gpsimd", "g", lambda g, m=m, r=r, slot=slot: g.indirect_dma_start(
                out=ring_u[r], out_offset=None, in_=ap["eup"],
                in_offset=IndirectOffsetOnAxis(ap=gfi[slot][:, m:m + 1], axis=0)))
            gath_val[p] = E.c["g"]
            E.wait("tensor", "g", gath_val[p])
            E.wait("tensor", "dv")
            E.wait("tensor", "ac")
            for c in range(8):
                E.op("tensor", "pe" if c == 7 else None,
                     lambda t, c=c, r=r: t.transpose(
                         P_T[:, c * 128:(c + 1) * 128],
                         in_=ring_d[r][:, c * 128:(c + 1) * 128],
                         identity=ident_b[:]))
            if mg == 0:
                E.wait("scalar", "pe")
                E.op("scalar", "ac", lambda s, mg=mg: s.copy(
                    out=wdfm[mg][:], in_=P_T[:, :]))
                E.wait("tensor", "ac")
            else:
                E.wait("vector", "pe")
                E.op("vector", "dv", lambda v, mg=mg: v.tensor_copy(
                    wdfm[mg][:], P_T[:, :]))
                E.wait("tensor", "dv")
            dbase = 640 + gb * 4 + mg * 2
            if mg == 0 and gg >= 2:
                E.wait("tensor", "dv", R.get(("diag_free", gg - 2)))
            if mg == 0 and gg >= 2:
                E.wait("tensor", "dv", R.get(("upg_done", gg - 2)))
            for c in range(8):
                E.op("tensor", "pe" if c == 7 else None,
                     lambda t, c=c, mg=mg, m=m, xslot=b % NXF, dbase=dbase: t.matmul(
                         P_W[:, dbase:dbase + 2],
                         lhsT=wdfm[mg][:, c * 128:(c + 1) * 128],
                         rhs=xnffm[xslot][:, c * 128 + 2 * m:c * 128 + 2 * m + 2],
                         start=(c == 0), stop=(c == 7)))
            if mg == 1:
                base = 640 + gb * 4
                E.wait("vector", "pe")
                if gg >= 2:
                    E.wait("vector", "pe", R.get(("upmm", gg - 2)))
                E.op("vector", None, lambda v, base=base, gb=gb: v.tensor_copy(
                    h4[gb][0:64, 0:2],
                    P_W[0:64, base:base + 4].rearrange("p (a b) -> p a b", a=2)[:, :, 0]))
                E.op("vector", "dv", lambda v, base=base, gb=gb: v.tensor_copy(
                    h4[gb][64:128, 0:2],
                    P_W[64:128, base:base + 4].rearrange("p (a b) -> p a b", a=2)[:, :, 1]))
                R[("diag_free", gg)] = E.c["dv"]
                E.wait("scalar", "dv")
                E.op("scalar", "ac", lambda s, gb=gb: s.activation(
                    out=h4[gb][:, 0:2], in_=h4[gb][:, 0:2], func=AF.Gelu))
                E.wait("vector", "ac")
                E.op("vector", "dv", lambda v, gb=gb, m=m, slot=slot: v.tensor_tensor(
                    out=hid4[gb][:, 0:2], in0=h4[gb][:, 0:2],
                    in1=gco[slot][:, m - 1:m + 1], op=ALU.mult))
                E.wait("vector", "dv")
                E.op("vector", None, lambda v, gb=gb: v.memset(chid[gb][:, 0:4], 0.0))
                E.op("vector", None, lambda v, gb=gb: v.tensor_copy(
                    chid[gb][0:64, 0:4].rearrange("p (a b) -> p a b", a=2)[:, :, 0],
                    hid4[gb][0:64, 0:2]))
                E.op("vector", "dv", lambda v, gb=gb: v.tensor_copy(
                    chid[gb][64:128, 0:4].rearrange("p (a b) -> p a b", a=2)[:, :, 1],
                    hid4[gb][64:128, 0:2]))
                if gg >= 1 and (gg - 1) >= (b * NPAIR) // 2:
                    emit_ups(gg - 1, slot)
        emit_ups(b * NPAIR // 2 + NPAIR // 2 - 1, slot)
        R[("pairs_done", b)] = E.c["pe"]
        R[("gath_done", b)] = E.c["g"]
        E.wait("tensor", "dv", R[("up_evac", b * NPAIR + NPAIR - 1)])
        for c in range(8):
            E.op("tensor", "pe" if c == 7 else None, lambda t, c=c: t.transpose(
                P_S[:, 1024 + c * 128:1024 + (c + 1) * 128],
                in_=out_fm[:, c * 128:(c + 1) * 128], identity=ident_f[:]))
        E.wait("vector", "pe")
        E.op("vector", "dv", lambda v, b=b: v.tensor_tensor(
            out=x_tm[:, b * D:(b + 1) * D], in0=P_S[:, 1024:2048],
            in1=x_tm[:, b * D:(b + 1) * D], op=ALU.add))
        E.wait("gpsimd", "dv")
        E.dma("gpsimd", "st", lambda g, b=b: g.dma_start(
            out=out[b * 128:(b + 1) * 128, :], in_=x_tm[:, b * D:(b + 1) * D]))

    def ffn(E):
        nbf = int(_os.environ.get("K_NBFFN", str(NB)))
        if FFN_AHEAD:
            for b in range(nbf):
                E.wait("vector", "dv", R[("attnadd", b)])
                ln2_ffn_norm(E, b)
                routing_block(E, b, b % NSLOT)
                if b >= 1:
                    expert_block(E, b - 1, (b - 1) % NSLOT)
            expert_block(E, nbf - 1, (nbf - 1) % NSLOT)
        else:
            for b in range(NB):
                E.wait("vector", "dv", R[("attnadd", b)])
                ln2_ffn_norm(E, b)
                routing_block(E, b, b % NSLOT)
                expert_block(E, b, b % NSLOT)

    E.c = dict(snapA)
    phase_b(E)

    with nc.Block() as blockB:
        for eng in ENGS:
            def _sect(handle, eng=eng):
                E.c = dict(snapA)
                E.active = (eng, handle)
                phase_b(E)
                E.active = None
            getattr(blockB, eng)(_sect)

    ctx_b.close()
    ctx_p.close()
    return nc


def kernel(L0_attn_in_features, L0_attn_W_qkv, L0_attn_W_o, L0_ffn_w_q,
           L0_ffn_keys, L0_ffn_expert_down, L0_ffn_expert_up, L0_ffn_norm_w):
    x = np.ascontiguousarray(L0_attn_in_features)
    nc = bass.Bass("TRN2", target_bir_lowering=False)
    build_program(nc)
    shared = {
        "wqkv": np.ascontiguousarray(L0_attn_W_qkv),
        "wo": np.ascontiguousarray(L0_attn_W_o),
        "wq": np.ascontiguousarray(L0_ffn_w_q),
        "keys": np.ascontiguousarray(L0_ffn_keys),
        "edown": np.ascontiguousarray(L0_ffn_expert_down),
        "eup": np.ascontiguousarray(L0_ffn_expert_up),
        "normw": np.ascontiguousarray(L0_ffn_norm_w),
    }
    in_maps = [dict(shared, x=np.ascontiguousarray(x[b])) for b in range(8)]
    res = run_bass_kernel_spmd(nc, in_maps, core_ids=list(range(8)))
    return np.stack([np.asarray(r["out"]) for r in res.results], axis=0).astype(np.float32)


# version 07
# speedup vs baseline: 1.0885x; 1.0885x over previous
import numpy as np

import concourse.bass as bass
import concourse.mybir as mybir
from concourse.bass_utils import run_bass_kernel_spmd
from concourse.bass_core_rust import IndirectOffsetOnAxis

F32 = mybir.dt.float32
BF16 = mybir.dt.bfloat16
I32 = mybir.dt.int32
AF = mybir.ActivationFunctionType
ALU = mybir.AluOpType
AXX = mybir.AxisListType.X

DBG_STAGE = None
FFN_AHEAD = True

T = 2048
D = 1024
NB = T // 128
NH = 16
FH = 8
NKEY = 256
NPAIR = 64
SCALE = 0.125
import os as _os
NRING = int(_os.environ.get("K_NRING", "6"))
NSLOT = 2
NXF = 2

ENGS = ("sync", "tensor", "vector", "scalar", "gpsimd")


class Emit:

    RECORD = False

    def __init__(self, nc, sems):
        self.nc = nc
        self.sems = sems
        self.active = None
        self.c = {k: 0 for k in sems}
        self.streams = {}

    def _line(self):
        import traceback
        out = []
        for fr in reversed(traceback.extract_stack()):
            if "kernel" in fr.filename.rsplit("/", 1)[-1]:
                out.append(f"{fr.name}:{fr.lineno}")
                if len(out) == 2:
                    break
        return ",".join(out)

    def op(self, eng, sem, fn):
        if sem is not None:
            self.c[sem] += 1
        if self.active is not None and self.active[0] == eng:
            if Emit.RECORD:
                self.streams.setdefault(eng, []).append(("i", sem, 1, self._line()))
                return
            ins = fn(self.active[1])
            if sem is not None:
                ins.then_inc(self.sems[sem], 1)

    def dma(self, eng, sem, fn):
        if sem is not None:
            self.c[sem] += 16
        if self.active is not None and self.active[0] == eng:
            if Emit.RECORD:
                self.streams.setdefault(eng, []).append(("i", sem, 16, self._line()))
                return
            ins = fn(self.active[1])
            if sem is not None:
                ins.then_inc(self.sems[sem], 16)

    def wait(self, eng, sem, val=None):
        if self.active is not None and self.active[0] == eng:
            tgt = self.c[sem] if val is None else val
            if tgt > 0:
                if Emit.RECORD:
                    self.streams.setdefault(eng, []).append(("w", sem, tgt, self._line()))
                    return
                self.active[1].wait_ge(self.sems[sem], tgt)

    def snap(self):
        return dict(self.c)

    _INSTANCES = []


def build_program(nc):
    ap = {}
    for nm, sh, dt in [
        ("x", [T, D], F32), ("wqkv", [3 * D, D], F32), ("wo", [D, D], F32),
        ("wq", [2 * D, D], F32), ("keys", [FH, NKEY, 2, 128], F32),
        ("edown", [65536, D], F32), ("eup", [65536, D], F32),
        ("normw", [D], F32),
    ]:
        ap[nm] = nc.dram_tensor(nm, sh, dt, kind="ExternalInput").ap()
    out = nc.dram_tensor("out", [T, D], F32, kind="ExternalOutput").ap()
    if DBG_STAGE == "QKV":
        kdbg = nc.dram_tensor("kdbg", [128, 8 * T], BF16, kind="ExternalOutput").ap()
        vdbg = nc.dram_tensor("vdbg", [128, NB * D], BF16, kind="ExternalOutput").ap()
        qdbg = nc.dram_tensor("qdbg", [NB, 128, 8 * 128], BF16, kind="ExternalOutput").ap()
    if DBG_STAGE == "ATT0":
        ofdbg = nc.dram_tensor("ofdbg", [128, 8 * 128], BF16, kind="ExternalOutput").ap()
        ptdbg = nc.dram_tensor("ptdbg", [128, T], BF16, kind="ExternalOutput").ap()
        ptrdbg = nc.dram_tensor("ptrdbg", [128, T], BF16, kind="ExternalOutput").ap()
        dendbg = nc.dram_tensor("dendbg", [128, 4], F32, kind="ExternalOutput").ap()
    q_sp = nc.dram_tensor("q_sp", [NB, 128, 8 * 128], BF16, kind="Internal").ap()

    from contextlib import ExitStack
    ctx_p = ExitStack()
    sem_names = ["w", "xl", "qs", "ql", "g", "pe", "dv", "ac", "st", "gp", "gt"]
    sems = {nm: ctx_p.enter_context(nc.semaphore("s_" + nm)) for nm in sem_names}
    E = Emit(nc, sems)
    Emit._INSTANCES.append(E)

    def sbp(name, shape, dt=F32):
        return ctx_p.enter_context(nc.sbuf_tensor(name, shape, dt))

    x_tm = sbp("x_tm", [128, NB * D], BF16)
    k_fm = sbp("k_fm", [128, 8 * T], BF16)
    v_aug = sbp("v_aug", [128, NB * NH * 65], BF16)
    ones_c = sbp("ones_c", [128, 1], BF16)
    ones_r = sbp("ones_r", [1, 128], BF16)
    km2 = sbp("km2", [1, 16], F32)
    tbm = sbp("tbm", [128, 12], F32)
    rowm = sbp("rowm", [128, 1], BF16)
    tb1 = sbp("tb1", [1, 4], F32)
    tb2 = sbp("tb2", [1, 1], F32)
    tb3 = sbp("tb3", [1, 1], BF16)
    negB = sbp("negB", [128, 2], F32)
    dbg8 = sbp("dbg8", [128, 8], F32)
    qbuf = [sbp(f"qbuf{i}", [128, 8 * 128], BF16) for i in range(2)]
    ident_b = sbp("ident_b", [128, 128], BF16)
    ident_f = sbp("ident_f", [128, 128], F32)
    iota256 = sbp("iota256", [128, NKEY], F32)
    epsA = sbp("epsA", [128, 1], F32)
    epsB = sbp("epsB", [128, 1], F32)
    t1b = sbp("t1b", [128, D], BF16)
    sx = sbp("sx", [128, 1], F32)
    sxx = sbp("sxx", [128, 1], F32)
    mln = sbp("mln", [128, 1], F32)
    m2ln = sbp("m2ln", [128, 1], F32)
    varln = sbp("varln", [128, 1], F32)
    stdln = sbp("stdln", [128, 1], F32)
    istd = sbp("istd", [128, 1], F32)
    scr256 = sbp("scr256", [128, NKEY], F32)

    P_S = ctx_p.enter_context(nc.psum_tensor("P_S", [128, 2048], F32))
    P_T = ctx_p.enter_context(nc.psum_tensor("P_T", [128, 1024], BF16))
    P_W = ctx_p.enter_context(nc.psum_tensor("P_W", [128, 1024], F32))
    P_M = ctx_p.enter_context(nc.psum_tensor("P_M", [128, 512], F32))

    ctx_a = ExitStack()
    xn_fm = ctx_a.enter_context(nc.sbuf_tensor("xn_fm", [128, 8 * T], BF16))
    wT = [ctx_a.enter_context(nc.sbuf_tensor(f"wT{c}", [128, D], BF16))
          for c in range(8)]
    wstg = [ctx_a.enter_context(nc.sbuf_tensor(f"wstg{i}", [128, D], F32))
            for i in range(2)]
    xf32 = [ctx_a.enter_context(nc.sbuf_tensor(f"xf32{i}", [128, D], F32))
            for i in range(2)]
    qsp4 = ctx_a.enter_context(nc.sbuf_tensor("qsp4", [128, 4096], BF16))

    R = {}
    lt_seq = [0]
    qk_seq = [0]

    def load_transpose(E, src, nft, dst, stage):
        for ft in range(nft):
            g = lt_seq[0]
            lt_seq[0] += 1
            buf = stage[g % 2]
            if g >= 2:
                E.wait("sync", "pe", R.get(("lt", g - 2)))
            E.dma("sync", "w", lambda h, ft=ft, buf=buf: h.dma_start(
                out=buf[:], in_=src[ft * 128:(ft + 1) * 128, :]))
            E.wait("tensor", "w")
            E.wait("tensor", "ac")
            for c in range(8):
                E.op("tensor", "pe" if c == 7 else None,
                     lambda t, c=c, buf=buf: t.transpose(
                         P_S[:, c * 128:(c + 1) * 128],
                         in_=buf[:, c * 128:(c + 1) * 128], identity=ident_f[:]))
            R[("lt", g)] = E.c["pe"]
            E.wait("scalar", "pe")
            for c in range(8):
                E.op("scalar", "ac", lambda s, c=c, ft=ft: s.copy(
                    out=dst(c, ft), in_=P_S[:, c * 128:(c + 1) * 128]))

    def phase_a(E):
        lt_seq[0] = 0
        qk_seq[0] = 0
        E.op("gpsimd", None, lambda g: g.memset(epsA[:], 1e-5))
        E.op("gpsimd", None, lambda g: g.memset(ones_c[:], 1.0))
        E.op("gpsimd", None, lambda g: g.memset(ones_r[:], 1.0))
        E.op("gpsimd", "gp", lambda g: g.memset(v_aug[:], 1.0))
        E.op("gpsimd", None, lambda g: g.memset(epsB[:], 1e-11))
        E.op("gpsimd", None, lambda g: g.iota(
            iota256[:], pattern=[[1, NKEY]], base=0, channel_multiplier=0,
            allow_small_or_imprecise_dtypes=True))
        E.op("gpsimd", None, lambda g: g.memset(ident_b[:], 0.0))
        E.op("gpsimd", "gp", lambda g: g.affine_select(
            out=ident_b[:], in_=ident_b[:], compare_op=ALU.not_equal,
            fill=1.0, base=0, pattern=[[-1, 128]], channel_multiplier=1))
        E.op("gpsimd", None, lambda g: g.memset(ident_f[:], 0.0))
        E.op("gpsimd", "gp", lambda g: g.affine_select(
            out=ident_f[:], in_=ident_f[:], compare_op=ALU.not_equal,
            fill=1.0, base=0, pattern=[[-1, 128]], channel_multiplier=1))
        E.wait("tensor", "gp")
        E.wait("vector", "gp")

        for b in range(NB):
            xb = xf32[b % 2]
            if b >= 2:
                E.wait("sync", "dv", R.get(("xfree", b - 2)))
            E.dma("sync", "xl", lambda h, b=b, xb=xb: h.dma_start(
                out=xb[:], in_=ap["x"][b * 128:(b + 1) * 128, :]))
            E.wait("vector", "xl")
            E.op("vector", None, lambda v, xb=xb: v.tensor_reduce(
                out=sx[:], in_=xb[:], axis=AXX, op=ALU.add))
            E.op("vector", "dv", lambda v, xb=xb: v.scalar_tensor_tensor(
                out=t1b[:], in0=xb[:], scalar=1.0, in1=xb[:],
                op0=ALU.mult, op1=ALU.mult, accum_out=sxx[:]))
            E.wait("vector", "dv")
            E.op("vector", "dv", lambda v: v.tensor_single_scalar(
                out=mln[:], in_=sx[:], scalar=1.0 / D, op=ALU.mult))
            E.wait("vector", "dv")
            E.op("vector", "dv", lambda v: v.tensor_tensor(
                out=m2ln[:], in0=mln[:], in1=mln[:], op=ALU.mult))
            E.wait("vector", "dv")
            E.op("vector", "dv", lambda v: v.scalar_tensor_tensor(
                out=varln[:], in0=sxx[:], scalar=1.0 / D, in1=m2ln[:],
                op0=ALU.mult, op1=ALU.subtract))
            E.wait("scalar", "dv")
            E.op("scalar", "ac", lambda s: s.activation(
                out=stdln[:], in_=varln[:], func=AF.Sqrt, bias=epsA[:], scale=1.0))
            E.wait("vector", "ac")
            E.op("vector", "dv", lambda v: v.reciprocal(istd[:], stdln[:]))
            E.wait("vector", "dv")
            E.op("vector", "dv", lambda v, xb=xb: v.scalar_tensor_tensor(
                out=t1b[:], in0=xb[:], scalar=mln[:],
                in1=istd[:].broadcast_to((128, D)),
                op0=ALU.subtract, op1=ALU.mult))
            E.op("vector", "dv", lambda v, b=b, xb=xb: v.tensor_copy(
                x_tm[:, b * D:(b + 1) * D], xb[:]))
            R[("xfree", b)] = E.c["dv"]
            E.wait("tensor", "dv")
            E.wait("tensor", "ac")
            for c in range(8):
                E.op("tensor", "pe" if c == 7 else None, lambda t, c=c: t.transpose(
                    P_T[:, c * 128:(c + 1) * 128], in_=t1b[:, c * 128:(c + 1) * 128],
                    identity=ident_b[:]))
            E.wait("scalar", "pe")
            for c in range(8):
                E.op("scalar", "ac", lambda s, c=c, b=b: s.copy(
                    out=xn_fm[:, c * T + b * 128:c * T + (b + 1) * 128],
                    in_=P_T[:, c * 128:(c + 1) * 128]))

        for part in range(3):
            load_transpose(E, ap["wqkv"][part * D:(part + 1) * D, :], 8,
                           lambda c, ft: wT[c][:, ft * 128:(ft + 1) * 128],
                           wstg)
            E.wait("tensor", "ac")
            for bq in range(NB // 4):
                if part < 2:
                    if part == 0 and bq >= 1:
                        E.wait("scalar", "qs", R.get(("spill", bq - 1)))
                    for ft in range(8):
                        qk = qk_seq[0]
                        qk_seq[0] += 1
                        ps = P_M[:, 0:512] if qk % 2 == 0 else P_W[:, 0:512]
                        if qk >= 2:
                            E.wait("tensor", "ac", R.get(("qkevac", qk - 2)))
                        for c in range(8):
                            E.op("tensor", "pe" if c == 7 else None,
                                 lambda t, ft=ft, c=c, bq=bq, ps=ps: t.matmul(
                                     ps,
                                     lhsT=wT[c][:, ft * 128:(ft + 1) * 128],
                                     rhs=xn_fm[:, c * T + bq * 512:c * T + (bq + 1) * 512],
                                     start=(c == 0), stop=(c == 7)))
                        E.wait("scalar", "pe")
                        if part == 0:
                            for sb4 in range(4):
                                E.op("scalar",
                                     "ac" if sb4 == 3 else None,
                                     lambda s, ft=ft, sb4=sb4, ps=ps: s.activation(
                                         out=qsp4[:, sb4 * 1024 + ft * 128:sb4 * 1024 + (ft + 1) * 128],
                                         in_=ps[:, sb4 * 128:(sb4 + 1) * 128],
                                         func=AF.Copy, scale=SCALE))
                            if ft == 7:
                                E.wait("sync", "ac")
                                for sb4 in range(4):
                                    E.dma("sync", "qs", lambda h, bq=bq, sb4=sb4: h.dma_start(
                                        out=q_sp[4 * bq + sb4],
                                        in_=qsp4[:, sb4 * 1024:(sb4 + 1) * 1024]))
                                R[("spill", bq)] = E.c["qs"]
                        else:
                            for sb4 in range(4):
                                b = bq * 4 + sb4
                                E.op("scalar", "ac" if sb4 == 3 else None,
                                     lambda s, ft=ft, b=b, sb4=sb4, ps=ps: s.copy(
                                         out=k_fm[:, ft * T + b * 128:ft * T + (b + 1) * 128],
                                         in_=ps[:, sb4 * 128:(sb4 + 1) * 128]))
                        R[("qkevac", qk)] = E.c["ac"]
                else:
                    for sb4 in range(4):
                        b = bq * 4 + sb4
                        E.wait("tensor", "ac")
                        for half in range(2):
                            for c in range(8):
                                E.op("tensor",
                                     "pe" if (c == 7 and half == 1) else None,
                                     lambda t, c=c, b=b, half=half: t.matmul(
                                         P_S[:, half * 512:(half + 1) * 512],
                                         lhsT=xn_fm[:, c * T + b * 128:c * T + (b + 1) * 128],
                                         rhs=wT[c][:, half * 512:(half + 1) * 512],
                                         start=(c == 0), stop=(c == 7)))
                        E.wait("scalar", "pe")
                        E.wait("scalar", "gp")
                        E.op("scalar", "ac", lambda s, b=b: s.copy(
                            out=v_aug[:, b * NH * 65:(b + 1) * NH * 65]
                                .rearrange("p (h u) -> p h u", h=NH)[:, :, 0:64],
                            in_=P_S[:, 0:1024].rearrange("p (h u) -> p h u", h=NH)))
        R["phaseA_pe"] = E.c["pe"]
        R["phaseA_ac"] = E.c["ac"]
        R["phaseA_dv"] = E.c["dv"]
        R["phaseA_qs"] = E.c["qs"]

    phase_a(E)
    snapA = E.snap()

    with nc.Block() as blockA:
        for eng in ENGS:
            def _sect(handle, eng=eng):
                E.c = {k: 0 for k in sems}
                E.active = (eng, handle)
                phase_a(E)
                E.active = None
            getattr(blockA, eng)(_sect)

    ctx_a.close()

    ctx_b = ExitStack()

    def sbb(name, shape, dt=F32):
        return ctx_b.enter_context(nc.sbuf_tensor(name, shape, dt))

    woT = [sbb(f"woT{c}", [128, D], BF16) for c in range(8)]
    wqT = [sbb(f"wqT{c}", [128, 2 * D], BF16) for c in range(8)]
    keysT = sbb("keysT", [128, 16 * NKEY], BF16)
    normw_b = sbb("normw_b", [128, D], BF16)
    p_tm = sbb("p_tm", [128, T], BF16)
    ptr_sb = sbb("ptr_sb", [128, T], BF16)
    ptr2 = sbb("ptr2", [128, T], BF16)
    wstg2 = [p_tm[:].bitcast(F32), ptr_sb[:].bitcast(F32)]
    o_fm = sbb("o_fm", [128, 8 * 128], BF16)
    o_pair = sbb("o_pair", [128, 128], BF16)
    gtmp = sbb("gtmp", [128, 64], F32)
    invd_att = sbb("invd_att", [128, 2], F32)
    xnffm = [sbb(f"xnffm{i}", [128, 8 * 128], BF16) for i in range(NXF)]
    qffn = sbb("qffn", [128, 2 * 128], BF16)
    v8a = sbb("v8a", [128, 8], F32)
    v8b = sbb("v8b", [128, 8], F32)
    i1x = sbb("i1x", [128, 8], F32)
    i2x = sbb("i2x", [128, 8], F32)
    grid = sbb("grid", [128, 64], F32)
    fs8 = sbb("fs8", [128, 8], F32)
    pk8 = sbb("pk8", [128, 8], F32)
    pki = sbb("pki", [128, 8], I32)
    a8i = sbb("a8i", [128, 8], I32)
    b8i = sbb("b8i", [128, 8], I32)
    a8 = sbb("a8", [128, 8], F32)
    b8 = sbb("b8", [128, 8], F32)
    i1s = sbb("i1s", [128, 8], F32)
    i2s = sbb("i2s", [128, 8], F32)
    tq1 = sbb("tq1", [128, 8], F32)
    tq2 = sbb("tq2", [128, 8], F32)
    fi_all = [sbb(f"fi_all{i}", [128, 64], F32) for i in range(NSLOT)]
    co_all = [sbb(f"co_all{i}", [128, 64], F32) for i in range(NSLOT)]
    gfi = [sbb(f"gfi{i}", [128, 64], I32) for i in range(NSLOT)]
    gco = [sbb(f"gco{i}", [128, 64], F32) for i in range(NSLOT)]
    gfi_f = sbb("gfi_f", [128, 64], F32)
    ring_d = [sbb(f"ring_d{i}", [128, D], BF16) for i in range(2)]
    ring_u = [sbb(f"ring_u{i}", [128, D], BF16) for i in range(2)]
    p_tmB = [t[:] for t in ring_d]
    ring_d = [t[:] for t in ring_d] + [p_tm[:, 0:D], p_tm[:, D:2 * D],
                                       qbuf[0][:], ptr2[:, 0:D]]
    ring_u = [t[:] for t in ring_u] + [ptr_sb[:, 0:D], ptr_sb[:, D:2 * D],
                                       qbuf[1][:], ptr2[:, D:2 * D]]
    ring_d = ring_d[:NRING]
    ring_u = ring_u[:NRING]
    wdfm = [sbb(f"wdfm{i}", [128, D], BF16) for i in range(2)]
    h4 = [sbb(f"h4_{i}", [128, 4], F32) for i in range(2)]
    hid4 = [sbb(f"hid4_{i}", [128, 4], F32) for i in range(2)]
    chid = [sbb(f"chid{i}", [128, 8], BF16) for i in range(2)]
    out_fm = sbb("out_fm", [128, 8 * 128], F32)

    def phase_b(E):
        lt_seq[0] = 24
        E.wait("sync", "pe", R["phaseA_pe"])
        E.wait("sync", "ac", R["phaseA_ac"])
        E.wait("sync", "dv", R["phaseA_dv"])
        load_transpose(E, ap["wo"], 8,
                       lambda c, ft: woT[c][:, ft * 128:(ft + 1) * 128], wstg2)
        load_transpose(E, ap["wq"], 16,
                       lambda c, ft: wqT[c][:, ft * 128:(ft + 1) * 128], wstg2)
        for h in range(FH):
            for half in range(2):
                j = 2 * h + half
                buf = wstg2[j % 2]
                if j >= 2:
                    E.wait("sync", "pe", R.get(("keytp", j - 2)))
                else:
                    E.wait("sync", "pe", R.get(("lt", lt_seq[0] - 2 + j)))
                E.dma("sync", "w", lambda s, h=h, half=half, buf=buf: s.dma_start(
                    out=buf[:, 0:256].rearrange("p (a b) -> p a b", a=2),
                    in_=ap["keys"][h, :, half, :].rearrange("(a p) d -> p a d", p=128)))
                E.wait("tensor", "w")
                E.wait("tensor", "ac")
                for c in range(2):
                    E.op("tensor", "pe" if c == 1 else None,
                         lambda t, c=c, buf=buf: t.transpose(
                             P_S[:, c * 128:(c + 1) * 128],
                             in_=buf[:, c * 128:(c + 1) * 128],
                             identity=ident_f[:]))
                R[("keytp", j)] = E.c["pe"]
                E.wait("scalar", "pe")
                for c in range(2):
                    E.op("scalar", "ac", lambda s, c=c, j=j: s.copy(
                        out=keysT[:, j * 256 + c * 128:j * 256 + (c + 1) * 128],
                        in_=P_S[:, c * 128:(c + 1) * 128]))
        E.dma("gpsimd", "g", lambda g: g.dma_start(
            out=normw_b[:], in_=ap["normw"].unsqueeze(0).broadcast_to((128, D))))
        R["w_done_ac"] = E.c["ac"]
        R["w_done_g"] = E.c["g"]

        if DBG_STAGE == "A":
            dbg_store(E)
            E.wait("sync", "st", 16 * NB)
            return
        if DBG_STAGE == "QKV":
            E.wait("gpsimd", "ac")
            E.wait("gpsimd", "qs")
            E.dma("gpsimd", "st", lambda g: g.dma_start(out=kdbg, in_=k_fm[:]))
            E.dma("gpsimd", "st", lambda g: g.dma_start(out=vdbg, in_=v_tm[:]))
            E.dma("gpsimd", "st", lambda g: g.dma_start(
                out=qdbg.rearrange("a b d -> (a b) d"),
                in_=q_sp.rearrange("a b d -> (a b) d")))
            E.dma("gpsimd", "st", lambda g: g.dma_start(
                out=out[0:128, :], in_=x_tm[:, 0:D]))
            E.wait("gpsimd", "st")
            return
        if DBG_STAGE == "ATT0":
            attention(E, only=1)
            E.wait("gpsimd", "dv")
            E.wait("gpsimd", "ac")
            E.dma("gpsimd", "st", lambda g: g.dma_start(out=ofdbg, in_=o_fm[:]))
            E.dma("gpsimd", "st", lambda g: g.dma_start(out=ptdbg[:, 0:1024], in_=qbuf[0][:]))
            E.dma("gpsimd", "st", lambda g: g.dma_start(out=ptrdbg, in_=ptr_sb[:]))
            E.dma("gpsimd", "st", lambda g: g.dma_start(
                out=dendbg[:, 0:2], in_=negB[:]))
            E.dma("gpsimd", "st", lambda g: g.dma_start(
                out=ptdbg[:, 1024:1032], in_=dbg8[:]))
            E.dma("gpsimd", "st", lambda g: g.dma_start(
                out=dendbg[:, 2:3], in_=invd_att[:], allow_slow_non_contiguous=True))
            E.dma("gpsimd", "st", lambda g: g.dma_start(
                out=dendbg[0:1, 3:4].broadcast_to((1, 1)), in_=km2[:, 15:16],
                allow_slow_non_contiguous=True))
            E.dma("gpsimd", "st", lambda g: g.dma_start(
                out=out[0:128, :], in_=x_tm[:, 0:D]))
            E.wait("gpsimd", "st")
            return
        attention(E)
        if DBG_STAGE == "ATT":
            dbg_store(E)
            E.wait("sync", "st", 16 * NB)
            return
        ffn(E)
        E.wait("sync", "st", E.c["st"])

    def dbg_store(E):
        E.wait("gpsimd", "dv")
        E.wait("gpsimd", "ac")
        E.wait("gpsimd", "pe")
        for b in range(NB):
            E.dma("gpsimd", "st", lambda g, b=b: g.dma_start(
                out=out[b * 128:(b + 1) * 128, :], in_=x_tm[:, b * D:(b + 1) * D]))

    def attention(E, only=NB):
        E.wait("sync", "qs", R["phaseA_qs"])
        E.wait("scalar", "ac", R["w_done_ac"])

        def p_half(h, g2):
            if h % 2 == 0:
                return p_tm[:, g2 * 1024:(g2 + 1) * 1024]
            return p_tmB[g2]

        def emit_S(i, qb, h):
            ft, po, hp = h // 2, (h % 2) * 64, h % 2
            for q4 in range(4):
                if q4 < 2:
                    E.wait("tensor", "ac")
                E.op("tensor", "pe", lambda t, q4=q4, ft=ft, po=po, qb=qb: t.matmul(
                    P_S[:, q4 * 512:(q4 + 1) * 512],
                    lhsT=qb[po:po + 64, ft * 128:(ft + 1) * 128],
                    rhs=k_fm[po:po + 64,
                             ft * T + q4 * 512:ft * T + (q4 + 1) * 512],
                    start=True, stop=True))
                E.wait("vector", "pe")
                E.op("vector", "dv", lambda v, q4=q4: v.reduce_max(
                    out=tbm[:, q4:q4 + 1], in_=P_S[:, q4 * 512:(q4 + 1) * 512],
                    axis=AXX))
            E.wait("vector", "dv")
            E.op("vector", None, lambda v: v.tensor_tensor(
                out=tbm[:, 4:5], in0=tbm[:, 0:1], in1=tbm[:, 1:2], op=ALU.max))
            E.op("vector", "dv", lambda v: v.tensor_tensor(
                out=tbm[:, 5:6], in0=tbm[:, 2:3], in1=tbm[:, 3:4], op=ALU.max))
            E.wait("vector", "dv")
            E.op("vector", "dv", lambda v, hp=hp: v.scalar_tensor_tensor(
                out=negB[:, hp:hp + 1], in0=tbm[:, 4:5], scalar=-1.0,
                in1=tbm[:, 5:6], op0=ALU.bypass, op1=ALU.max))
            E.wait("vector", "dv")
            E.op("vector", "dv", lambda v, hp=hp: v.tensor_single_scalar(
                out=negB[:, hp:hp + 1], in_=negB[:, hp:hp + 1], scalar=-1.0,
                op=ALU.mult))
            for g2 in range(2):
                E.wait("scalar", "pe")
                E.wait("scalar", "dv")
                E.op("scalar", "ac", lambda s, g2=g2, hp=hp, h=h: s.activation(
                    out=p_half(h, g2),
                    in_=P_S[:, g2 * 1024:(g2 + 1) * 1024], func=AF.Exp,
                    bias=negB[:, hp:hp + 1], scale=1.0,
                    accum_out=tbm[:, 6 + 2 * hp + g2:7 + 2 * hp + g2]))
                R[("exph", h, g2)] = E.c["ac"]

        def emit_P(i, h):
            ft, po, hp = h // 2, (h % 2) * 64, h % 2
            ptr = ptr_sb if hp == 0 else ptr2
            for g2 in range(2):
                E.wait("tensor", "ac", R[("exph", h, g2)])
                if R.get(("ofm_ac", h - 1)):
                    E.wait("tensor", "ac", R[("ofm_ac", h - 1)])
                E.wait("tensor", "dv")
                for j in range(8):
                    E.op("tensor", "pe" if j == 7 else None,
                         lambda t, j=j, h=h, g2=g2: t.transpose(
                             P_T[:, j * 128:(j + 1) * 128],
                             in_=p_half(h, g2)[:, j * 128:(j + 1) * 128],
                             identity=ident_b[:]))
                E.wait("vector", "pe")
                E.op("vector", "dv", lambda v, g2=g2, ptr=ptr: v.tensor_copy(
                    ptr[:, g2 * 1024:(g2 + 1) * 1024], P_T[:, :]))
            E.wait("vector", "ac", R[("exph", h, 1)])
            E.op("vector", "dv", lambda v, hp=hp: v.tensor_tensor(
                out=tbm[:, 10:11], in0=tbm[:, 6 + 2 * hp:7 + 2 * hp],
                in1=tbm[:, 7 + 2 * hp:8 + 2 * hp], op=ALU.add))
            E.wait("vector", "dv")
            E.op("vector", "dv", lambda v, hp=hp: v.reciprocal(
                invd_att[:, hp:hp + 1], tbm[:, 10:11]))
            E.wait("tensor", "dv")
            for kb in range(16):
                E.op("tensor", "pe" if kb == 15 else None,
                     lambda t, kb=kb, h=h, ptr=ptr: t.matmul(
                         P_M[:, 256:320],
                         lhsT=ptr[:, kb * 128:(kb + 1) * 128],
                         rhs=v_aug[:, (kb * NH + h) * 65:(kb * NH + h) * 65 + 64],
                         start=(kb == 0), stop=(kb == 15)))
            E.wait("vector", "pe")
            E.op("vector", "dv", lambda v, po=po, hp=hp: v.tensor_scalar_mul(
                o_pair[:, po:po + 64], P_M[:, 256:320], invd_att[:, hp:hp + 1]))
            if hp == 1:
                E.wait("tensor", "dv")
                E.op("tensor", "pe", lambda t: t.transpose(
                    P_T[:, 0:128], in_=o_pair[:], identity=ident_b[:]))
                E.wait("scalar", "pe")
                E.op("scalar", "ac", lambda s, ft=ft: s.copy(
                    out=o_fm[:, ft * 128:(ft + 1) * 128],
                    in_=P_T[:, 0:128]))
                R[("ofm_ac", h)] = E.c["ac"]
            else:
                R[("ofm_ac", h)] = R.get(("ofm_ac", h - 1), 0)

        for i in range(only):
            qb = qbuf[i % 2]
            if i >= 2:
                E.wait("sync", "pe", R.get(("qfree", i - 2)))
            E.dma("sync", "ql", lambda h, i=i, qb=qb: h.dma_start(
                out=qb[:], in_=q_sp[i]))
            E.wait("tensor", "dv")
            nhl = int(_os.environ.get("K_NHLIM", str(NH)))
            for h in range(nhl + 1):
                if h < nhl:
                    emit_S(i, qb, h)
                if h >= 1:
                    emit_P(i, h - 1)
            R[("qfree", i)] = E.c["pe"]
            E.wait("tensor", "ac")
            for half in range(2):
                for c in range(8):
                    E.op("tensor", "pe" if (c == 7 and half == 1) else None,
                         lambda t, c=c, half=half: t.matmul(
                             P_W[:, half * 512:(half + 1) * 512],
                             lhsT=o_fm[:, c * 128:(c + 1) * 128],
                             rhs=woT[c][:, half * 512:(half + 1) * 512],
                             start=(c == 0), stop=(c == 7)))
            E.wait("vector", "pe")
            E.op("vector", "dv", lambda v, i=i: v.tensor_tensor(
                out=x_tm[:, i * D:(i + 1) * D], in0=P_W[:, :],
                in1=x_tm[:, i * D:(i + 1) * D], op=ALU.add))
            R[("attnadd", i)] = E.c["dv"]
            E.wait("tensor", "dv")
        R["attn_dv"] = E.c["dv"]

    def ln2_ffn_norm(E, b):
        xs = x_tm[:, b * D:(b + 1) * D]
        E.op("vector", None, lambda v: v.tensor_reduce(
            out=sx[:], in_=xs, axis=AXX, op=ALU.add))
        E.op("vector", "dv", lambda v: v.scalar_tensor_tensor(
            out=t1b[:], in0=xs, scalar=1.0, in1=xs,
            op0=ALU.mult, op1=ALU.mult, accum_out=sxx[:]))
        E.wait("vector", "dv")
        E.op("vector", "dv", lambda v: v.tensor_single_scalar(
            out=mln[:], in_=sx[:], scalar=1.0 / D, op=ALU.mult))
        E.wait("vector", "dv")
        E.op("vector", "dv", lambda v: v.tensor_tensor(
            out=m2ln[:], in0=mln[:], in1=mln[:], op=ALU.mult))
        E.wait("vector", "dv")
        E.op("vector", "dv", lambda v: v.scalar_tensor_tensor(
            out=varln[:], in0=sxx[:], scalar=1.0 / D, in1=m2ln[:],
            op0=ALU.mult, op1=ALU.subtract))
        E.wait("scalar", "dv")
        E.op("scalar", "ac", lambda s: s.activation(
            out=stdln[:], in_=varln[:], func=AF.Sqrt, bias=epsB[:], scale=1.000001))
        E.wait("vector", "ac")
        E.op("vector", "dv", lambda v: v.reciprocal(istd[:], stdln[:]))
        E.wait("vector", "dv")
        E.op("vector", "dv", lambda v: v.scalar_tensor_tensor(
            out=t1b[:], in0=xs, scalar=mln[:], in1=normw_b[:],
            op0=ALU.subtract, op1=ALU.mult))
        E.wait("vector", "dv")
        E.op("vector", "dv", lambda v: v.tensor_scalar_mul(t1b[:], t1b[:], istd[:]))

    def routing_block(E, b, slot):
        xslot = b % NXF
        if b >= NXF:
            E.wait("vector", "pe", R.get(("pairs_done", b - NXF)))
        E.wait("tensor", "dv")
        E.wait("tensor", "ac")
        if b >= NXF:
            E.wait("tensor", "pe", R.get(("pairs_done", b - NXF)))
        for c in range(8):
            E.op("tensor", "pe" if c == 7 else None, lambda t, c=c: t.transpose(
                P_T[:, c * 128:(c + 1) * 128], in_=t1b[:, c * 128:(c + 1) * 128],
                identity=ident_b[:]))
        E.wait("scalar", "pe")
        for c in range(8):
            E.op("scalar", "ac", lambda s, c=c, xslot=xslot: s.copy(
                out=xnffm[xslot][:, c * 128:(c + 1) * 128],
                in_=P_T[:, c * 128:(c + 1) * 128]))

        fia, coa = fi_all[slot], co_all[slot]
        if b >= NSLOT:
            E.wait("vector", "g", R.get(("gath_done", b - NSLOT)))
        for h in range(FH):
            for sub in range(2):
                ft = 2 * h + sub
                E.wait("tensor", "ac")
                for c in range(8):
                    E.op("tensor", "pe" if c == 7 else None,
                         lambda t, ft=ft, c=c, xslot=xslot: t.matmul(
                             P_S[:, 0:128],
                             lhsT=wqT[c][:, ft * 128:(ft + 1) * 128],
                             rhs=xnffm[xslot][:, c * 128:(c + 1) * 128],
                             start=(c == 0), stop=(c == 7)))
                E.wait("scalar", "pe")
                E.op("scalar", "ac", lambda s, sub=sub: s.copy(
                    out=qffn[:, sub * 128:(sub + 1) * 128], in_=P_S[:, 0:128]))
            E.wait("tensor", "ac")
            E.wait("tensor", "dv")
            for half in range(2):
                j = 2 * h + half
                E.op("tensor", "pe" if half == 1 else None,
                     lambda t, half=half, j=j: t.matmul(
                         P_W[:, half * 256:(half + 1) * 256],
                         lhsT=qffn[:, half * 128:(half + 1) * 128],
                         rhs=keysT[:, j * 256:(j + 1) * 256],
                         start=True, stop=True))
            E.wait("vector", "pe")
            E.op("vector", None, lambda v: v.max(out=v8a[:], in_=P_W[:, 0:256]))
            E.op("vector", "dv", lambda v: v.max(out=v8b[:], in_=P_W[:, 256:512]))
            E.wait("vector", "dv")
            for k in range(8):
                E.op("vector", None, lambda v, k=k: v.scalar_tensor_tensor(
                    out=scr256[:], in0=P_W[:, 0:256], scalar=v8a[:, k:k + 1],
                    in1=iota256[:], op0=ALU.is_equal, op1=ALU.mult,
                    accum_out=i1x[:, k:k + 1]))
                E.op("vector", "dv" if k == 7 else None,
                     lambda v, k=k: v.scalar_tensor_tensor(
                         out=scr256[:], in0=P_W[:, 256:512], scalar=v8b[:, k:k + 1],
                         in1=iota256[:], op0=ALU.is_equal, op1=ALU.mult,
                         accum_out=i2x[:, k:k + 1]))
            E.op("vector", None, lambda v: v.tensor_tensor(
                out=grid[:].rearrange("p (a b) -> p a b", a=8),
                in0=v8a[:].unsqueeze(2).broadcast_to((128, 8, 8)),
                in1=v8b[:].unsqueeze(1).broadcast_to((128, 8, 8)),
                op=ALU.add))
            E.op("vector", "dv", lambda v: v.max(out=fs8[:], in_=grid[:]))
            E.wait("vector", "dv")
            for k in range(8):
                E.op("vector", "dv" if k == 7 else None,
                     lambda v, k=k: v.scalar_tensor_tensor(
                         out=scr256[:, 0:64], in0=grid[:], scalar=fs8[:, k:k + 1],
                         in1=iota256[:, 0:64], op0=ALU.is_equal, op1=ALU.mult,
                         accum_out=pk8[:, k:k + 1]))
            E.wait("vector", "dv")
            E.op("vector", "dv", lambda v: v.tensor_copy(pki[:], pk8[:]))
            E.wait("vector", "dv")
            E.op("vector", None, lambda v: v.tensor_single_scalar(
                out=b8i[:], in_=pki[:], scalar=7, op=ALU.bitwise_and))
            E.op("vector", "dv", lambda v: v.tensor_single_scalar(
                out=a8i[:], in_=pki[:], scalar=3, op=ALU.logical_shift_right))
            E.wait("vector", "dv")
            E.op("vector", None, lambda v: v.tensor_copy(b8[:], b8i[:]))
            E.op("vector", None, lambda v: v.tensor_copy(a8[:], a8i[:]))
            E.op("vector", None, lambda v: v.memset(i1s[:], 0.0))
            E.op("vector", "dv", lambda v: v.memset(i2s[:], 0.0))
            E.wait("vector", "dv")
            for r in range(8):
                E.op("vector", None, lambda v, r=r: v.scalar_tensor_tensor(
                    out=tq1[:], in0=a8[:], scalar=float(r),
                    in1=i1x[:, r:r + 1].broadcast_to((128, 8)),
                    op0=ALU.is_equal, op1=ALU.mult))
                E.op("vector", None, lambda v, r=r: v.scalar_tensor_tensor(
                    out=tq2[:], in0=b8[:], scalar=float(r),
                    in1=i2x[:, r:r + 1].broadcast_to((128, 8)),
                    op0=ALU.is_equal, op1=ALU.mult))
                E.op("vector", None, lambda v: v.tensor_add(
                    out=i1s[:], in0=i1s[:], in1=tq1[:]))
                E.op("vector", "dv" if r == 7 else None, lambda v: v.tensor_add(
                    out=i2s[:], in0=i2s[:], in1=tq2[:]))
            E.wait("vector", "dv")
            E.op("vector", "dv", lambda v, h=h, fia=fia: v.scalar_tensor_tensor(
                out=fia[:, h * 8:(h + 1) * 8], in0=i1s[:], scalar=256.0, in1=i2s[:],
                op0=ALU.mult, op1=ALU.add))
            E.wait("vector", "dv")
            E.op("vector", None, lambda v, h=h, fia=fia: v.tensor_single_scalar(
                out=fia[:, h * 8:(h + 1) * 8], in_=fia[:, h * 8:(h + 1) * 8],
                scalar=65535.0, op=ALU.min))
            E.op("vector", "dv", lambda v, h=h, coa=coa: v.tensor_single_scalar(
                out=coa[:, h * 8:(h + 1) * 8], in_=fs8[:], scalar=0.0, op=ALU.max))

        E.wait("tensor", "dv")
        for (srct, tgt) in ((fia, gfi_f), (coa, gco[slot])):
            E.wait("tensor", "gt")
            E.op("tensor", "pe", lambda t, srct=srct: t.transpose(
                P_W[0:64, 512:640], in_=srct[:], identity=ident_f[:]))
            E.wait("vector", "pe")
            E.wait("vector", "gt")
            E.op("vector", None, lambda v, tgt=tgt: v.tensor_copy(
                tgt[0:64, :], P_W[0:64, 512:640][:, 0:128:2]))
            E.op("vector", "dv", lambda v: v.tensor_copy(
                gtmp[0:64, :], P_W[0:64, 512:640][:, 1:128:2]))
            E.wait("sync", "dv")
            E.dma("sync", "gt", lambda h, tgt=tgt: h.dma_start(
                out=tgt[64:128, :], in_=gtmp[0:64, :]))
            E.wait("tensor", "dv")
        E.wait("vector", "gt")
        E.op("vector", "dv", lambda v: v.tensor_copy(gfi[slot][:], gfi_f[:]))
        R[("route_done", b)] = E.c["dv"]
        R[("route_done_gt", b)] = E.c["gt"]

    pair_free = {}
    gath_val = {}

    def expert_block(E, b, slot):
        E.wait("vector", "gt", R[("route_done_gt", b)])
        E.op("vector", "dv", lambda v: v.memset(out_fm[:], 0.0))
        E.wait("tensor", "dv")
        for m in range(NPAIR):
            p = b * NPAIR + m
            r = p % NRING
            mg = m % 2
            gg = p // 2
            gb = gg % 2
            E.wait("gpsimd", "dv", R[("route_done", b)])
            E.wait("gpsimd", "gt", R[("route_done_gt", b)])
            if p >= NRING:
                E.wait("gpsimd", "pe", pair_free.get(p - NRING))
            E.dma("gpsimd", "g", lambda g, m=m, r=r, slot=slot: g.indirect_dma_start(
                out=ring_d[r], out_offset=None, in_=ap["edown"],
                in_offset=IndirectOffsetOnAxis(ap=gfi[slot][:, m:m + 1], axis=0)))
            E.dma("gpsimd", "g", lambda g, m=m, r=r, slot=slot: g.indirect_dma_start(
                out=ring_u[r], out_offset=None, in_=ap["eup"],
                in_offset=IndirectOffsetOnAxis(ap=gfi[slot][:, m:m + 1], axis=0)))
            gath_val[p] = E.c["g"]
            E.wait("tensor", "g", gath_val[p])
            E.wait("tensor", "dv")
            E.wait("tensor", "ac")
            for c in range(8):
                E.op("tensor", "pe" if c == 7 else None,
                     lambda t, c=c, r=r: t.transpose(
                         P_T[:, c * 128:(c + 1) * 128],
                         in_=ring_d[r][:, c * 128:(c + 1) * 128],
                         identity=ident_b[:]))
            if mg == 0:
                E.wait("scalar", "pe")
                E.op("scalar", "ac", lambda s, mg=mg: s.copy(
                    out=wdfm[mg][:], in_=P_T[:, :]))
                E.wait("tensor", "ac")
            else:
                E.wait("vector", "pe")
                E.op("vector", "dv", lambda v, mg=mg: v.tensor_copy(
                    wdfm[mg][:], P_T[:, :]))
                E.wait("tensor", "dv")
            dbase = 640 + gb * 4 + mg * 2
            if mg == 0 and gg >= 2:
                E.wait("tensor", "dv", R.get(("diag_free", gg - 2)))
            if mg == 0 and gg >= 1:
                E.wait("tensor", "dv", R.get(("upg_done", gg - 1)))
            for c in range(8):
                E.op("tensor", "pe" if c == 7 else None,
                     lambda t, c=c, mg=mg, m=m, xslot=b % NXF, dbase=dbase: t.matmul(
                         P_W[:, dbase:dbase + 2],
                         lhsT=wdfm[mg][:, c * 128:(c + 1) * 128],
                         rhs=xnffm[xslot][:, c * 128 + 2 * m:c * 128 + 2 * m + 2],
                         start=(c == 0), stop=(c == 7)))
            if mg == 1:
                base = 640 + gb * 4
                E.wait("vector", "pe")
                E.op("vector", None, lambda v, base=base, gb=gb: v.tensor_copy(
                    h4[gb][0:64, 0:2],
                    P_W[0:64, base:base + 4].rearrange("p (a b) -> p a b", a=2)[:, :, 0]))
                E.op("vector", "dv", lambda v, base=base, gb=gb: v.tensor_copy(
                    h4[gb][64:128, 0:2],
                    P_W[64:128, base:base + 4].rearrange("p (a b) -> p a b", a=2)[:, :, 1]))
                R[("diag_free", gg)] = E.c["dv"]
                E.wait("scalar", "dv")
                E.op("scalar", "ac", lambda s, gb=gb: s.activation(
                    out=h4[gb][:, 0:2], in_=h4[gb][:, 0:2], func=AF.Gelu))
                E.wait("vector", "ac")
                E.op("vector", "dv", lambda v, gb=gb, m=m, slot=slot: v.tensor_tensor(
                    out=hid4[gb][:, 0:2], in0=h4[gb][:, 0:2],
                    in1=gco[slot][:, m - 1:m + 1], op=ALU.mult))
                E.wait("vector", "dv")
                E.op("vector", None, lambda v, gb=gb: v.memset(chid[gb][:, 0:4], 0.0))
                E.op("vector", None, lambda v, gb=gb: v.tensor_copy(
                    chid[gb][0:64, 0:4].rearrange("p (a b) -> p a b", a=2)[:, :, 0],
                    hid4[gb][0:64, 0:2]))
                E.op("vector", "dv", lambda v, gb=gb: v.tensor_copy(
                    chid[gb][64:128, 0:4].rearrange("p (a b) -> p a b", a=2)[:, :, 1],
                    hid4[gb][64:128, 0:2]))
                E.wait("tensor", "dv")
                for mm in range(m - 1, m + 1):
                    pp = b * NPAIR + mm
                    rr = pp % NRING
                    upsum = P_W[:, 656:672] if mm % 2 == 0 else P_M[:, 256:272]
                    if pp >= 2:
                        E.wait("tensor", "dv", R.get(("up_evac", pp - 2)))
                    for c in range(8):
                        E.op("tensor", "pe" if c == 7 else None,
                             lambda t, c=c, rr=rr, mm=mm, gb=gb, upsum=upsum: t.matmul(
                                 upsum[:, c * 2:c * 2 + 2],
                                 lhsT=ring_u[rr][:, c * 128:(c + 1) * 128],
                                 rhs=chid[gb][:, (mm % 2) * 2:(mm % 2) * 2 + 2],
                                 start=True, stop=True))
                    pair_free[pp] = E.c["pe"]
                    E.wait("vector", "pe")
                    E.op("vector", "dv", lambda v, mm=mm, upsum=upsum: v.tensor_tensor(
                        out=out_fm[:].rearrange("p (c t) -> p c t", c=8)
                            [:, :, 2 * mm:2 * mm + 2],
                        in0=upsum.rearrange("p (c t) -> p c t", t=2),
                        in1=out_fm[:].rearrange("p (c t) -> p c t", c=8)
                            [:, :, 2 * mm:2 * mm + 2],
                        op=ALU.add))
                    R[("up_evac", pp)] = E.c["dv"]
                    if mm % 2 == 1:
                        R[("upg_done", gg)] = E.c["dv"]
        R[("pairs_done", b)] = E.c["pe"]
        R[("gath_done", b)] = E.c["g"]
        E.wait("tensor", "dv", R[("up_evac", b * NPAIR + NPAIR - 1)])
        for c in range(8):
            E.op("tensor", "pe" if c == 7 else None, lambda t, c=c: t.transpose(
                P_S[:, 1024 + c * 128:1024 + (c + 1) * 128],
                in_=out_fm[:, c * 128:(c + 1) * 128], identity=ident_f[:]))
        E.wait("vector", "pe")
        E.op("vector", "dv", lambda v, b=b: v.tensor_tensor(
            out=x_tm[:, b * D:(b + 1) * D], in0=P_S[:, 1024:2048],
            in1=x_tm[:, b * D:(b + 1) * D], op=ALU.add))
        E.wait("gpsimd", "dv")
        E.dma("gpsimd", "st", lambda g, b=b: g.dma_start(
            out=out[b * 128:(b + 1) * 128, :], in_=x_tm[:, b * D:(b + 1) * D]))

    def ffn(E):
        nbf = int(_os.environ.get("K_NBFFN", str(NB)))
        if FFN_AHEAD:
            for b in range(nbf):
                E.wait("vector", "dv", R[("attnadd", b)])
                ln2_ffn_norm(E, b)
                routing_block(E, b, b % NSLOT)
                if b >= 1:
                    expert_block(E, b - 1, (b - 1) % NSLOT)
            expert_block(E, nbf - 1, (nbf - 1) % NSLOT)
        else:
            for b in range(NB):
                E.wait("vector", "dv", R[("attnadd", b)])
                ln2_ffn_norm(E, b)
                routing_block(E, b, b % NSLOT)
                expert_block(E, b, b % NSLOT)

    E.c = dict(snapA)
    phase_b(E)

    with nc.Block() as blockB:
        for eng in ENGS:
            def _sect(handle, eng=eng):
                E.c = dict(snapA)
                E.active = (eng, handle)
                phase_b(E)
                E.active = None
            getattr(blockB, eng)(_sect)

    ctx_b.close()
    ctx_p.close()
    return nc


def kernel(L0_attn_in_features, L0_attn_W_qkv, L0_attn_W_o, L0_ffn_w_q,
           L0_ffn_keys, L0_ffn_expert_down, L0_ffn_expert_up, L0_ffn_norm_w):
    x = np.ascontiguousarray(L0_attn_in_features)
    nc = bass.Bass("TRN2", target_bir_lowering=False)
    build_program(nc)
    shared = {
        "wqkv": np.ascontiguousarray(L0_attn_W_qkv),
        "wo": np.ascontiguousarray(L0_attn_W_o),
        "wq": np.ascontiguousarray(L0_ffn_w_q),
        "keys": np.ascontiguousarray(L0_ffn_keys),
        "edown": np.ascontiguousarray(L0_ffn_expert_down),
        "eup": np.ascontiguousarray(L0_ffn_expert_up),
        "normw": np.ascontiguousarray(L0_ffn_norm_w),
    }
    in_maps = [dict(shared, x=np.ascontiguousarray(x[b])) for b in range(8)]
    res = run_bass_kernel_spmd(nc, in_maps, core_ids=list(range(8)))
    return np.stack([np.asarray(r["out"]) for r in res.results], axis=0).astype(np.float32)


# version 08
# speedup vs baseline: 1.0964x; 1.0072x over previous
import numpy as np

import concourse.bass as bass
import concourse.mybir as mybir
from concourse.bass_utils import run_bass_kernel_spmd
from concourse.bass_core_rust import IndirectOffsetOnAxis

F32 = mybir.dt.float32
BF16 = mybir.dt.bfloat16
I32 = mybir.dt.int32
AF = mybir.ActivationFunctionType
ALU = mybir.AluOpType
AXX = mybir.AxisListType.X

DBG_STAGE = None
FFN_AHEAD = True

T = 2048
D = 1024
NB = T // 128
NH = 16
FH = 8
NKEY = 256
NPAIR = 64
SCALE = 0.125
import os as _os
NRING = int(_os.environ.get("K_NRING", "6"))
NSLOT = 2
NXF = 2

ENGS = ("sync", "tensor", "vector", "scalar", "gpsimd")


class Emit:

    RECORD = False

    def __init__(self, nc, sems):
        self.nc = nc
        self.sems = sems
        self.active = None
        self.c = {k: 0 for k in sems}
        self.streams = {}

    def _line(self):
        import traceback
        out = []
        for fr in reversed(traceback.extract_stack()):
            if "kernel" in fr.filename.rsplit("/", 1)[-1]:
                out.append(f"{fr.name}:{fr.lineno}")
                if len(out) == 2:
                    break
        return ",".join(out)

    def op(self, eng, sem, fn):
        if sem is not None:
            self.c[sem] += 1
        if self.active is not None and self.active[0] == eng:
            if Emit.RECORD:
                self.streams.setdefault(eng, []).append(("i", sem, 1, self._line()))
                return
            ins = fn(self.active[1])
            if sem is not None:
                ins.then_inc(self.sems[sem], 1)

    def dma(self, eng, sem, fn):
        if sem is not None:
            self.c[sem] += 16
        if self.active is not None and self.active[0] == eng:
            if Emit.RECORD:
                self.streams.setdefault(eng, []).append(("i", sem, 16, self._line()))
                return
            ins = fn(self.active[1])
            if sem is not None:
                ins.then_inc(self.sems[sem], 16)

    def wait(self, eng, sem, val=None):
        if self.active is not None and self.active[0] == eng:
            tgt = self.c[sem] if val is None else val
            if tgt > 0:
                if Emit.RECORD:
                    self.streams.setdefault(eng, []).append(("w", sem, tgt, self._line()))
                    return
                self.active[1].wait_ge(self.sems[sem], tgt)

    def snap(self):
        return dict(self.c)

    _INSTANCES = []


def build_program(nc):
    ap = {}
    for nm, sh, dt in [
        ("x", [T, D], F32), ("wqkv", [3 * D, D], F32), ("wo", [D, D], F32),
        ("wq", [2 * D, D], F32), ("keys", [FH, NKEY, 2, 128], F32),
        ("edown", [65536, D], F32), ("eup", [65536, D], F32),
        ("normw", [D], F32),
    ]:
        ap[nm] = nc.dram_tensor(nm, sh, dt, kind="ExternalInput").ap()
    out = nc.dram_tensor("out", [T, D], F32, kind="ExternalOutput").ap()
    if DBG_STAGE == "QKV":
        kdbg = nc.dram_tensor("kdbg", [128, 8 * T], BF16, kind="ExternalOutput").ap()
        vdbg = nc.dram_tensor("vdbg", [128, NB * D], BF16, kind="ExternalOutput").ap()
        qdbg = nc.dram_tensor("qdbg", [NB, 128, 8 * 128], BF16, kind="ExternalOutput").ap()
    if DBG_STAGE == "ATT0":
        ofdbg = nc.dram_tensor("ofdbg", [128, 8 * 128], BF16, kind="ExternalOutput").ap()
        ptdbg = nc.dram_tensor("ptdbg", [128, T], BF16, kind="ExternalOutput").ap()
        ptrdbg = nc.dram_tensor("ptrdbg", [128, T], BF16, kind="ExternalOutput").ap()
        dendbg = nc.dram_tensor("dendbg", [128, 4], F32, kind="ExternalOutput").ap()
    q_sp = nc.dram_tensor("q_sp", [NB, 128, 8 * 128], BF16, kind="Internal").ap()

    from contextlib import ExitStack
    ctx_p = ExitStack()
    sem_names = ["w", "xl", "qs", "ql", "g", "pe", "dv", "ac", "st", "gp", "gt"]
    sems = {nm: ctx_p.enter_context(nc.semaphore("s_" + nm)) for nm in sem_names}
    E = Emit(nc, sems)
    Emit._INSTANCES.append(E)

    def sbp(name, shape, dt=F32):
        return ctx_p.enter_context(nc.sbuf_tensor(name, shape, dt))

    x_tm = sbp("x_tm", [128, NB * D], BF16)
    k_fm = sbp("k_fm", [128, 8 * T], BF16)
    v_aug = sbp("v_aug", [128, NB * NH * 65], BF16)
    ones_c = sbp("ones_c", [128, 1], BF16)
    ones_r = sbp("ones_r", [1, 128], BF16)
    km2 = sbp("km2", [1, 16], F32)
    tbm = sbp("tbm", [128, 12], F32)
    rowm = sbp("rowm", [128, 1], BF16)
    tb1 = sbp("tb1", [1, 4], F32)
    tb2 = sbp("tb2", [1, 1], F32)
    tb3 = sbp("tb3", [1, 1], BF16)
    negB = sbp("negB", [128, 2], F32)
    dbg8 = sbp("dbg8", [128, 8], F32)
    qbuf = [sbp(f"qbuf{i}", [128, 8 * 128], BF16) for i in range(2)]
    ident_b = sbp("ident_b", [128, 128], BF16)
    ident_f = sbp("ident_f", [128, 128], F32)
    iota256 = sbp("iota256", [128, NKEY], F32)
    epsA = sbp("epsA", [128, 1], F32)
    epsB = sbp("epsB", [128, 1], F32)
    t1b = sbp("t1b", [128, D], BF16)
    sx = sbp("sx", [128, 1], F32)
    sxx = sbp("sxx", [128, 1], F32)
    mln = sbp("mln", [128, 1], F32)
    m2ln = sbp("m2ln", [128, 1], F32)
    varln = sbp("varln", [128, 1], F32)
    stdln = sbp("stdln", [128, 1], F32)
    istd = sbp("istd", [128, 1], F32)
    scr256 = sbp("scr256", [128, NKEY], F32)

    P_S = ctx_p.enter_context(nc.psum_tensor("P_S", [128, 2048], F32))
    P_T = ctx_p.enter_context(nc.psum_tensor("P_T", [128, 1024], BF16))
    P_W = ctx_p.enter_context(nc.psum_tensor("P_W", [128, 1024], F32))
    P_M = ctx_p.enter_context(nc.psum_tensor("P_M", [128, 512], F32))

    ctx_a = ExitStack()
    xn_fm = ctx_a.enter_context(nc.sbuf_tensor("xn_fm", [128, 8 * T], BF16))
    wT = [ctx_a.enter_context(nc.sbuf_tensor(f"wT{c}", [128, D], BF16))
          for c in range(8)]
    wstg = [ctx_a.enter_context(nc.sbuf_tensor(f"wstg{i}", [128, D], F32))
            for i in range(2)]
    xf32 = [ctx_a.enter_context(nc.sbuf_tensor(f"xf32{i}", [128, D], F32))
            for i in range(2)]
    qsp4 = ctx_a.enter_context(nc.sbuf_tensor("qsp4", [128, 4096], BF16))

    R = {}
    lt_seq = [0]
    qk_seq = [0]

    def load_transpose(E, src, nft, dst, stage):
        for ft in range(nft):
            g = lt_seq[0]
            lt_seq[0] += 1
            buf = stage[g % 2]
            if g >= 2:
                E.wait("sync", "pe", R.get(("lt", g - 2)))
            E.dma("sync", "w", lambda h, ft=ft, buf=buf: h.dma_start(
                out=buf[:], in_=src[ft * 128:(ft + 1) * 128, :]))
            E.wait("tensor", "w")
            E.wait("tensor", "ac")
            for c in range(8):
                E.op("tensor", "pe" if c == 7 else None,
                     lambda t, c=c, buf=buf: t.transpose(
                         P_S[:, c * 128:(c + 1) * 128],
                         in_=buf[:, c * 128:(c + 1) * 128], identity=ident_f[:]))
            R[("lt", g)] = E.c["pe"]
            E.wait("scalar", "pe")
            for c in range(8):
                E.op("scalar", "ac", lambda s, c=c, ft=ft: s.copy(
                    out=dst(c, ft), in_=P_S[:, c * 128:(c + 1) * 128]))

    def phase_a(E):
        lt_seq[0] = 0
        qk_seq[0] = 0
        E.op("gpsimd", None, lambda g: g.memset(epsA[:], 1e-5))
        E.op("gpsimd", None, lambda g: g.memset(ones_c[:], 1.0))
        E.op("gpsimd", None, lambda g: g.memset(ones_r[:], 1.0))
        E.op("gpsimd", "gp", lambda g: g.memset(v_aug[:], 1.0))
        E.op("gpsimd", None, lambda g: g.memset(epsB[:], 1e-11))
        E.op("gpsimd", None, lambda g: g.iota(
            iota256[:], pattern=[[1, NKEY]], base=0, channel_multiplier=0,
            allow_small_or_imprecise_dtypes=True))
        E.op("gpsimd", None, lambda g: g.memset(ident_b[:], 0.0))
        E.op("gpsimd", "gp", lambda g: g.affine_select(
            out=ident_b[:], in_=ident_b[:], compare_op=ALU.not_equal,
            fill=1.0, base=0, pattern=[[-1, 128]], channel_multiplier=1))
        E.op("gpsimd", None, lambda g: g.memset(ident_f[:], 0.0))
        E.op("gpsimd", "gp", lambda g: g.affine_select(
            out=ident_f[:], in_=ident_f[:], compare_op=ALU.not_equal,
            fill=1.0, base=0, pattern=[[-1, 128]], channel_multiplier=1))
        E.wait("tensor", "gp")
        E.wait("vector", "gp")

        for b in range(NB):
            xb = xf32[b % 2]
            if b >= 2:
                E.wait("sync", "dv", R.get(("xfree", b - 2)))
            E.dma("sync", "xl", lambda h, b=b, xb=xb: h.dma_start(
                out=xb[:], in_=ap["x"][b * 128:(b + 1) * 128, :]))
            E.wait("vector", "xl")
            E.op("vector", None, lambda v, xb=xb: v.tensor_reduce(
                out=sx[:], in_=xb[:], axis=AXX, op=ALU.add))
            E.op("vector", "dv", lambda v, xb=xb: v.scalar_tensor_tensor(
                out=t1b[:], in0=xb[:], scalar=1.0, in1=xb[:],
                op0=ALU.mult, op1=ALU.mult, accum_out=sxx[:]))
            E.wait("vector", "dv")
            E.op("vector", "dv", lambda v: v.tensor_single_scalar(
                out=mln[:], in_=sx[:], scalar=1.0 / D, op=ALU.mult))
            E.wait("vector", "dv")
            E.op("vector", "dv", lambda v: v.tensor_tensor(
                out=m2ln[:], in0=mln[:], in1=mln[:], op=ALU.mult))
            E.wait("vector", "dv")
            E.op("vector", "dv", lambda v: v.scalar_tensor_tensor(
                out=varln[:], in0=sxx[:], scalar=1.0 / D, in1=m2ln[:],
                op0=ALU.mult, op1=ALU.subtract))
            E.wait("scalar", "dv")
            E.op("scalar", "ac", lambda s: s.activation(
                out=stdln[:], in_=varln[:], func=AF.Sqrt, bias=epsA[:], scale=1.0))
            E.wait("vector", "ac")
            E.op("vector", "dv", lambda v: v.reciprocal(istd[:], stdln[:]))
            E.wait("vector", "dv")
            E.op("vector", "dv", lambda v, xb=xb: v.scalar_tensor_tensor(
                out=t1b[:], in0=xb[:], scalar=mln[:],
                in1=istd[:].broadcast_to((128, D)),
                op0=ALU.subtract, op1=ALU.mult))
            E.op("vector", "dv", lambda v, b=b, xb=xb: v.tensor_copy(
                x_tm[:, b * D:(b + 1) * D], xb[:]))
            R[("xfree", b)] = E.c["dv"]
            E.wait("tensor", "dv")
            E.wait("tensor", "ac")
            for c in range(8):
                E.op("tensor", "pe" if c == 7 else None, lambda t, c=c: t.transpose(
                    P_T[:, c * 128:(c + 1) * 128], in_=t1b[:, c * 128:(c + 1) * 128],
                    identity=ident_b[:]))
            E.wait("scalar", "pe")
            for c in range(8):
                E.op("scalar", "ac", lambda s, c=c, b=b: s.copy(
                    out=xn_fm[:, c * T + b * 128:c * T + (b + 1) * 128],
                    in_=P_T[:, c * 128:(c + 1) * 128]))

        for part in range(3):
            load_transpose(E, ap["wqkv"][part * D:(part + 1) * D, :], 8,
                           lambda c, ft: wT[c][:, ft * 128:(ft + 1) * 128],
                           wstg)
            E.wait("tensor", "ac")
            for bq in range(NB // 4):
                if part < 2:
                    if part == 0 and bq >= 1:
                        E.wait("scalar", "qs", R.get(("spill", bq - 1)))
                    for ft in range(8):
                        qk = qk_seq[0]
                        qk_seq[0] += 1
                        ps = P_M[:, 0:512] if qk % 2 == 0 else P_W[:, 0:512]
                        if qk >= 2:
                            E.wait("tensor", "ac", R.get(("qkevac", qk - 2)))
                        for c in range(8):
                            E.op("tensor", "pe" if c == 7 else None,
                                 lambda t, ft=ft, c=c, bq=bq, ps=ps: t.matmul(
                                     ps,
                                     lhsT=wT[c][:, ft * 128:(ft + 1) * 128],
                                     rhs=xn_fm[:, c * T + bq * 512:c * T + (bq + 1) * 512],
                                     start=(c == 0), stop=(c == 7)))
                        E.wait("scalar", "pe")
                        if part == 0:
                            for sb4 in range(4):
                                E.op("scalar",
                                     "ac" if sb4 == 3 else None,
                                     lambda s, ft=ft, sb4=sb4, ps=ps: s.activation(
                                         out=qsp4[:, sb4 * 1024 + ft * 128:sb4 * 1024 + (ft + 1) * 128],
                                         in_=ps[:, sb4 * 128:(sb4 + 1) * 128],
                                         func=AF.Copy, scale=SCALE))
                            if ft == 7:
                                E.wait("sync", "ac")
                                for sb4 in range(4):
                                    E.dma("sync", "qs", lambda h, bq=bq, sb4=sb4: h.dma_start(
                                        out=q_sp[4 * bq + sb4],
                                        in_=qsp4[:, sb4 * 1024:(sb4 + 1) * 1024]))
                                R[("spill", bq)] = E.c["qs"]
                        else:
                            for sb4 in range(4):
                                b = bq * 4 + sb4
                                E.op("scalar", "ac" if sb4 == 3 else None,
                                     lambda s, ft=ft, b=b, sb4=sb4, ps=ps: s.copy(
                                         out=k_fm[:, ft * T + b * 128:ft * T + (b + 1) * 128],
                                         in_=ps[:, sb4 * 128:(sb4 + 1) * 128]))
                        R[("qkevac", qk)] = E.c["ac"]
                else:
                    for sb4 in range(4):
                        b = bq * 4 + sb4
                        E.wait("tensor", "ac")
                        for half in range(2):
                            for c in range(8):
                                E.op("tensor",
                                     "pe" if (c == 7 and half == 1) else None,
                                     lambda t, c=c, b=b, half=half: t.matmul(
                                         P_S[:, half * 512:(half + 1) * 512],
                                         lhsT=xn_fm[:, c * T + b * 128:c * T + (b + 1) * 128],
                                         rhs=wT[c][:, half * 512:(half + 1) * 512],
                                         start=(c == 0), stop=(c == 7)))
                        E.wait("scalar", "pe")
                        E.wait("scalar", "gp")
                        E.op("scalar", "ac", lambda s, b=b: s.copy(
                            out=v_aug[:, b * NH * 65:(b + 1) * NH * 65]
                                .rearrange("p (h u) -> p h u", h=NH)[:, :, 0:64],
                            in_=P_S[:, 0:1024].rearrange("p (h u) -> p h u", h=NH)))
        R["phaseA_pe"] = E.c["pe"]
        R["phaseA_ac"] = E.c["ac"]
        R["phaseA_dv"] = E.c["dv"]
        R["phaseA_qs"] = E.c["qs"]

    phase_a(E)
    snapA = E.snap()

    with nc.Block() as blockA:
        for eng in ENGS:
            def _sect(handle, eng=eng):
                E.c = {k: 0 for k in sems}
                E.active = (eng, handle)
                phase_a(E)
                E.active = None
            getattr(blockA, eng)(_sect)

    ctx_a.close()

    ctx_b = ExitStack()

    def sbb(name, shape, dt=F32):
        return ctx_b.enter_context(nc.sbuf_tensor(name, shape, dt))

    woT = [sbb(f"woT{c}", [128, D], BF16) for c in range(8)]
    wqT = [sbb(f"wqT{c}", [128, 2 * D], BF16) for c in range(8)]
    keysT = sbb("keysT", [128, 16 * NKEY], BF16)
    normw_b = sbb("normw_b", [128, D], BF16)
    p_tm = sbb("p_tm", [128, T], BF16)
    ptr_sb = sbb("ptr_sb", [128, T], BF16)
    ptr2 = sbb("ptr2", [128, T], BF16)
    wstg2 = [p_tm[:].bitcast(F32), ptr_sb[:].bitcast(F32)]
    o_fm = sbb("o_fm", [128, 8 * 128], BF16)
    o_pair = sbb("o_pair", [128, 128], BF16)
    gtmp = sbb("gtmp", [128, 64], F32)
    invd_att = sbb("invd_att", [128, 2], F32)
    xnffm = [sbb(f"xnffm{i}", [128, 8 * 128], BF16) for i in range(NXF)]
    qffn = sbb("qffn", [128, 2 * 128], BF16)
    v8a = sbb("v8a", [128, 8], F32)
    v8b = sbb("v8b", [128, 8], F32)
    i1x = sbb("i1x", [128, 8], F32)
    i2x = sbb("i2x", [128, 8], F32)
    grid = sbb("grid", [128, 64], F32)
    fs8 = sbb("fs8", [128, 8], F32)
    pk8 = sbb("pk8", [128, 8], F32)
    pki = sbb("pki", [128, 8], I32)
    a8i = sbb("a8i", [128, 8], I32)
    b8i = sbb("b8i", [128, 8], I32)
    a8 = sbb("a8", [128, 8], F32)
    b8 = sbb("b8", [128, 8], F32)
    i1s = sbb("i1s", [128, 8], F32)
    i2s = sbb("i2s", [128, 8], F32)
    tq1 = sbb("tq1", [128, 8], F32)
    tq2 = sbb("tq2", [128, 8], F32)
    fi_all = [sbb(f"fi_all{i}", [128, 64], F32) for i in range(NSLOT)]
    co_all = [sbb(f"co_all{i}", [128, 64], F32) for i in range(NSLOT)]
    gfi = [sbb(f"gfi{i}", [128, 64], I32) for i in range(NSLOT)]
    gco = [sbb(f"gco{i}", [128, 64], F32) for i in range(NSLOT)]
    gfi_f = sbb("gfi_f", [128, 64], F32)
    ring_d = [sbb(f"ring_d{i}", [128, D], BF16) for i in range(2)]
    ring_u = [sbb(f"ring_u{i}", [128, D], BF16) for i in range(2)]
    p_tmB = [t[:] for t in ring_d]
    ring_d = [t[:] for t in ring_d] + [p_tm[:, 0:D], p_tm[:, D:2 * D],
                                       qbuf[0][:], ptr2[:, 0:D]]
    ring_u = [t[:] for t in ring_u] + [ptr_sb[:, 0:D], ptr_sb[:, D:2 * D],
                                       qbuf[1][:], ptr2[:, D:2 * D]]
    ring_d = ring_d[:NRING]
    ring_u = ring_u[:NRING]
    wdfm = [sbb(f"wdfm{i}", [128, D], BF16) for i in range(2)]
    h4 = [sbb(f"h4_{i}", [128, 4], F32) for i in range(2)]
    hid4 = [sbb(f"hid4_{i}", [128, 4], F32) for i in range(2)]
    chid = [sbb(f"chid{i}", [128, 8], BF16) for i in range(2)]
    out_fm = sbb("out_fm", [128, 8 * 128], F32)

    def phase_b(E):
        lt_seq[0] = 24
        E.wait("sync", "pe", R["phaseA_pe"])
        E.wait("sync", "ac", R["phaseA_ac"])
        E.wait("sync", "dv", R["phaseA_dv"])
        load_transpose(E, ap["wo"], 8,
                       lambda c, ft: woT[c][:, ft * 128:(ft + 1) * 128], wstg2)
        load_transpose(E, ap["wq"], 16,
                       lambda c, ft: wqT[c][:, ft * 128:(ft + 1) * 128], wstg2)
        for h in range(FH):
            for half in range(2):
                j = 2 * h + half
                buf = wstg2[j % 2]
                if j >= 2:
                    E.wait("sync", "pe", R.get(("keytp", j - 2)))
                else:
                    E.wait("sync", "pe", R.get(("lt", lt_seq[0] - 2 + j)))
                E.dma("sync", "w", lambda s, h=h, half=half, buf=buf: s.dma_start(
                    out=buf[:, 0:256].rearrange("p (a b) -> p a b", a=2),
                    in_=ap["keys"][h, :, half, :].rearrange("(a p) d -> p a d", p=128)))
                E.wait("tensor", "w")
                E.wait("tensor", "ac")
                for c in range(2):
                    E.op("tensor", "pe" if c == 1 else None,
                         lambda t, c=c, buf=buf: t.transpose(
                             P_S[:, c * 128:(c + 1) * 128],
                             in_=buf[:, c * 128:(c + 1) * 128],
                             identity=ident_f[:]))
                R[("keytp", j)] = E.c["pe"]
                E.wait("scalar", "pe")
                for c in range(2):
                    E.op("scalar", "ac", lambda s, c=c, j=j: s.copy(
                        out=keysT[:, j * 256 + c * 128:j * 256 + (c + 1) * 128],
                        in_=P_S[:, c * 128:(c + 1) * 128]))
        E.dma("gpsimd", "g", lambda g: g.dma_start(
            out=normw_b[:], in_=ap["normw"].unsqueeze(0).broadcast_to((128, D))))
        R["w_done_ac"] = E.c["ac"]
        R["w_done_g"] = E.c["g"]

        if DBG_STAGE == "A":
            dbg_store(E)
            E.wait("sync", "st", 16 * NB)
            return
        if DBG_STAGE == "QKV":
            E.wait("gpsimd", "ac")
            E.wait("gpsimd", "qs")
            E.dma("gpsimd", "st", lambda g: g.dma_start(out=kdbg, in_=k_fm[:]))
            E.dma("gpsimd", "st", lambda g: g.dma_start(out=vdbg, in_=v_tm[:]))
            E.dma("gpsimd", "st", lambda g: g.dma_start(
                out=qdbg.rearrange("a b d -> (a b) d"),
                in_=q_sp.rearrange("a b d -> (a b) d")))
            E.dma("gpsimd", "st", lambda g: g.dma_start(
                out=out[0:128, :], in_=x_tm[:, 0:D]))
            E.wait("gpsimd", "st")
            return
        if DBG_STAGE == "ATT0":
            attention(E, only=1)
            E.wait("gpsimd", "dv")
            E.wait("gpsimd", "ac")
            E.dma("gpsimd", "st", lambda g: g.dma_start(out=ofdbg, in_=o_fm[:]))
            E.dma("gpsimd", "st", lambda g: g.dma_start(out=ptdbg[:, 0:1024], in_=qbuf[0][:]))
            E.dma("gpsimd", "st", lambda g: g.dma_start(out=ptrdbg, in_=ptr_sb[:]))
            E.dma("gpsimd", "st", lambda g: g.dma_start(
                out=dendbg[:, 0:2], in_=negB[:]))
            E.dma("gpsimd", "st", lambda g: g.dma_start(
                out=ptdbg[:, 1024:1032], in_=dbg8[:]))
            E.dma("gpsimd", "st", lambda g: g.dma_start(
                out=dendbg[:, 2:3], in_=invd_att[:], allow_slow_non_contiguous=True))
            E.dma("gpsimd", "st", lambda g: g.dma_start(
                out=dendbg[0:1, 3:4].broadcast_to((1, 1)), in_=km2[:, 15:16],
                allow_slow_non_contiguous=True))
            E.dma("gpsimd", "st", lambda g: g.dma_start(
                out=out[0:128, :], in_=x_tm[:, 0:D]))
            E.wait("gpsimd", "st")
            return
        attention(E)
        if DBG_STAGE == "ATT":
            dbg_store(E)
            E.wait("sync", "st", 16 * NB)
            return
        ffn(E)
        E.wait("sync", "st", E.c["st"])

    def dbg_store(E):
        E.wait("gpsimd", "dv")
        E.wait("gpsimd", "ac")
        E.wait("gpsimd", "pe")
        for b in range(NB):
            E.dma("gpsimd", "st", lambda g, b=b: g.dma_start(
                out=out[b * 128:(b + 1) * 128, :], in_=x_tm[:, b * D:(b + 1) * D]))

    def attention(E, only=NB):
        E.wait("sync", "qs", R["phaseA_qs"])
        E.wait("scalar", "ac", R["w_done_ac"])

        def p_half(h, g2):
            if h % 2 == 0:
                return p_tm[:, g2 * 1024:(g2 + 1) * 1024]
            return p_tmB[g2]

        def emit_S(i, qb, h):
            ft, po, hp = h // 2, (h % 2) * 64, h % 2
            for q4 in range(4):
                if q4 < 2:
                    E.wait("tensor", "ac")
                E.op("tensor", "pe", lambda t, q4=q4, ft=ft, po=po, qb=qb: t.matmul(
                    P_S[:, q4 * 512:(q4 + 1) * 512],
                    lhsT=qb[po:po + 64, ft * 128:(ft + 1) * 128],
                    rhs=k_fm[po:po + 64,
                             ft * T + q4 * 512:ft * T + (q4 + 1) * 512],
                    start=True, stop=True))
                E.wait("vector", "pe")
                E.op("vector", "dv", lambda v, q4=q4: v.reduce_max(
                    out=tbm[:, q4:q4 + 1], in_=P_S[:, q4 * 512:(q4 + 1) * 512],
                    axis=AXX))
            E.wait("vector", "dv")
            E.op("vector", None, lambda v: v.tensor_tensor(
                out=tbm[:, 4:5], in0=tbm[:, 0:1], in1=tbm[:, 1:2], op=ALU.max))
            E.op("vector", "dv", lambda v: v.tensor_tensor(
                out=tbm[:, 5:6], in0=tbm[:, 2:3], in1=tbm[:, 3:4], op=ALU.max))
            E.wait("vector", "dv")
            E.op("vector", "dv", lambda v, hp=hp: v.scalar_tensor_tensor(
                out=negB[:, hp:hp + 1], in0=tbm[:, 4:5], scalar=-1.0,
                in1=tbm[:, 5:6], op0=ALU.bypass, op1=ALU.max))
            E.wait("vector", "dv")
            E.op("vector", "dv", lambda v, hp=hp: v.tensor_single_scalar(
                out=negB[:, hp:hp + 1], in_=negB[:, hp:hp + 1], scalar=-1.0,
                op=ALU.mult))
            for g2 in range(2):
                E.wait("scalar", "pe")
                E.wait("scalar", "dv")
                E.op("scalar", "ac", lambda s, g2=g2, hp=hp, h=h: s.activation(
                    out=p_half(h, g2),
                    in_=P_S[:, g2 * 1024:(g2 + 1) * 1024], func=AF.Exp,
                    bias=negB[:, hp:hp + 1], scale=1.0,
                    accum_out=tbm[:, 6 + 2 * hp + g2:7 + 2 * hp + g2]))
                R[("exph", h, g2)] = E.c["ac"]

        def emit_P(i, h):
            ft, po, hp = h // 2, (h % 2) * 64, h % 2
            ptr = ptr_sb if hp == 0 else ptr2
            for g2 in range(2):
                E.wait("tensor", "ac", R[("exph", h, g2)])
                if R.get(("ofm_ac", h - 1)):
                    E.wait("tensor", "ac", R[("ofm_ac", h - 1)])
                E.wait("tensor", "dv")
                for j in range(8):
                    E.op("tensor", "pe" if j == 7 else None,
                         lambda t, j=j, h=h, g2=g2: t.transpose(
                             P_T[:, j * 128:(j + 1) * 128],
                             in_=p_half(h, g2)[:, j * 128:(j + 1) * 128],
                             identity=ident_b[:]))
                E.wait("vector", "pe")
                E.op("vector", "dv", lambda v, g2=g2, ptr=ptr: v.tensor_copy(
                    ptr[:, g2 * 1024:(g2 + 1) * 1024], P_T[:, :]))
            E.wait("vector", "ac", R[("exph", h, 1)])
            E.op("vector", "dv", lambda v, hp=hp: v.tensor_tensor(
                out=tbm[:, 10:11], in0=tbm[:, 6 + 2 * hp:7 + 2 * hp],
                in1=tbm[:, 7 + 2 * hp:8 + 2 * hp], op=ALU.add))
            E.wait("vector", "dv")
            E.op("vector", "dv", lambda v, hp=hp: v.reciprocal(
                invd_att[:, hp:hp + 1], tbm[:, 10:11]))
            E.wait("tensor", "dv")
            for kb in range(16):
                E.op("tensor", "pe" if kb == 15 else None,
                     lambda t, kb=kb, h=h, ptr=ptr: t.matmul(
                         P_M[:, 256:320],
                         lhsT=ptr[:, kb * 128:(kb + 1) * 128],
                         rhs=v_aug[:, (kb * NH + h) * 65:(kb * NH + h) * 65 + 64],
                         start=(kb == 0), stop=(kb == 15)))
            E.wait("vector", "pe")
            E.op("vector", "dv", lambda v, po=po, hp=hp: v.tensor_scalar_mul(
                o_pair[:, po:po + 64], P_M[:, 256:320], invd_att[:, hp:hp + 1]))
            if hp == 1:
                E.wait("tensor", "dv")
                E.op("tensor", "pe", lambda t: t.transpose(
                    P_T[:, 0:128], in_=o_pair[:], identity=ident_b[:]))
                E.wait("scalar", "pe")
                E.op("scalar", "ac", lambda s, ft=ft: s.copy(
                    out=o_fm[:, ft * 128:(ft + 1) * 128],
                    in_=P_T[:, 0:128]))
                R[("ofm_ac", h)] = E.c["ac"]
            else:
                R[("ofm_ac", h)] = R.get(("ofm_ac", h - 1), 0)

        nhl = int(_os.environ.get("K_NHLIM", str(NH)))
        for i in range(only):
            qb = qbuf[i % 2]
            if i == 0:
                E.dma("sync", "ql", lambda h, i=i, qb=qb: h.dma_start(
                    out=qb[:], in_=q_sp[i]))
            E.wait("tensor", "dv")
            for h in range(nhl + 1):
                if h == 0:
                    if i == 0:
                        E.wait("tensor", "ql")
                        emit_S(i, qb, 0)
                    continue
                if h < nhl:
                    emit_S(i, qb, h)
                emit_P(i, h - 1)
            R[("qfree", i)] = E.c["pe"]
            if i + 1 < only:
                qb2 = qbuf[(i + 1) % 2]
                if i + 1 >= 2:
                    E.wait("sync", "pe", R.get(("qfree", i - 1)))
                E.dma("sync", "ql", lambda h, i=i, qb2=qb2: h.dma_start(
                    out=qb2[:], in_=q_sp[i + 1]))
                E.wait("tensor", "ql")
                emit_S(i + 1, qb2, 0)
            E.wait("tensor", "ac", R[("ofm_ac", nhl - 1)])
            for half in range(2):
                for c in range(8):
                    E.op("tensor", "pe" if (c == 7 and half == 1) else None,
                         lambda t, c=c, half=half: t.matmul(
                             P_W[:, half * 512:(half + 1) * 512],
                             lhsT=o_fm[:, c * 128:(c + 1) * 128],
                             rhs=woT[c][:, half * 512:(half + 1) * 512],
                             start=(c == 0), stop=(c == 7)))
            E.wait("vector", "pe")
            E.op("vector", "dv", lambda v, i=i: v.tensor_tensor(
                out=x_tm[:, i * D:(i + 1) * D], in0=P_W[:, :],
                in1=x_tm[:, i * D:(i + 1) * D], op=ALU.add))
            R[("attnadd", i)] = E.c["dv"]
            E.wait("tensor", "dv")
        R["attn_dv"] = E.c["dv"]

    def ln2_ffn_norm(E, b):
        xs = x_tm[:, b * D:(b + 1) * D]
        E.op("vector", None, lambda v: v.tensor_reduce(
            out=sx[:], in_=xs, axis=AXX, op=ALU.add))
        E.op("vector", "dv", lambda v: v.scalar_tensor_tensor(
            out=t1b[:], in0=xs, scalar=1.0, in1=xs,
            op0=ALU.mult, op1=ALU.mult, accum_out=sxx[:]))
        E.wait("vector", "dv")
        E.op("vector", "dv", lambda v: v.tensor_single_scalar(
            out=mln[:], in_=sx[:], scalar=1.0 / D, op=ALU.mult))
        E.wait("vector", "dv")
        E.op("vector", "dv", lambda v: v.tensor_tensor(
            out=m2ln[:], in0=mln[:], in1=mln[:], op=ALU.mult))
        E.wait("vector", "dv")
        E.op("vector", "dv", lambda v: v.scalar_tensor_tensor(
            out=varln[:], in0=sxx[:], scalar=1.0 / D, in1=m2ln[:],
            op0=ALU.mult, op1=ALU.subtract))
        E.wait("scalar", "dv")
        E.op("scalar", "ac", lambda s: s.activation(
            out=stdln[:], in_=varln[:], func=AF.Sqrt, bias=epsB[:], scale=1.000001))
        E.wait("vector", "ac")
        E.op("vector", "dv", lambda v: v.reciprocal(istd[:], stdln[:]))
        E.wait("vector", "dv")
        E.op("vector", "dv", lambda v: v.scalar_tensor_tensor(
            out=t1b[:], in0=xs, scalar=mln[:], in1=normw_b[:],
            op0=ALU.subtract, op1=ALU.mult))
        E.wait("vector", "dv")
        E.op("vector", "dv", lambda v: v.tensor_scalar_mul(t1b[:], t1b[:], istd[:]))

    def routing_block(E, b, slot):
        xslot = b % NXF
        if b >= NXF:
            E.wait("vector", "pe", R.get(("pairs_done", b - NXF)))
        E.wait("tensor", "dv")
        E.wait("tensor", "ac")
        if b >= NXF:
            E.wait("tensor", "pe", R.get(("pairs_done", b - NXF)))
        for c in range(8):
            E.op("tensor", "pe" if c == 7 else None, lambda t, c=c: t.transpose(
                P_T[:, c * 128:(c + 1) * 128], in_=t1b[:, c * 128:(c + 1) * 128],
                identity=ident_b[:]))
        E.wait("scalar", "pe")
        for c in range(8):
            E.op("scalar", "ac", lambda s, c=c, xslot=xslot: s.copy(
                out=xnffm[xslot][:, c * 128:(c + 1) * 128],
                in_=P_T[:, c * 128:(c + 1) * 128]))

        fia, coa = fi_all[slot], co_all[slot]
        if b >= NSLOT:
            E.wait("vector", "g", R.get(("gath_done", b - NSLOT)))
        for h in range(FH):
            for sub in range(2):
                ft = 2 * h + sub
                E.wait("tensor", "ac")
                for c in range(8):
                    E.op("tensor", "pe" if c == 7 else None,
                         lambda t, ft=ft, c=c, xslot=xslot: t.matmul(
                             P_S[:, 0:128],
                             lhsT=wqT[c][:, ft * 128:(ft + 1) * 128],
                             rhs=xnffm[xslot][:, c * 128:(c + 1) * 128],
                             start=(c == 0), stop=(c == 7)))
                E.wait("scalar", "pe")
                E.op("scalar", "ac", lambda s, sub=sub: s.copy(
                    out=qffn[:, sub * 128:(sub + 1) * 128], in_=P_S[:, 0:128]))
            E.wait("tensor", "ac")
            E.wait("tensor", "dv")
            for half in range(2):
                j = 2 * h + half
                E.op("tensor", "pe" if half == 1 else None,
                     lambda t, half=half, j=j: t.matmul(
                         P_W[:, half * 256:(half + 1) * 256],
                         lhsT=qffn[:, half * 128:(half + 1) * 128],
                         rhs=keysT[:, j * 256:(j + 1) * 256],
                         start=True, stop=True))
            E.wait("vector", "pe")
            E.op("vector", None, lambda v: v.max(out=v8a[:], in_=P_W[:, 0:256]))
            E.op("vector", "dv", lambda v: v.max(out=v8b[:], in_=P_W[:, 256:512]))
            E.wait("vector", "dv")
            for k in range(8):
                E.op("vector", None, lambda v, k=k: v.scalar_tensor_tensor(
                    out=scr256[:], in0=P_W[:, 0:256], scalar=v8a[:, k:k + 1],
                    in1=iota256[:], op0=ALU.is_equal, op1=ALU.mult,
                    accum_out=i1x[:, k:k + 1]))
                E.op("vector", "dv" if k == 7 else None,
                     lambda v, k=k: v.scalar_tensor_tensor(
                         out=scr256[:], in0=P_W[:, 256:512], scalar=v8b[:, k:k + 1],
                         in1=iota256[:], op0=ALU.is_equal, op1=ALU.mult,
                         accum_out=i2x[:, k:k + 1]))
            E.op("vector", None, lambda v: v.tensor_tensor(
                out=grid[:].rearrange("p (a b) -> p a b", a=8),
                in0=v8a[:].unsqueeze(2).broadcast_to((128, 8, 8)),
                in1=v8b[:].unsqueeze(1).broadcast_to((128, 8, 8)),
                op=ALU.add))
            E.op("vector", "dv", lambda v: v.max(out=fs8[:], in_=grid[:]))
            E.wait("vector", "dv")
            for k in range(8):
                E.op("vector", "dv" if k == 7 else None,
                     lambda v, k=k: v.scalar_tensor_tensor(
                         out=scr256[:, 0:64], in0=grid[:], scalar=fs8[:, k:k + 1],
                         in1=iota256[:, 0:64], op0=ALU.is_equal, op1=ALU.mult,
                         accum_out=pk8[:, k:k + 1]))
            E.wait("vector", "dv")
            E.op("vector", "dv", lambda v: v.tensor_copy(pki[:], pk8[:]))
            E.wait("vector", "dv")
            E.op("vector", None, lambda v: v.tensor_single_scalar(
                out=b8i[:], in_=pki[:], scalar=7, op=ALU.bitwise_and))
            E.op("vector", "dv", lambda v: v.tensor_single_scalar(
                out=a8i[:], in_=pki[:], scalar=3, op=ALU.logical_shift_right))
            E.wait("vector", "dv")
            E.op("vector", None, lambda v: v.tensor_copy(b8[:], b8i[:]))
            E.op("vector", None, lambda v: v.tensor_copy(a8[:], a8i[:]))
            E.op("vector", None, lambda v: v.memset(i1s[:], 0.0))
            E.op("vector", "dv", lambda v: v.memset(i2s[:], 0.0))
            E.wait("vector", "dv")
            for r in range(8):
                E.op("vector", None, lambda v, r=r: v.scalar_tensor_tensor(
                    out=tq1[:], in0=a8[:], scalar=float(r),
                    in1=i1x[:, r:r + 1].broadcast_to((128, 8)),
                    op0=ALU.is_equal, op1=ALU.mult))
                E.op("vector", None, lambda v, r=r: v.scalar_tensor_tensor(
                    out=tq2[:], in0=b8[:], scalar=float(r),
                    in1=i2x[:, r:r + 1].broadcast_to((128, 8)),
                    op0=ALU.is_equal, op1=ALU.mult))
                E.op("vector", None, lambda v: v.tensor_add(
                    out=i1s[:], in0=i1s[:], in1=tq1[:]))
                E.op("vector", "dv" if r == 7 else None, lambda v: v.tensor_add(
                    out=i2s[:], in0=i2s[:], in1=tq2[:]))
            E.wait("vector", "dv")
            E.op("vector", "dv", lambda v, h=h, fia=fia: v.scalar_tensor_tensor(
                out=fia[:, h * 8:(h + 1) * 8], in0=i1s[:], scalar=256.0, in1=i2s[:],
                op0=ALU.mult, op1=ALU.add))
            E.wait("vector", "dv")
            E.op("vector", None, lambda v, h=h, fia=fia: v.tensor_single_scalar(
                out=fia[:, h * 8:(h + 1) * 8], in_=fia[:, h * 8:(h + 1) * 8],
                scalar=65535.0, op=ALU.min))
            E.op("vector", "dv", lambda v, h=h, coa=coa: v.tensor_single_scalar(
                out=coa[:, h * 8:(h + 1) * 8], in_=fs8[:], scalar=0.0, op=ALU.max))

        E.wait("tensor", "dv")
        for (srct, tgt) in ((fia, gfi_f), (coa, gco[slot])):
            E.wait("tensor", "gt")
            E.op("tensor", "pe", lambda t, srct=srct: t.transpose(
                P_W[0:64, 512:640], in_=srct[:], identity=ident_f[:]))
            E.wait("vector", "pe")
            E.wait("vector", "gt")
            E.op("vector", None, lambda v, tgt=tgt: v.tensor_copy(
                tgt[0:64, :], P_W[0:64, 512:640][:, 0:128:2]))
            E.op("vector", "dv", lambda v: v.tensor_copy(
                gtmp[0:64, :], P_W[0:64, 512:640][:, 1:128:2]))
            E.wait("sync", "dv")
            E.dma("sync", "gt", lambda h, tgt=tgt: h.dma_start(
                out=tgt[64:128, :], in_=gtmp[0:64, :]))
            E.wait("tensor", "dv")
        E.wait("vector", "gt")
        E.op("vector", "dv", lambda v: v.tensor_copy(gfi[slot][:], gfi_f[:]))
        R[("route_done", b)] = E.c["dv"]
        R[("route_done_gt", b)] = E.c["gt"]

    pair_free = {}
    gath_val = {}

    def expert_block(E, b, slot):
        E.wait("vector", "gt", R[("route_done_gt", b)])
        E.op("vector", "dv", lambda v: v.memset(out_fm[:], 0.0))
        E.wait("tensor", "dv")
        for m in range(NPAIR):
            p = b * NPAIR + m
            r = p % NRING
            mg = m % 2
            gg = p // 2
            gb = gg % 2
            E.wait("gpsimd", "dv", R[("route_done", b)])
            E.wait("gpsimd", "gt", R[("route_done_gt", b)])
            if p >= NRING:
                E.wait("gpsimd", "pe", pair_free.get(p - NRING))
            E.dma("gpsimd", "g", lambda g, m=m, r=r, slot=slot: g.indirect_dma_start(
                out=ring_d[r], out_offset=None, in_=ap["edown"],
                in_offset=IndirectOffsetOnAxis(ap=gfi[slot][:, m:m + 1], axis=0)))
            E.dma("gpsimd", "g", lambda g, m=m, r=r, slot=slot: g.indirect_dma_start(
                out=ring_u[r], out_offset=None, in_=ap["eup"],
                in_offset=IndirectOffsetOnAxis(ap=gfi[slot][:, m:m + 1], axis=0)))
            gath_val[p] = E.c["g"]
            E.wait("tensor", "g", gath_val[p])
            E.wait("tensor", "dv")
            E.wait("tensor", "ac")
            for c in range(8):
                E.op("tensor", "pe" if c == 7 else None,
                     lambda t, c=c, r=r: t.transpose(
                         P_T[:, c * 128:(c + 1) * 128],
                         in_=ring_d[r][:, c * 128:(c + 1) * 128],
                         identity=ident_b[:]))
            if mg == 0:
                E.wait("scalar", "pe")
                E.op("scalar", "ac", lambda s, mg=mg: s.copy(
                    out=wdfm[mg][:], in_=P_T[:, :]))
                E.wait("tensor", "ac")
            else:
                E.wait("vector", "pe")
                E.op("vector", "dv", lambda v, mg=mg: v.tensor_copy(
                    wdfm[mg][:], P_T[:, :]))
                E.wait("tensor", "dv")
            dbase = 640 + gb * 4 + mg * 2
            if mg == 0 and gg >= 2:
                E.wait("tensor", "dv", R.get(("diag_free", gg - 2)))
            if mg == 0 and gg >= 1:
                E.wait("tensor", "dv", R.get(("upg_done", gg - 1)))
            for c in range(8):
                E.op("tensor", "pe" if c == 7 else None,
                     lambda t, c=c, mg=mg, m=m, xslot=b % NXF, dbase=dbase: t.matmul(
                         P_W[:, dbase:dbase + 2],
                         lhsT=wdfm[mg][:, c * 128:(c + 1) * 128],
                         rhs=xnffm[xslot][:, c * 128 + 2 * m:c * 128 + 2 * m + 2],
                         start=(c == 0), stop=(c == 7)))
            if mg == 1:
                base = 640 + gb * 4
                E.wait("vector", "pe")
                E.op("vector", None, lambda v, base=base, gb=gb: v.tensor_copy(
                    h4[gb][0:64, 0:2],
                    P_W[0:64, base:base + 4].rearrange("p (a b) -> p a b", a=2)[:, :, 0]))
                E.op("vector", "dv", lambda v, base=base, gb=gb: v.tensor_copy(
                    h4[gb][64:128, 0:2],
                    P_W[64:128, base:base + 4].rearrange("p (a b) -> p a b", a=2)[:, :, 1]))
                R[("diag_free", gg)] = E.c["dv"]
                E.wait("scalar", "dv")
                E.op("scalar", "ac", lambda s, gb=gb: s.activation(
                    out=h4[gb][:, 0:2], in_=h4[gb][:, 0:2], func=AF.Gelu))
                E.wait("vector", "ac")
                E.op("vector", "dv", lambda v, gb=gb, m=m, slot=slot: v.tensor_tensor(
                    out=hid4[gb][:, 0:2], in0=h4[gb][:, 0:2],
                    in1=gco[slot][:, m - 1:m + 1], op=ALU.mult))
                E.wait("vector", "dv")
                E.op("vector", None, lambda v, gb=gb: v.memset(chid[gb][:, 0:4], 0.0))
                E.op("vector", None, lambda v, gb=gb: v.tensor_copy(
                    chid[gb][0:64, 0:4].rearrange("p (a b) -> p a b", a=2)[:, :, 0],
                    hid4[gb][0:64, 0:2]))
                E.op("vector", "dv", lambda v, gb=gb: v.tensor_copy(
                    chid[gb][64:128, 0:4].rearrange("p (a b) -> p a b", a=2)[:, :, 1],
                    hid4[gb][64:128, 0:2]))
                E.wait("tensor", "dv")
                for mm in range(m - 1, m + 1):
                    pp = b * NPAIR + mm
                    rr = pp % NRING
                    upsum = P_W[:, 656:672] if mm % 2 == 0 else P_M[:, 256:272]
                    if pp >= 2:
                        E.wait("tensor", "dv", R.get(("up_evac", pp - 2)))
                    for c in range(8):
                        E.op("tensor", "pe" if c == 7 else None,
                             lambda t, c=c, rr=rr, mm=mm, gb=gb, upsum=upsum: t.matmul(
                                 upsum[:, c * 2:c * 2 + 2],
                                 lhsT=ring_u[rr][:, c * 128:(c + 1) * 128],
                                 rhs=chid[gb][:, (mm % 2) * 2:(mm % 2) * 2 + 2],
                                 start=True, stop=True))
                    pair_free[pp] = E.c["pe"]
                    E.wait("vector", "pe")
                    E.op("vector", "dv", lambda v, mm=mm, upsum=upsum: v.tensor_tensor(
                        out=out_fm[:].rearrange("p (c t) -> p c t", c=8)
                            [:, :, 2 * mm:2 * mm + 2],
                        in0=upsum.rearrange("p (c t) -> p c t", t=2),
                        in1=out_fm[:].rearrange("p (c t) -> p c t", c=8)
                            [:, :, 2 * mm:2 * mm + 2],
                        op=ALU.add))
                    R[("up_evac", pp)] = E.c["dv"]
                    if mm % 2 == 1:
                        R[("upg_done", gg)] = E.c["dv"]
        R[("pairs_done", b)] = E.c["pe"]
        R[("gath_done", b)] = E.c["g"]
        E.wait("tensor", "dv", R[("up_evac", b * NPAIR + NPAIR - 1)])
        for c in range(8):
            E.op("tensor", "pe" if c == 7 else None, lambda t, c=c: t.transpose(
                P_S[:, 1024 + c * 128:1024 + (c + 1) * 128],
                in_=out_fm[:, c * 128:(c + 1) * 128], identity=ident_f[:]))
        E.wait("vector", "pe")
        E.op("vector", "dv", lambda v, b=b: v.tensor_tensor(
            out=x_tm[:, b * D:(b + 1) * D], in0=P_S[:, 1024:2048],
            in1=x_tm[:, b * D:(b + 1) * D], op=ALU.add))
        E.wait("gpsimd", "dv")
        E.dma("gpsimd", "st", lambda g, b=b: g.dma_start(
            out=out[b * 128:(b + 1) * 128, :], in_=x_tm[:, b * D:(b + 1) * D]))

    def ffn(E):
        nbf = int(_os.environ.get("K_NBFFN", str(NB)))
        if FFN_AHEAD:
            for b in range(nbf):
                E.wait("vector", "dv", R[("attnadd", b)])
                ln2_ffn_norm(E, b)
                routing_block(E, b, b % NSLOT)
                if b >= 1:
                    expert_block(E, b - 1, (b - 1) % NSLOT)
            expert_block(E, nbf - 1, (nbf - 1) % NSLOT)
        else:
            for b in range(NB):
                E.wait("vector", "dv", R[("attnadd", b)])
                ln2_ffn_norm(E, b)
                routing_block(E, b, b % NSLOT)
                expert_block(E, b, b % NSLOT)

    E.c = dict(snapA)
    phase_b(E)

    with nc.Block() as blockB:
        for eng in ENGS:
            def _sect(handle, eng=eng):
                E.c = dict(snapA)
                E.active = (eng, handle)
                phase_b(E)
                E.active = None
            getattr(blockB, eng)(_sect)

    ctx_b.close()
    ctx_p.close()
    return nc


def kernel(L0_attn_in_features, L0_attn_W_qkv, L0_attn_W_o, L0_ffn_w_q,
           L0_ffn_keys, L0_ffn_expert_down, L0_ffn_expert_up, L0_ffn_norm_w):
    x = np.ascontiguousarray(L0_attn_in_features)
    nc = bass.Bass("TRN2", target_bir_lowering=False)
    build_program(nc)
    shared = {
        "wqkv": np.ascontiguousarray(L0_attn_W_qkv),
        "wo": np.ascontiguousarray(L0_attn_W_o),
        "wq": np.ascontiguousarray(L0_ffn_w_q),
        "keys": np.ascontiguousarray(L0_ffn_keys),
        "edown": np.ascontiguousarray(L0_ffn_expert_down),
        "eup": np.ascontiguousarray(L0_ffn_expert_up),
        "normw": np.ascontiguousarray(L0_ffn_norm_w),
    }
    in_maps = [dict(shared, x=np.ascontiguousarray(x[b])) for b in range(8)]
    res = run_bass_kernel_spmd(nc, in_maps, core_ids=list(range(8)))
    return np.stack([np.asarray(r["out"]) for r in res.results], axis=0).astype(np.float32)


# version 09 (final)
# speedup vs baseline: 1.0973x; 1.0008x over previous
import numpy as np

import concourse.bass as bass
import concourse.mybir as mybir
from concourse.bass_utils import run_bass_kernel_spmd
from concourse.bass_core_rust import IndirectOffsetOnAxis

F32 = mybir.dt.float32
BF16 = mybir.dt.bfloat16
I32 = mybir.dt.int32
AF = mybir.ActivationFunctionType
ALU = mybir.AluOpType
AXX = mybir.AxisListType.X

DBG_STAGE = None
FFN_AHEAD = True

T = 2048
D = 1024
NB = T // 128
NH = 16
FH = 8
NKEY = 256
NPAIR = 64
SCALE = 0.125
import os as _os
NRING = int(_os.environ.get("K_NRING", "8"))
NSLOT = 2
NXF = 2

ENGS = ("sync", "tensor", "vector", "scalar", "gpsimd")


class Emit:

    RECORD = False

    def __init__(self, nc, sems):
        self.nc = nc
        self.sems = sems
        self.active = None
        self.c = {k: 0 for k in sems}
        self.streams = {}

    def _line(self):
        import traceback
        out = []
        for fr in reversed(traceback.extract_stack()):
            if "kernel" in fr.filename.rsplit("/", 1)[-1]:
                out.append(f"{fr.name}:{fr.lineno}")
                if len(out) == 2:
                    break
        return ",".join(out)

    def op(self, eng, sem, fn):
        if sem is not None:
            self.c[sem] += 1
        if self.active is not None and self.active[0] == eng:
            if Emit.RECORD:
                self.streams.setdefault(eng, []).append(("i", sem, 1, self._line()))
                return
            ins = fn(self.active[1])
            if sem is not None:
                ins.then_inc(self.sems[sem], 1)

    def dma(self, eng, sem, fn):
        if sem is not None:
            self.c[sem] += 16
        if self.active is not None and self.active[0] == eng:
            if Emit.RECORD:
                self.streams.setdefault(eng, []).append(("i", sem, 16, self._line()))
                return
            ins = fn(self.active[1])
            if sem is not None:
                ins.then_inc(self.sems[sem], 16)

    def wait(self, eng, sem, val=None):
        if self.active is not None and self.active[0] == eng:
            tgt = self.c[sem] if val is None else val
            if tgt > 0:
                if Emit.RECORD:
                    self.streams.setdefault(eng, []).append(("w", sem, tgt, self._line()))
                    return
                self.active[1].wait_ge(self.sems[sem], tgt)

    def snap(self):
        return dict(self.c)

    _INSTANCES = []


def build_program(nc):
    ap = {}
    for nm, sh, dt in [
        ("x", [T, D], F32), ("wqkv", [3 * D, D], F32), ("wo", [D, D], F32),
        ("wq", [2 * D, D], F32), ("keys", [FH, NKEY, 2, 128], F32),
        ("edown", [65536, D], F32), ("eup", [65536, D], F32),
        ("normw", [D], F32),
    ]:
        ap[nm] = nc.dram_tensor(nm, sh, dt, kind="ExternalInput").ap()
    out = nc.dram_tensor("out", [T, D], F32, kind="ExternalOutput").ap()
    if DBG_STAGE == "QKV":
        kdbg = nc.dram_tensor("kdbg", [128, 8 * T], BF16, kind="ExternalOutput").ap()
        vdbg = nc.dram_tensor("vdbg", [128, NB * D], BF16, kind="ExternalOutput").ap()
        qdbg = nc.dram_tensor("qdbg", [NB, 128, 8 * 128], BF16, kind="ExternalOutput").ap()
    if DBG_STAGE == "ATT0":
        ofdbg = nc.dram_tensor("ofdbg", [128, 8 * 128], BF16, kind="ExternalOutput").ap()
        ptdbg = nc.dram_tensor("ptdbg", [128, T], BF16, kind="ExternalOutput").ap()
        ptrdbg = nc.dram_tensor("ptrdbg", [128, T], BF16, kind="ExternalOutput").ap()
        dendbg = nc.dram_tensor("dendbg", [128, 4], F32, kind="ExternalOutput").ap()
    q_sp = nc.dram_tensor("q_sp", [NB, 128, 8 * 128], BF16, kind="Internal").ap()

    from contextlib import ExitStack
    ctx_p = ExitStack()
    sem_names = ["w", "xl", "qs", "ql", "g", "pe", "dv", "ac", "st", "gp", "gt"]
    sems = {nm: ctx_p.enter_context(nc.semaphore("s_" + nm)) for nm in sem_names}
    E = Emit(nc, sems)
    Emit._INSTANCES.append(E)

    def sbp(name, shape, dt=F32):
        return ctx_p.enter_context(nc.sbuf_tensor(name, shape, dt))

    x_tm = sbp("x_tm", [128, NB * D], BF16)
    k_fm = sbp("k_fm", [128, 8 * T], BF16)
    v_aug = sbp("v_aug", [128, NB * NH * 65], BF16)
    ones_c = sbp("ones_c", [128, 1], BF16)
    ones_r = sbp("ones_r", [1, 128], BF16)
    km2 = sbp("km2", [1, 16], F32)
    tbm = sbp("tbm", [128, 12], F32)
    rowm = sbp("rowm", [128, 1], BF16)
    tb1 = sbp("tb1", [1, 4], F32)
    tb2 = sbp("tb2", [1, 1], F32)
    tb3 = sbp("tb3", [1, 1], BF16)
    negB = sbp("negB", [128, 2], F32)
    dbg8 = sbp("dbg8", [128, 8], F32)
    qbuf = [sbp(f"qbuf{i}", [128, 8 * 128], BF16) for i in range(2)]
    ident_b = sbp("ident_b", [128, 128], BF16)
    ident_f = sbp("ident_f", [128, 128], F32)
    iota256 = sbp("iota256", [128, NKEY], F32)
    epsA = sbp("epsA", [128, 1], F32)
    epsB = sbp("epsB", [128, 1], F32)
    t1b = sbp("t1b", [128, D], BF16)
    sx = sbp("sx", [128, 1], F32)
    sxx = sbp("sxx", [128, 1], F32)
    mln = sbp("mln", [128, 1], F32)
    m2ln = sbp("m2ln", [128, 1], F32)
    varln = sbp("varln", [128, 1], F32)
    stdln = sbp("stdln", [128, 1], F32)
    istd = sbp("istd", [128, 1], F32)
    scr256 = sbp("scr256", [128, NKEY], F32)

    P_S = ctx_p.enter_context(nc.psum_tensor("P_S", [128, 2048], F32))
    P_T = ctx_p.enter_context(nc.psum_tensor("P_T", [128, 1024], BF16))
    P_W = ctx_p.enter_context(nc.psum_tensor("P_W", [128, 1024], F32))
    P_M = ctx_p.enter_context(nc.psum_tensor("P_M", [128, 512], F32))

    ctx_a = ExitStack()
    xn_fm = ctx_a.enter_context(nc.sbuf_tensor("xn_fm", [128, 8 * T], BF16))
    wT = [ctx_a.enter_context(nc.sbuf_tensor(f"wT{c}", [128, D], BF16))
          for c in range(8)]
    wstg = [ctx_a.enter_context(nc.sbuf_tensor(f"wstg{i}", [128, D], F32))
            for i in range(2)]
    xf32 = [ctx_a.enter_context(nc.sbuf_tensor(f"xf32{i}", [128, D], F32))
            for i in range(2)]
    qsp4 = ctx_a.enter_context(nc.sbuf_tensor("qsp4", [128, 4096], BF16))

    R = {}
    lt_seq = [0]
    qk_seq = [0]

    def load_transpose(E, src, nft, dst, stage):
        for ft in range(nft):
            g = lt_seq[0]
            lt_seq[0] += 1
            buf = stage[g % 2]
            if g >= 2:
                E.wait("sync", "pe", R.get(("lt", g - 2)))
            E.dma("sync", "w", lambda h, ft=ft, buf=buf: h.dma_start(
                out=buf[:], in_=src[ft * 128:(ft + 1) * 128, :]))
            E.wait("tensor", "w")
            E.wait("tensor", "ac")
            for c in range(8):
                E.op("tensor", "pe" if c == 7 else None,
                     lambda t, c=c, buf=buf: t.transpose(
                         P_S[:, c * 128:(c + 1) * 128],
                         in_=buf[:, c * 128:(c + 1) * 128], identity=ident_f[:]))
            R[("lt", g)] = E.c["pe"]
            E.wait("scalar", "pe")
            for c in range(8):
                E.op("scalar", "ac", lambda s, c=c, ft=ft: s.copy(
                    out=dst(c, ft), in_=P_S[:, c * 128:(c + 1) * 128]))

    def phase_a(E):
        lt_seq[0] = 0
        qk_seq[0] = 0
        E.op("gpsimd", None, lambda g: g.memset(epsA[:], 1e-5))
        E.op("gpsimd", None, lambda g: g.memset(ones_c[:], 1.0))
        E.op("gpsimd", None, lambda g: g.memset(ones_r[:], 1.0))
        E.op("gpsimd", "gp", lambda g: g.memset(v_aug[:], 1.0))
        E.op("gpsimd", None, lambda g: g.memset(epsB[:], 1e-11))
        E.op("gpsimd", None, lambda g: g.iota(
            iota256[:], pattern=[[1, NKEY]], base=0, channel_multiplier=0,
            allow_small_or_imprecise_dtypes=True))
        E.op("gpsimd", None, lambda g: g.memset(ident_b[:], 0.0))
        E.op("gpsimd", "gp", lambda g: g.affine_select(
            out=ident_b[:], in_=ident_b[:], compare_op=ALU.not_equal,
            fill=1.0, base=0, pattern=[[-1, 128]], channel_multiplier=1))
        E.op("gpsimd", None, lambda g: g.memset(ident_f[:], 0.0))
        E.op("gpsimd", "gp", lambda g: g.affine_select(
            out=ident_f[:], in_=ident_f[:], compare_op=ALU.not_equal,
            fill=1.0, base=0, pattern=[[-1, 128]], channel_multiplier=1))
        E.wait("tensor", "gp")
        E.wait("vector", "gp")

        for b in range(NB):
            xb = xf32[b % 2]
            if b >= 2:
                E.wait("sync", "dv", R.get(("xfree", b - 2)))
            E.dma("sync", "xl", lambda h, b=b, xb=xb: h.dma_start(
                out=xb[:], in_=ap["x"][b * 128:(b + 1) * 128, :]))
            E.wait("vector", "xl")
            E.op("vector", None, lambda v, xb=xb: v.tensor_reduce(
                out=sx[:], in_=xb[:], axis=AXX, op=ALU.add))
            E.op("vector", "dv", lambda v, xb=xb: v.scalar_tensor_tensor(
                out=t1b[:], in0=xb[:], scalar=1.0, in1=xb[:],
                op0=ALU.mult, op1=ALU.mult, accum_out=sxx[:]))
            E.wait("vector", "dv")
            E.op("vector", "dv", lambda v: v.tensor_single_scalar(
                out=mln[:], in_=sx[:], scalar=1.0 / D, op=ALU.mult))
            E.wait("vector", "dv")
            E.op("vector", "dv", lambda v: v.tensor_tensor(
                out=m2ln[:], in0=mln[:], in1=mln[:], op=ALU.mult))
            E.wait("vector", "dv")
            E.op("vector", "dv", lambda v: v.scalar_tensor_tensor(
                out=varln[:], in0=sxx[:], scalar=1.0 / D, in1=m2ln[:],
                op0=ALU.mult, op1=ALU.subtract))
            E.wait("scalar", "dv")
            E.op("scalar", "ac", lambda s: s.activation(
                out=stdln[:], in_=varln[:], func=AF.Sqrt, bias=epsA[:], scale=1.0))
            E.wait("vector", "ac")
            E.op("vector", "dv", lambda v: v.reciprocal(istd[:], stdln[:]))
            E.wait("vector", "dv")
            E.op("vector", "dv", lambda v, xb=xb: v.scalar_tensor_tensor(
                out=t1b[:], in0=xb[:], scalar=mln[:],
                in1=istd[:].broadcast_to((128, D)),
                op0=ALU.subtract, op1=ALU.mult))
            E.op("vector", "dv", lambda v, b=b, xb=xb: v.tensor_copy(
                x_tm[:, b * D:(b + 1) * D], xb[:]))
            R[("xfree", b)] = E.c["dv"]
            E.wait("tensor", "dv")
            E.wait("tensor", "ac")
            for c in range(8):
                E.op("tensor", "pe" if c == 7 else None, lambda t, c=c: t.transpose(
                    P_T[:, c * 128:(c + 1) * 128], in_=t1b[:, c * 128:(c + 1) * 128],
                    identity=ident_b[:]))
            E.wait("scalar", "pe")
            for c in range(8):
                E.op("scalar", "ac", lambda s, c=c, b=b: s.copy(
                    out=xn_fm[:, c * T + b * 128:c * T + (b + 1) * 128],
                    in_=P_T[:, c * 128:(c + 1) * 128]))

        for part in range(3):
            load_transpose(E, ap["wqkv"][part * D:(part + 1) * D, :], 8,
                           lambda c, ft: wT[c][:, ft * 128:(ft + 1) * 128],
                           wstg)
            E.wait("tensor", "ac")
            for bq in range(NB // 4):
                if part < 2:
                    if part == 0 and bq >= 1:
                        E.wait("scalar", "qs", R.get(("spill", bq - 1)))
                    for ft in range(8):
                        qk = qk_seq[0]
                        qk_seq[0] += 1
                        ps = P_M[:, 0:512] if qk % 2 == 0 else P_W[:, 0:512]
                        if qk >= 2:
                            E.wait("tensor", "ac", R.get(("qkevac", qk - 2)))
                        for c in range(8):
                            E.op("tensor", "pe" if c == 7 else None,
                                 lambda t, ft=ft, c=c, bq=bq, ps=ps: t.matmul(
                                     ps,
                                     lhsT=wT[c][:, ft * 128:(ft + 1) * 128],
                                     rhs=xn_fm[:, c * T + bq * 512:c * T + (bq + 1) * 512],
                                     start=(c == 0), stop=(c == 7)))
                        E.wait("scalar", "pe")
                        if part == 0:
                            for sb4 in range(4):
                                E.op("scalar",
                                     "ac" if sb4 == 3 else None,
                                     lambda s, ft=ft, sb4=sb4, ps=ps: s.activation(
                                         out=qsp4[:, sb4 * 1024 + ft * 128:sb4 * 1024 + (ft + 1) * 128],
                                         in_=ps[:, sb4 * 128:(sb4 + 1) * 128],
                                         func=AF.Copy, scale=SCALE))
                            if ft == 7:
                                E.wait("sync", "ac")
                                for sb4 in range(4):
                                    E.dma("sync", "qs", lambda h, bq=bq, sb4=sb4: h.dma_start(
                                        out=q_sp[4 * bq + sb4],
                                        in_=qsp4[:, sb4 * 1024:(sb4 + 1) * 1024]))
                                R[("spill", bq)] = E.c["qs"]
                        else:
                            for sb4 in range(4):
                                b = bq * 4 + sb4
                                E.op("scalar", "ac" if sb4 == 3 else None,
                                     lambda s, ft=ft, b=b, sb4=sb4, ps=ps: s.copy(
                                         out=k_fm[:, ft * T + b * 128:ft * T + (b + 1) * 128],
                                         in_=ps[:, sb4 * 128:(sb4 + 1) * 128]))
                        R[("qkevac", qk)] = E.c["ac"]
                else:
                    for sb4 in range(4):
                        b = bq * 4 + sb4
                        E.wait("tensor", "ac")
                        for half in range(2):
                            for c in range(8):
                                E.op("tensor",
                                     "pe" if (c == 7 and half == 1) else None,
                                     lambda t, c=c, b=b, half=half: t.matmul(
                                         P_S[:, half * 512:(half + 1) * 512],
                                         lhsT=xn_fm[:, c * T + b * 128:c * T + (b + 1) * 128],
                                         rhs=wT[c][:, half * 512:(half + 1) * 512],
                                         start=(c == 0), stop=(c == 7)))
                        E.wait("scalar", "pe")
                        E.wait("scalar", "gp")
                        E.op("scalar", "ac", lambda s, b=b: s.copy(
                            out=v_aug[:, b * NH * 65:(b + 1) * NH * 65]
                                .rearrange("p (h u) -> p h u", h=NH)[:, :, 0:64],
                            in_=P_S[:, 0:1024].rearrange("p (h u) -> p h u", h=NH)))
        R["phaseA_pe"] = E.c["pe"]
        R["phaseA_ac"] = E.c["ac"]
        R["phaseA_dv"] = E.c["dv"]
        R["phaseA_qs"] = E.c["qs"]

    phase_a(E)
    snapA = E.snap()

    with nc.Block() as blockA:
        for eng in ENGS:
            def _sect(handle, eng=eng):
                E.c = {k: 0 for k in sems}
                E.active = (eng, handle)
                phase_a(E)
                E.active = None
            getattr(blockA, eng)(_sect)

    ctx_a.close()

    ctx_b = ExitStack()

    def sbb(name, shape, dt=F32):
        return ctx_b.enter_context(nc.sbuf_tensor(name, shape, dt))

    woT = [sbb(f"woT{c}", [128, D], BF16) for c in range(8)]
    wqT = [sbb(f"wqT{c}", [128, 2 * D], BF16) for c in range(8)]
    keysT = sbb("keysT", [128, 16 * NKEY], BF16)
    normw_b = sbb("normw_b", [128, D], BF16)
    p_tm = sbb("p_tm", [128, T], BF16)
    ptr_sb = sbb("ptr_sb", [128, T], BF16)
    ptr2 = sbb("ptr2", [128, T], BF16)
    wstg2 = [p_tm[:].bitcast(F32), ptr_sb[:].bitcast(F32)]
    o_fm = sbb("o_fm", [128, 8 * 128], BF16)
    o_pair = sbb("o_pair", [128, 128], BF16)
    gtmp = sbb("gtmp", [128, 64], F32)
    invd_att = sbb("invd_att", [128, 2], F32)
    xnffm = [sbb(f"xnffm{i}", [128, 8 * 128], BF16) for i in range(NXF)]
    qffn = sbb("qffn", [128, 2 * 128], BF16)
    v8a = sbb("v8a", [128, 8], F32)
    v8b = sbb("v8b", [128, 8], F32)
    i1x = sbb("i1x", [128, 8], F32)
    i2x = sbb("i2x", [128, 8], F32)
    grid = sbb("grid", [128, 64], F32)
    fs8 = sbb("fs8", [128, 8], F32)
    pk8 = sbb("pk8", [128, 8], F32)
    pki = sbb("pki", [128, 8], I32)
    a8i = sbb("a8i", [128, 8], I32)
    b8i = sbb("b8i", [128, 8], I32)
    a8 = sbb("a8", [128, 8], F32)
    b8 = sbb("b8", [128, 8], F32)
    i1s = sbb("i1s", [128, 8], F32)
    i2s = sbb("i2s", [128, 8], F32)
    tq1 = sbb("tq1", [128, 8], F32)
    tq2 = sbb("tq2", [128, 8], F32)
    fi_all = [sbb(f"fi_all{i}", [128, 64], F32) for i in range(NSLOT)]
    co_all = [sbb(f"co_all{i}", [128, 64], F32) for i in range(NSLOT)]
    gfi = [sbb(f"gfi{i}", [128, 64], I32) for i in range(NSLOT)]
    gco = [sbb(f"gco{i}", [128, 64], F32) for i in range(NSLOT)]
    gfi_f = sbb("gfi_f", [128, 64], F32)
    ring_d = [sbb(f"ring_d{i}", [128, D], BF16) for i in range(2)]
    ring_u = [sbb(f"ring_u{i}", [128, D], BF16) for i in range(2)]
    p_tmB = [t[:] for t in ring_d]
    ring_d = [t[:] for t in ring_d] + [p_tm[:, 0:D], p_tm[:, D:2 * D],
                                       qbuf[0][:], ptr2[:, 0:D],
                                       o_fm[:], k_fm[:, 0:D]]
    ring_u = [t[:] for t in ring_u] + [ptr_sb[:, 0:D], ptr_sb[:, D:2 * D],
                                       qbuf[1][:], ptr2[:, D:2 * D],
                                       k_fm[:, D:2 * D], k_fm[:, 2 * D:3 * D]]
    ring_d = ring_d[:NRING]
    ring_u = ring_u[:NRING]
    wdfm = [sbb(f"wdfm{i}", [128, D], BF16) for i in range(2)]
    h4 = [sbb(f"h4_{i}", [128, 4], F32) for i in range(2)]
    hid4 = [sbb(f"hid4_{i}", [128, 4], F32) for i in range(2)]
    chid = [sbb(f"chid{i}", [128, 8], BF16) for i in range(2)]
    out_fm = sbb("out_fm", [128, 8 * 128], F32)

    def phase_b(E):
        lt_seq[0] = 24
        E.wait("sync", "pe", R["phaseA_pe"])
        E.wait("sync", "ac", R["phaseA_ac"])
        E.wait("sync", "dv", R["phaseA_dv"])
        load_transpose(E, ap["wo"], 8,
                       lambda c, ft: woT[c][:, ft * 128:(ft + 1) * 128], wstg2)
        load_transpose(E, ap["wq"], 16,
                       lambda c, ft: wqT[c][:, ft * 128:(ft + 1) * 128], wstg2)
        for h in range(FH):
            for half in range(2):
                j = 2 * h + half
                buf = wstg2[j % 2]
                if j >= 2:
                    E.wait("sync", "pe", R.get(("keytp", j - 2)))
                else:
                    E.wait("sync", "pe", R.get(("lt", lt_seq[0] - 2 + j)))
                E.dma("sync", "w", lambda s, h=h, half=half, buf=buf: s.dma_start(
                    out=buf[:, 0:256].rearrange("p (a b) -> p a b", a=2),
                    in_=ap["keys"][h, :, half, :].rearrange("(a p) d -> p a d", p=128)))
                E.wait("tensor", "w")
                E.wait("tensor", "ac")
                for c in range(2):
                    E.op("tensor", "pe" if c == 1 else None,
                         lambda t, c=c, buf=buf: t.transpose(
                             P_S[:, c * 128:(c + 1) * 128],
                             in_=buf[:, c * 128:(c + 1) * 128],
                             identity=ident_f[:]))
                R[("keytp", j)] = E.c["pe"]
                E.wait("scalar", "pe")
                for c in range(2):
                    E.op("scalar", "ac", lambda s, c=c, j=j: s.copy(
                        out=keysT[:, j * 256 + c * 128:j * 256 + (c + 1) * 128],
                        in_=P_S[:, c * 128:(c + 1) * 128]))
        E.dma("gpsimd", "g", lambda g: g.dma_start(
            out=normw_b[:], in_=ap["normw"].unsqueeze(0).broadcast_to((128, D))))
        R["w_done_ac"] = E.c["ac"]
        R["w_done_g"] = E.c["g"]

        if DBG_STAGE == "A":
            dbg_store(E)
            E.wait("sync", "st", 16 * NB)
            return
        if DBG_STAGE == "QKV":
            E.wait("gpsimd", "ac")
            E.wait("gpsimd", "qs")
            E.dma("gpsimd", "st", lambda g: g.dma_start(out=kdbg, in_=k_fm[:]))
            E.dma("gpsimd", "st", lambda g: g.dma_start(out=vdbg, in_=v_tm[:]))
            E.dma("gpsimd", "st", lambda g: g.dma_start(
                out=qdbg.rearrange("a b d -> (a b) d"),
                in_=q_sp.rearrange("a b d -> (a b) d")))
            E.dma("gpsimd", "st", lambda g: g.dma_start(
                out=out[0:128, :], in_=x_tm[:, 0:D]))
            E.wait("gpsimd", "st")
            return
        if DBG_STAGE == "ATT0":
            attention(E, only=1)
            E.wait("gpsimd", "dv")
            E.wait("gpsimd", "ac")
            E.dma("gpsimd", "st", lambda g: g.dma_start(out=ofdbg, in_=o_fm[:]))
            E.dma("gpsimd", "st", lambda g: g.dma_start(out=ptdbg[:, 0:1024], in_=qbuf[0][:]))
            E.dma("gpsimd", "st", lambda g: g.dma_start(out=ptrdbg, in_=ptr_sb[:]))
            E.dma("gpsimd", "st", lambda g: g.dma_start(
                out=dendbg[:, 0:2], in_=negB[:]))
            E.dma("gpsimd", "st", lambda g: g.dma_start(
                out=ptdbg[:, 1024:1032], in_=dbg8[:]))
            E.dma("gpsimd", "st", lambda g: g.dma_start(
                out=dendbg[:, 2:3], in_=invd_att[:], allow_slow_non_contiguous=True))
            E.dma("gpsimd", "st", lambda g: g.dma_start(
                out=dendbg[0:1, 3:4].broadcast_to((1, 1)), in_=km2[:, 15:16],
                allow_slow_non_contiguous=True))
            E.dma("gpsimd", "st", lambda g: g.dma_start(
                out=out[0:128, :], in_=x_tm[:, 0:D]))
            E.wait("gpsimd", "st")
            return
        attention(E)
        if DBG_STAGE == "ATT":
            dbg_store(E)
            E.wait("sync", "st", 16 * NB)
            return
        ffn(E)
        E.wait("sync", "st", E.c["st"])

    def dbg_store(E):
        E.wait("gpsimd", "dv")
        E.wait("gpsimd", "ac")
        E.wait("gpsimd", "pe")
        for b in range(NB):
            E.dma("gpsimd", "st", lambda g, b=b: g.dma_start(
                out=out[b * 128:(b + 1) * 128, :], in_=x_tm[:, b * D:(b + 1) * D]))

    def attention(E, only=NB):
        E.wait("sync", "qs", R["phaseA_qs"])
        E.wait("scalar", "ac", R["w_done_ac"])

        def p_half(h, g2):
            if h % 2 == 0:
                return p_tm[:, g2 * 1024:(g2 + 1) * 1024]
            return p_tmB[g2]

        def emit_S(i, qb, h):
            ft, po, hp = h // 2, (h % 2) * 64, h % 2
            for q4 in range(4):
                if q4 < 2:
                    E.wait("tensor", "ac")
                E.op("tensor", "pe", lambda t, q4=q4, ft=ft, po=po, qb=qb: t.matmul(
                    P_S[:, q4 * 512:(q4 + 1) * 512],
                    lhsT=qb[po:po + 64, ft * 128:(ft + 1) * 128],
                    rhs=k_fm[po:po + 64,
                             ft * T + q4 * 512:ft * T + (q4 + 1) * 512],
                    start=True, stop=True))
                E.wait("vector", "pe")
                E.op("vector", "dv", lambda v, q4=q4: v.reduce_max(
                    out=tbm[:, q4:q4 + 1], in_=P_S[:, q4 * 512:(q4 + 1) * 512],
                    axis=AXX))
            E.wait("vector", "dv")
            E.op("vector", None, lambda v: v.tensor_tensor(
                out=tbm[:, 4:5], in0=tbm[:, 0:1], in1=tbm[:, 1:2], op=ALU.max))
            E.op("vector", "dv", lambda v: v.tensor_tensor(
                out=tbm[:, 5:6], in0=tbm[:, 2:3], in1=tbm[:, 3:4], op=ALU.max))
            E.wait("vector", "dv")
            E.op("vector", "dv", lambda v, hp=hp: v.scalar_tensor_tensor(
                out=negB[:, hp:hp + 1], in0=tbm[:, 4:5], scalar=-1.0,
                in1=tbm[:, 5:6], op0=ALU.bypass, op1=ALU.max))
            E.wait("vector", "dv")
            E.op("vector", "dv", lambda v, hp=hp: v.tensor_single_scalar(
                out=negB[:, hp:hp + 1], in_=negB[:, hp:hp + 1], scalar=-1.0,
                op=ALU.mult))
            for g2 in range(2):
                E.wait("scalar", "pe")
                E.wait("scalar", "dv")
                E.op("scalar", "ac", lambda s, g2=g2, hp=hp, h=h: s.activation(
                    out=p_half(h, g2),
                    in_=P_S[:, g2 * 1024:(g2 + 1) * 1024], func=AF.Exp,
                    bias=negB[:, hp:hp + 1], scale=1.0,
                    accum_out=tbm[:, 6 + 2 * hp + g2:7 + 2 * hp + g2]))
                R[("exph", h, g2)] = E.c["ac"]

        def emit_P(i, h):
            ft, po, hp = h // 2, (h % 2) * 64, h % 2
            ptr = ptr_sb if hp == 0 else ptr2
            for g2 in range(2):
                E.wait("tensor", "ac", R[("exph", h, g2)])
                if R.get(("ofm_ac", h - 1)):
                    E.wait("tensor", "ac", R[("ofm_ac", h - 1)])
                E.wait("tensor", "dv")
                for j in range(8):
                    E.op("tensor", "pe" if j == 7 else None,
                         lambda t, j=j, h=h, g2=g2: t.transpose(
                             P_T[:, j * 128:(j + 1) * 128],
                             in_=p_half(h, g2)[:, j * 128:(j + 1) * 128],
                             identity=ident_b[:]))
                E.wait("vector", "pe")
                E.op("vector", "dv", lambda v, g2=g2, ptr=ptr: v.tensor_copy(
                    ptr[:, g2 * 1024:(g2 + 1) * 1024], P_T[:, :]))
            E.wait("vector", "ac", R[("exph", h, 1)])
            E.op("vector", "dv", lambda v, hp=hp: v.tensor_tensor(
                out=tbm[:, 10:11], in0=tbm[:, 6 + 2 * hp:7 + 2 * hp],
                in1=tbm[:, 7 + 2 * hp:8 + 2 * hp], op=ALU.add))
            E.wait("vector", "dv")
            E.op("vector", "dv", lambda v, hp=hp: v.reciprocal(
                invd_att[:, hp:hp + 1], tbm[:, 10:11]))
            E.wait("tensor", "dv")
            for kb in range(16):
                E.op("tensor", "pe" if kb == 15 else None,
                     lambda t, kb=kb, h=h, ptr=ptr: t.matmul(
                         P_M[:, 256:320],
                         lhsT=ptr[:, kb * 128:(kb + 1) * 128],
                         rhs=v_aug[:, (kb * NH + h) * 65:(kb * NH + h) * 65 + 64],
                         start=(kb == 0), stop=(kb == 15)))
            E.wait("vector", "pe")
            E.op("vector", "dv", lambda v, po=po, hp=hp: v.tensor_scalar_mul(
                o_pair[:, po:po + 64], P_M[:, 256:320], invd_att[:, hp:hp + 1]))
            if hp == 1:
                E.wait("tensor", "dv")
                E.op("tensor", "pe", lambda t: t.transpose(
                    P_T[:, 0:128], in_=o_pair[:], identity=ident_b[:]))
                E.wait("scalar", "pe")
                E.op("scalar", "ac", lambda s, ft=ft: s.copy(
                    out=o_fm[:, ft * 128:(ft + 1) * 128],
                    in_=P_T[:, 0:128]))
                R[("ofm_ac", h)] = E.c["ac"]
            else:
                R[("ofm_ac", h)] = R.get(("ofm_ac", h - 1), 0)

        nhl = int(_os.environ.get("K_NHLIM", str(NH)))
        for i in range(only):
            qb = qbuf[i % 2]
            if i == 0:
                E.dma("sync", "ql", lambda h, i=i, qb=qb: h.dma_start(
                    out=qb[:], in_=q_sp[i]))
            E.wait("tensor", "dv")
            for h in range(nhl + 1):
                if h == 0:
                    if i == 0:
                        E.wait("tensor", "ql")
                        emit_S(i, qb, 0)
                    continue
                if h < nhl:
                    emit_S(i, qb, h)
                emit_P(i, h - 1)
            R[("qfree", i)] = E.c["pe"]
            if i + 1 < only:
                qb2 = qbuf[(i + 1) % 2]
                if i + 1 >= 2:
                    E.wait("sync", "pe", R.get(("qfree", i - 1)))
                E.dma("sync", "ql", lambda h, i=i, qb2=qb2: h.dma_start(
                    out=qb2[:], in_=q_sp[i + 1]))
                E.wait("tensor", "ql")
                emit_S(i + 1, qb2, 0)
            E.wait("tensor", "ac", R[("ofm_ac", nhl - 1)])
            for half in range(2):
                for c in range(8):
                    E.op("tensor", "pe" if (c == 7 and half == 1) else None,
                         lambda t, c=c, half=half: t.matmul(
                             P_W[:, half * 512:(half + 1) * 512],
                             lhsT=o_fm[:, c * 128:(c + 1) * 128],
                             rhs=woT[c][:, half * 512:(half + 1) * 512],
                             start=(c == 0), stop=(c == 7)))
            E.wait("vector", "pe")
            E.op("vector", "dv", lambda v, i=i: v.tensor_tensor(
                out=x_tm[:, i * D:(i + 1) * D], in0=P_W[:, :],
                in1=x_tm[:, i * D:(i + 1) * D], op=ALU.add))
            R[("attnadd", i)] = E.c["dv"]
            E.wait("tensor", "dv")
        R["attn_dv"] = E.c["dv"]

    def ln2_ffn_norm(E, b):
        xs = x_tm[:, b * D:(b + 1) * D]
        E.op("vector", None, lambda v: v.tensor_reduce(
            out=sx[:], in_=xs, axis=AXX, op=ALU.add))
        E.op("vector", "dv", lambda v: v.scalar_tensor_tensor(
            out=t1b[:], in0=xs, scalar=1.0, in1=xs,
            op0=ALU.mult, op1=ALU.mult, accum_out=sxx[:]))
        E.wait("vector", "dv")
        E.op("vector", "dv", lambda v: v.tensor_single_scalar(
            out=mln[:], in_=sx[:], scalar=1.0 / D, op=ALU.mult))
        E.wait("vector", "dv")
        E.op("vector", "dv", lambda v: v.tensor_tensor(
            out=m2ln[:], in0=mln[:], in1=mln[:], op=ALU.mult))
        E.wait("vector", "dv")
        E.op("vector", "dv", lambda v: v.scalar_tensor_tensor(
            out=varln[:], in0=sxx[:], scalar=1.0 / D, in1=m2ln[:],
            op0=ALU.mult, op1=ALU.subtract))
        E.wait("scalar", "dv")
        E.op("scalar", "ac", lambda s: s.activation(
            out=stdln[:], in_=varln[:], func=AF.Sqrt, bias=epsB[:], scale=1.000001))
        E.wait("vector", "ac")
        E.op("vector", "dv", lambda v: v.reciprocal(istd[:], stdln[:]))
        E.wait("vector", "dv")
        E.op("vector", "dv", lambda v: v.scalar_tensor_tensor(
            out=t1b[:], in0=xs, scalar=mln[:], in1=normw_b[:],
            op0=ALU.subtract, op1=ALU.mult))
        E.wait("vector", "dv")
        E.op("vector", "dv", lambda v: v.tensor_scalar_mul(t1b[:], t1b[:], istd[:]))

    def routing_block(E, b, slot):
        xslot = b % NXF
        if b >= NXF:
            E.wait("vector", "pe", R.get(("pairs_done", b - NXF)))
        E.wait("tensor", "dv")
        E.wait("tensor", "ac")
        if b >= NXF:
            E.wait("tensor", "pe", R.get(("pairs_done", b - NXF)))
        for c in range(8):
            E.op("tensor", "pe" if c == 7 else None, lambda t, c=c: t.transpose(
                P_T[:, c * 128:(c + 1) * 128], in_=t1b[:, c * 128:(c + 1) * 128],
                identity=ident_b[:]))
        E.wait("scalar", "pe")
        for c in range(8):
            E.op("scalar", "ac", lambda s, c=c, xslot=xslot: s.copy(
                out=xnffm[xslot][:, c * 128:(c + 1) * 128],
                in_=P_T[:, c * 128:(c + 1) * 128]))

        fia, coa = fi_all[slot], co_all[slot]
        if b >= NSLOT:
            E.wait("vector", "g", R.get(("gath_done", b - NSLOT)))
        for h in range(FH):
            for sub in range(2):
                ft = 2 * h + sub
                E.wait("tensor", "ac")
                for c in range(8):
                    E.op("tensor", "pe" if c == 7 else None,
                         lambda t, ft=ft, c=c, xslot=xslot: t.matmul(
                             P_S[:, 0:128],
                             lhsT=wqT[c][:, ft * 128:(ft + 1) * 128],
                             rhs=xnffm[xslot][:, c * 128:(c + 1) * 128],
                             start=(c == 0), stop=(c == 7)))
                E.wait("scalar", "pe")
                E.op("scalar", "ac", lambda s, sub=sub: s.copy(
                    out=qffn[:, sub * 128:(sub + 1) * 128], in_=P_S[:, 0:128]))
            E.wait("tensor", "ac")
            E.wait("tensor", "dv")
            for half in range(2):
                j = 2 * h + half
                E.op("tensor", "pe" if half == 1 else None,
                     lambda t, half=half, j=j: t.matmul(
                         P_W[:, half * 256:(half + 1) * 256],
                         lhsT=qffn[:, half * 128:(half + 1) * 128],
                         rhs=keysT[:, j * 256:(j + 1) * 256],
                         start=True, stop=True))
            E.wait("vector", "pe")
            E.op("vector", None, lambda v: v.max(out=v8a[:], in_=P_W[:, 0:256]))
            E.op("vector", "dv", lambda v: v.max(out=v8b[:], in_=P_W[:, 256:512]))
            E.wait("vector", "dv")
            for k in range(8):
                E.op("vector", None, lambda v, k=k: v.scalar_tensor_tensor(
                    out=scr256[:], in0=P_W[:, 0:256], scalar=v8a[:, k:k + 1],
                    in1=iota256[:], op0=ALU.is_equal, op1=ALU.mult,
                    accum_out=i1x[:, k:k + 1]))
                E.op("vector", "dv" if k == 7 else None,
                     lambda v, k=k: v.scalar_tensor_tensor(
                         out=scr256[:], in0=P_W[:, 256:512], scalar=v8b[:, k:k + 1],
                         in1=iota256[:], op0=ALU.is_equal, op1=ALU.mult,
                         accum_out=i2x[:, k:k + 1]))
            E.op("vector", None, lambda v: v.tensor_tensor(
                out=grid[:].rearrange("p (a b) -> p a b", a=8),
                in0=v8a[:].unsqueeze(2).broadcast_to((128, 8, 8)),
                in1=v8b[:].unsqueeze(1).broadcast_to((128, 8, 8)),
                op=ALU.add))
            E.op("vector", "dv", lambda v: v.max(out=fs8[:], in_=grid[:]))
            E.wait("vector", "dv")
            for k in range(8):
                E.op("vector", "dv" if k == 7 else None,
                     lambda v, k=k: v.scalar_tensor_tensor(
                         out=scr256[:, 0:64], in0=grid[:], scalar=fs8[:, k:k + 1],
                         in1=iota256[:, 0:64], op0=ALU.is_equal, op1=ALU.mult,
                         accum_out=pk8[:, k:k + 1]))
            E.wait("vector", "dv")
            E.op("vector", "dv", lambda v: v.tensor_copy(pki[:], pk8[:]))
            E.wait("vector", "dv")
            E.op("vector", None, lambda v: v.tensor_single_scalar(
                out=b8i[:], in_=pki[:], scalar=7, op=ALU.bitwise_and))
            E.op("vector", "dv", lambda v: v.tensor_single_scalar(
                out=a8i[:], in_=pki[:], scalar=3, op=ALU.logical_shift_right))
            E.wait("vector", "dv")
            E.op("vector", None, lambda v: v.tensor_copy(b8[:], b8i[:]))
            E.op("vector", None, lambda v: v.tensor_copy(a8[:], a8i[:]))
            E.op("vector", None, lambda v: v.memset(i1s[:], 0.0))
            E.op("vector", "dv", lambda v: v.memset(i2s[:], 0.0))
            E.wait("vector", "dv")
            for r in range(8):
                E.op("vector", None, lambda v, r=r: v.scalar_tensor_tensor(
                    out=tq1[:], in0=a8[:], scalar=float(r),
                    in1=i1x[:, r:r + 1].broadcast_to((128, 8)),
                    op0=ALU.is_equal, op1=ALU.mult))
                E.op("vector", None, lambda v, r=r: v.scalar_tensor_tensor(
                    out=tq2[:], in0=b8[:], scalar=float(r),
                    in1=i2x[:, r:r + 1].broadcast_to((128, 8)),
                    op0=ALU.is_equal, op1=ALU.mult))
                E.op("vector", None, lambda v: v.tensor_add(
                    out=i1s[:], in0=i1s[:], in1=tq1[:]))
                E.op("vector", "dv" if r == 7 else None, lambda v: v.tensor_add(
                    out=i2s[:], in0=i2s[:], in1=tq2[:]))
            E.wait("vector", "dv")
            E.op("vector", "dv", lambda v, h=h, fia=fia: v.scalar_tensor_tensor(
                out=fia[:, h * 8:(h + 1) * 8], in0=i1s[:], scalar=256.0, in1=i2s[:],
                op0=ALU.mult, op1=ALU.add))
            E.wait("vector", "dv")
            E.op("vector", None, lambda v, h=h, fia=fia: v.tensor_single_scalar(
                out=fia[:, h * 8:(h + 1) * 8], in_=fia[:, h * 8:(h + 1) * 8],
                scalar=65535.0, op=ALU.min))
            E.op("vector", "dv", lambda v, h=h, coa=coa: v.tensor_single_scalar(
                out=coa[:, h * 8:(h + 1) * 8], in_=fs8[:], scalar=0.0, op=ALU.max))

        E.wait("tensor", "dv")
        for (srct, tgt) in ((fia, gfi_f), (coa, gco[slot])):
            E.wait("tensor", "gt")
            E.op("tensor", "pe", lambda t, srct=srct: t.transpose(
                P_W[0:64, 512:640], in_=srct[:], identity=ident_f[:]))
            E.wait("vector", "pe")
            E.wait("vector", "gt")
            E.op("vector", None, lambda v, tgt=tgt: v.tensor_copy(
                tgt[0:64, :], P_W[0:64, 512:640][:, 0:128:2]))
            E.op("vector", "dv", lambda v: v.tensor_copy(
                gtmp[0:64, :], P_W[0:64, 512:640][:, 1:128:2]))
            E.wait("sync", "dv")
            E.dma("sync", "gt", lambda h, tgt=tgt: h.dma_start(
                out=tgt[64:128, :], in_=gtmp[0:64, :]))
            E.wait("tensor", "dv")
        E.wait("vector", "gt")
        E.op("vector", "dv", lambda v: v.tensor_copy(gfi[slot][:], gfi_f[:]))
        R[("route_done", b)] = E.c["dv"]
        R[("route_done_gt", b)] = E.c["gt"]

    pair_free = {}
    gath_val = {}

    def expert_block(E, b, slot):
        E.wait("vector", "gt", R[("route_done_gt", b)])
        E.op("vector", "dv", lambda v: v.memset(out_fm[:], 0.0))
        E.wait("tensor", "dv")
        for m in range(NPAIR):
            p = b * NPAIR + m
            r = p % NRING
            mg = m % 2
            gg = p // 2
            gb = gg % 2
            E.wait("gpsimd", "dv", R[("route_done", b)])
            E.wait("gpsimd", "gt", R[("route_done_gt", b)])
            if p >= NRING:
                E.wait("gpsimd", "pe", pair_free.get(p - NRING))
            E.dma("gpsimd", "g", lambda g, m=m, r=r, slot=slot: g.indirect_dma_start(
                out=ring_d[r], out_offset=None, in_=ap["edown"],
                in_offset=IndirectOffsetOnAxis(ap=gfi[slot][:, m:m + 1], axis=0)))
            E.dma("gpsimd", "g", lambda g, m=m, r=r, slot=slot: g.indirect_dma_start(
                out=ring_u[r], out_offset=None, in_=ap["eup"],
                in_offset=IndirectOffsetOnAxis(ap=gfi[slot][:, m:m + 1], axis=0)))
            gath_val[p] = E.c["g"]
            E.wait("tensor", "g", gath_val[p])
            E.wait("tensor", "dv")
            E.wait("tensor", "ac")
            for c in range(8):
                E.op("tensor", "pe" if c == 7 else None,
                     lambda t, c=c, r=r: t.transpose(
                         P_T[:, c * 128:(c + 1) * 128],
                         in_=ring_d[r][:, c * 128:(c + 1) * 128],
                         identity=ident_b[:]))
            if mg == 0:
                E.wait("scalar", "pe")
                E.op("scalar", "ac", lambda s, mg=mg: s.copy(
                    out=wdfm[mg][:], in_=P_T[:, :]))
                E.wait("tensor", "ac")
            else:
                E.wait("vector", "pe")
                E.op("vector", "dv", lambda v, mg=mg: v.tensor_copy(
                    wdfm[mg][:], P_T[:, :]))
                E.wait("tensor", "dv")
            dbase = 640 + gb * 4 + mg * 2
            if mg == 0 and gg >= 2:
                E.wait("tensor", "dv", R.get(("diag_free", gg - 2)))
            if mg == 0 and gg >= 1:
                E.wait("tensor", "dv", R.get(("upg_done", gg - 1)))
            for c in range(8):
                E.op("tensor", "pe" if c == 7 else None,
                     lambda t, c=c, mg=mg, m=m, xslot=b % NXF, dbase=dbase: t.matmul(
                         P_W[:, dbase:dbase + 2],
                         lhsT=wdfm[mg][:, c * 128:(c + 1) * 128],
                         rhs=xnffm[xslot][:, c * 128 + 2 * m:c * 128 + 2 * m + 2],
                         start=(c == 0), stop=(c == 7)))
            if mg == 1:
                base = 640 + gb * 4
                E.wait("vector", "pe")
                E.op("vector", None, lambda v, base=base, gb=gb: v.tensor_copy(
                    h4[gb][0:64, 0:2],
                    P_W[0:64, base:base + 4].rearrange("p (a b) -> p a b", a=2)[:, :, 0]))
                E.op("vector", "dv", lambda v, base=base, gb=gb: v.tensor_copy(
                    h4[gb][64:128, 0:2],
                    P_W[64:128, base:base + 4].rearrange("p (a b) -> p a b", a=2)[:, :, 1]))
                R[("diag_free", gg)] = E.c["dv"]
                E.wait("scalar", "dv")
                E.op("scalar", "ac", lambda s, gb=gb: s.activation(
                    out=h4[gb][:, 0:2], in_=h4[gb][:, 0:2], func=AF.Gelu))
                E.wait("vector", "ac")
                E.op("vector", "dv", lambda v, gb=gb, m=m, slot=slot: v.tensor_tensor(
                    out=hid4[gb][:, 0:2], in0=h4[gb][:, 0:2],
                    in1=gco[slot][:, m - 1:m + 1], op=ALU.mult))
                E.wait("vector", "dv")
                E.op("vector", None, lambda v, gb=gb: v.memset(chid[gb][:, 0:4], 0.0))
                E.op("vector", None, lambda v, gb=gb: v.tensor_copy(
                    chid[gb][0:64, 0:4].rearrange("p (a b) -> p a b", a=2)[:, :, 0],
                    hid4[gb][0:64, 0:2]))
                E.op("vector", "dv", lambda v, gb=gb: v.tensor_copy(
                    chid[gb][64:128, 0:4].rearrange("p (a b) -> p a b", a=2)[:, :, 1],
                    hid4[gb][64:128, 0:2]))
                E.wait("tensor", "dv")
                for mm in range(m - 1, m + 1):
                    pp = b * NPAIR + mm
                    rr = pp % NRING
                    upsum = P_W[:, 656:672] if mm % 2 == 0 else P_M[:, 256:272]
                    if pp >= 2:
                        E.wait("tensor", "dv", R.get(("up_evac", pp - 2)))
                    for c in range(8):
                        E.op("tensor", "pe" if c == 7 else None,
                             lambda t, c=c, rr=rr, mm=mm, gb=gb, upsum=upsum: t.matmul(
                                 upsum[:, c * 2:c * 2 + 2],
                                 lhsT=ring_u[rr][:, c * 128:(c + 1) * 128],
                                 rhs=chid[gb][:, (mm % 2) * 2:(mm % 2) * 2 + 2],
                                 start=True, stop=True))
                    pair_free[pp] = E.c["pe"]
                    E.wait("vector", "pe")
                    E.op("vector", "dv", lambda v, mm=mm, upsum=upsum: v.tensor_tensor(
                        out=out_fm[:].rearrange("p (c t) -> p c t", c=8)
                            [:, :, 2 * mm:2 * mm + 2],
                        in0=upsum.rearrange("p (c t) -> p c t", t=2),
                        in1=out_fm[:].rearrange("p (c t) -> p c t", c=8)
                            [:, :, 2 * mm:2 * mm + 2],
                        op=ALU.add))
                    R[("up_evac", pp)] = E.c["dv"]
                    if mm % 2 == 1:
                        R[("upg_done", gg)] = E.c["dv"]
        R[("pairs_done", b)] = E.c["pe"]
        R[("gath_done", b)] = E.c["g"]
        E.wait("tensor", "dv", R[("up_evac", b * NPAIR + NPAIR - 1)])
        for c in range(8):
            E.op("tensor", "pe" if c == 7 else None, lambda t, c=c: t.transpose(
                P_S[:, 1024 + c * 128:1024 + (c + 1) * 128],
                in_=out_fm[:, c * 128:(c + 1) * 128], identity=ident_f[:]))
        E.wait("vector", "pe")
        E.op("vector", "dv", lambda v, b=b: v.tensor_tensor(
            out=x_tm[:, b * D:(b + 1) * D], in0=P_S[:, 1024:2048],
            in1=x_tm[:, b * D:(b + 1) * D], op=ALU.add))
        E.wait("gpsimd", "dv")
        E.dma("gpsimd", "st", lambda g, b=b: g.dma_start(
            out=out[b * 128:(b + 1) * 128, :], in_=x_tm[:, b * D:(b + 1) * D]))

    def ffn(E):
        nbf = int(_os.environ.get("K_NBFFN", str(NB)))
        if FFN_AHEAD:
            for b in range(nbf):
                E.wait("vector", "dv", R[("attnadd", b)])
                ln2_ffn_norm(E, b)
                routing_block(E, b, b % NSLOT)
                if b >= 1:
                    expert_block(E, b - 1, (b - 1) % NSLOT)
            expert_block(E, nbf - 1, (nbf - 1) % NSLOT)
        else:
            for b in range(NB):
                E.wait("vector", "dv", R[("attnadd", b)])
                ln2_ffn_norm(E, b)
                routing_block(E, b, b % NSLOT)
                expert_block(E, b, b % NSLOT)

    E.c = dict(snapA)
    phase_b(E)

    with nc.Block() as blockB:
        for eng in ENGS:
            def _sect(handle, eng=eng):
                E.c = dict(snapA)
                E.active = (eng, handle)
                phase_b(E)
                E.active = None
            getattr(blockB, eng)(_sect)

    ctx_b.close()
    ctx_p.close()
    return nc


def kernel(L0_attn_in_features, L0_attn_W_qkv, L0_attn_W_o, L0_ffn_w_q,
           L0_ffn_keys, L0_ffn_expert_down, L0_ffn_expert_up, L0_ffn_norm_w):
    x = np.ascontiguousarray(L0_attn_in_features)
    nc = bass.Bass("TRN2", target_bir_lowering=False)
    build_program(nc)
    shared = {
        "wqkv": np.ascontiguousarray(L0_attn_W_qkv),
        "wo": np.ascontiguousarray(L0_attn_W_o),
        "wq": np.ascontiguousarray(L0_ffn_w_q),
        "keys": np.ascontiguousarray(L0_ffn_keys),
        "edown": np.ascontiguousarray(L0_ffn_expert_down),
        "eup": np.ascontiguousarray(L0_ffn_expert_up),
        "normw": np.ascontiguousarray(L0_ffn_norm_w),
    }
    in_maps = [dict(shared, x=np.ascontiguousarray(x[b])) for b in range(8)]
    res = run_bass_kernel_spmd(nc, in_maps, core_ids=list(range(8)))
    return np.stack([np.asarray(r["out"]) for r in res.results], axis=0).astype(np.float32)
